# version 07
# baseline (speedup 1.0000x reference)
import contextlib
import numpy as np
import ml_dtypes
import concourse.bass as bass
import concourse.mybir as mybir
from concourse.bass_utils import run_bass_kernel_spmd

F32 = mybir.dt.float32
BF16 = mybir.dt.bfloat16
I8 = mybir.dt.int8
ALU = mybir.AluOpType
ACT = mybir.ActivationFunctionType
AX = mybir.AxisListType.X

NCORE = 8
TOK = 2048
C = 1024
CC = 8
I = 2816
IC = 22
NT = 4
TT = 512
EPS = 1e-5

ENGS = ("sync", "gpsimd", "vector", "scalar", "tensor")

_STOP_AFTER = None
_DUMPS = []


class B:

    def __init__(self, nc, es):
        self.nc = nc
        self.es = es
        self.q = {e: [] for e in ENGS}
        self.sems = {}

    def sem(self, name):
        if name not in self.sems:
            h = self.es.enter_context(self.nc.semaphore(name))
            self.sems[name] = [h, 0]
        return self.sems[name]

    def op(self, eng, fn, sem=None, k=1):
        if sem is None:
            self.q[eng].append(fn)
            return None
        rec = self.sem(sem)
        rec[1] += k
        h = rec[0]
        self.q[eng].append(lambda e, fn=fn, h=h, k=k: fn(e).then_inc(h, k))
        return rec[1]

    def dma(self, eng, out, in_, sem=None, **kw):
        if sem is None:
            sem = {"sync": "ds", "gpsimd": "dg", "scalar": "dsc"}[eng]
        return self.op(eng, lambda e: e.dma_start(out=out, in_=in_, **kw), sem, 16)

    def wait(self, eng, sem, val=None):
        rec = self.sem(sem)
        v = rec[1] if val is None else val
        if v is None or v <= 0:
            return
        h = rec[0]
        self.q[eng].append(lambda e, h=h, v=v: e.wait_ge(h, v))

    def cur(self, sem):
        return self.sem(sem)[1]

    def barrier(self, engs=ENGS):
        snap = [(name, rec[1]) for name, rec in self.sems.items() if rec[1] > 0]
        for e in engs:
            for name, v in snap:
                self.wait(e, name, v)


class Banks:
    def __init__(self, nc, es):
        self.t = [es.enter_context(nc.psum_tensor(f"pb{i}", [128, TT], F32))
                  for i in range(8)]
        self.guard = [None] * 8

    def acquire(self, b, i):
        if self.guard[i] is not None:
            b.wait("tensor", *self.guard[i])
            self.guard[i] = None
        return self.t[i]

    def release(self, i, sem, val):
        self.guard[i] = (sem, val)


class Slab:

    def __init__(self, tiles, name):
        self.tiles = tiles
        self.name = name
        self.guard = [None] * len(tiles)

    def load(self, b, i, src):
        sl = i % len(self.tiles)
        if self.guard[sl] is not None:
            b.wait("gpsimd", self.guard[sl][0], self.guard[sl][1])
            self.guard[sl] = None
        b.dma("gpsimd", self.tiles[sl][:], src, sem=f"{self.name}{sl}")

    def use(self, b, i):
        sl = i % len(self.tiles)
        b.wait("tensor", f"{self.name}{sl}")
        return self.tiles[sl]

    def done(self, i, mmval, sem="t"):
        self.guard[i % len(self.tiles)] = (sem, mmval)


def _slab(wT, n_out_chunks):
    K = wT.shape[0]
    kc = K // 128
    a = wT.reshape(kc, 128, n_out_chunks, 128)
    return np.ascontiguousarray(a.transpose(2, 1, 0, 3).reshape(n_out_chunks, 128, K))


def _vec8(v):
    return np.ascontiguousarray(np.asarray(v, np.float32).reshape(CC, 128).T)


def build_program():
    nc = bass.Bass("TRN2", target_bir_lowering=False)
    ap = lambda t: t.ap()

    xT = ap(nc.dram_tensor("xT", [C, TOK], F32, kind="ExternalInput"))
    wq_s = ap(nc.dram_tensor("wq_s", [CC, 128, C], F32, kind="ExternalInput"))
    wk_s = ap(nc.dram_tensor("wk_s", [CC, 128, C], F32, kind="ExternalInput"))
    wv_s = ap(nc.dram_tensor("wv_s", [CC, 128, C], F32, kind="ExternalInput"))
    wg_s = ap(nc.dram_tensor("wg_s", [CC, 128, C], F32, kind="ExternalInput"))
    wo_s = ap(nc.dram_tensor("wo_s", [CC, 128, C], F32, kind="ExternalInput"))
    w0_s = ap(nc.dram_tensor("w0_s", [CC, 128, C], F32, kind="ExternalInput"))
    a_s = ap(nc.dram_tensor("a_s", [128, C], F32, kind="ExternalInput"))
    b_s = ap(nc.dram_tensor("b_s", [CC, 128, 128], F32, kind="ExternalInput"))
    wgT = ap(nc.dram_tensor("wgT", [C, I], F32, kind="ExternalInput"))
    wuT = ap(nc.dram_tensor("wuT", [C, I], F32, kind="ExternalInput"))
    wdT = ap(nc.dram_tensor("wdT", [I, C], F32, kind="ExternalInput"))
    wgN = ap(nc.dram_tensor("wgN", [I, C], F32, kind="ExternalInput"))
    wuN = ap(nc.dram_tensor("wuN", [I, C], F32, kind="ExternalInput"))
    wdN = ap(nc.dram_tensor("wdN", [C, I], F32, kind="ExternalInput"))
    logr = ap(nc.dram_tensor("logr", [128, CC], F32, kind="ExternalInput"))
    lng = ap(nc.dram_tensor("lng", [128, CC], F32, kind="ExternalInput"))
    lnb = ap(nc.dram_tensor("lnb", [128, CC], F32, kind="ExternalInput"))
    lsg = ap(nc.dram_tensor("lsg", [128, CC], F32, kind="ExternalInput"))
    clipx = ap(nc.dram_tensor("clipx", [1, 1], F32, kind="ExternalInput"))
    clipd = ap(nc.dram_tensor("clipd", [1, 1], F32, kind="ExternalInput"))
    posr = ap(nc.dram_tensor("posr", [1, TOK], F32, kind="ExternalInput"))
    iseven = ap(nc.dram_tensor("iseven", [1, 1], F32, kind="ExternalInput"))
    isodd = ap(nc.dram_tensor("isodd", [1, 1], F32, kind="ExternalInput"))

    oxT = ap(nc.dram_tensor("oxT", [C, TOK], F32, kind="ExternalOutput"))

    wgq = ap(nc.dram_tensor("wgq", [C, I], I8, kind="Internal"))
    wuq = ap(nc.dram_tensor("wuq", [C, I], I8, kind="Internal"))
    wdq = ap(nc.dram_tensor("wdq", [I, C], I8, kind="Internal"))
    t_sp = ap(nc.dram_tensor("t_sp", [I, TOK], F32, kind="Internal"))
    xa_sp = ap(nc.dram_tensor("xa_sp", [C, TOK], BF16, kind="Internal"))
    err_sp = ap(nc.dram_tensor("err_sp", [C, TOK], F32, kind="Internal"))
    sh_sp = ap(nc.dram_tensor("sh_sp", [C, TOK], F32, kind="Internal"))
    op_sp = ap(nc.dram_tensor("op_sp", [C, TOK], BF16, kind="Internal"))
    srow = ap(nc.dram_tensor("srow", [1, max(I, C)], F32, kind="Internal"))
    stage = ap(nc.dram_tensor("stage", [1, 16], F32, kind="Internal"))
    cw_i = ap(nc.dram_tensor("cw_i", [1, 8], F32, kind="Internal"))
    cw_o = ap(nc.dram_tensor("cw_o", [1, 8], F32, kind="Internal"))
    cm_i = [ap(nc.dram_tensor(f"cm_i{k}", [1, 8], F32, kind="Internal")) for k in range(4)]
    cm_o = [ap(nc.dram_tensor(f"cm_o{k}", [1, 8], F32, kind="Internal")) for k in range(4)]
    cy_i = ap(nc.dram_tensor("cy_i", [128, 8], F32, kind="Internal"))
    cy_o = ap(nc.dram_tensor("cy_o", [128, 8], F32, kind="Internal"))

    dumps = {}

    def dump_tensor(name, shape, dtype=F32):
        if name in _DUMPS:
            dumps[name] = ap(nc.dram_tensor("dbg_" + name, shape, dtype,
                                            kind="ExternalOutput"))
            return dumps[name]
        return None

    es = contextlib.ExitStack()
    with es:
        sbuf = lambda n, s, d: es.enter_context(nc.sbuf_tensor(n, s, d))
        b = B(nc, es)
        banks = Banks(nc, es)

        XR = []
        lrT = sbuf("lrT", [128, CC], F32)
        gT = sbuf("gT", [128, CC], F32)
        bT = sbuf("bT", [128, CC], F32)
        lsT = sbuf("lsT", [128, CC], F32)
        sgN = sbuf("sgN", [128, IC], F32)
        suN = sbuf("suN", [128, IC], F32)
        shN = sbuf("shN", [128, CC], F32)
        sgC = sbuf("sgC", [128, IC], F32)
        suC = sbuf("suC", [128, IC], F32)
        sdC = sbuf("sdC", [128, CC], F32)
        onesb = sbuf("onesb", [128, 128], BF16)
        sigx = sbuf("sigx", [1, 2], F32)
        rrow = sbuf("rrow", [1, 128], F32)
        sc1 = sbuf("sc1", [1, 8], F32)
        invb = sbuf("invb", [128, 1], F32)
        scb = sbuf("scb", [128, 1], F32)
        amaxA = sbuf("amaxA", [128, 32], F32)
        colsc = sbuf("colsc", [128, 1], F32)
        ieb = sbuf("ieb", [128, 1], F32)
        iob = sbuf("iob", [128, 1], F32)
        carry = sbuf("carry", [128, CC], F32)
        epsb = sbuf("epsb", [128, 1], F32)

        b.dma("sync", lrT[:], logr)
        b.dma("sync", gT[:], lng)
        b.dma("sync", bT[:], lnb)
        b.dma("sync", lsT[:], lsg)
        b.dma("gpsimd", ieb[:], iseven.partition_broadcast(128))
        b.dma("gpsimd", iob[:], isodd.partition_broadcast(128))
        b.dma("gpsimd", sc1[0:1, 0:1], clipx)
        b.dma("gpsimd", sc1[0:1, 1:2], clipd)

        b.op("vector", lambda e: e.memset(rrow[:], 0.0), "v")
        b.op("vector", lambda e: e.memset(epsb[:], EPS), "v")
        b.op("vector", lambda e: e.memset(onesb[:], 1.0), "v")
        b.wait("gpsimd", "v")
        b.dma("gpsimd", cw_i, rrow[0:1, 0:8])
        b.wait("gpsimd", "dg")
        b.op("gpsimd", lambda e: e.collective_compute(
            "AllReduce", ALU.add, replica_groups=[list(range(NCORE))],
            ins=[cw_i], outs=[cw_o]), "cc")

        b.wait("scalar", "dg")
        b.op("scalar", lambda e: e.activation(sigx[0:1, 0:2], sc1[0:1, 0:2], ACT.Sigmoid), "s")
        b.wait("scalar", "ds")
        b.op("scalar", lambda e: e.activation(lrT[:], lrT[:], ACT.Exp), "s")
        b.wait("vector", "s")
        b.op("vector", lambda e: e.tensor_scalar_min(lrT[:], lrT[:], 1.0), "v")

        WPREP = [
            ("g", wgN, wgT, wgq, IC, CC, C, I, sgN),
            ("u", wuN, wuT, wuq, IC, CC, C, I, suN),
            ("d", wdN, wdT, wdq, CC, IC, I, C, shN),
        ]
        wp_marks = {}
        natA = []
        trnA = []

        wp_ng = {}

        def emit_wp_nat(nat, wn, n_nat, nat_w, sN, tagn):
            ns = len(nat)
            ng = wp_ng.setdefault(tagn, [None] * ns)
            for ch in range(n_nat):
                sl = ch % ns
                if ng[sl] is not None:
                    b.wait("sync", "v", ng[sl])
                b.dma("sync", nat[sl][:, 0:nat_w], wn[128 * ch:128 * (ch + 1), :],
                      sem=f"{tagn}{sl}")
                b.wait("vector", f"{tagn}{sl}")
                ng[sl] = b.op("vector", lambda e, sl=sl, ch=ch, nat_w=nat_w, sN=sN:
                              e.reduce_max(sN[:, ch:ch + 1], nat[sl][:, 0:nat_w], AX,
                                           apply_absolute_value=True), "v")

        def emit_weight_prep_a():
            emit_wp_nat(natA, wgN, IC, C, sgN, "wn")
            emit_wp_nat(natA, wuN, IC, C, suN, "wn")
            emit_wp_nat(natA, wdN, CC, I, shN, "wn")

        def emit_weight_prep_b():
            with contextlib.ExitStack() as p0:
                sb0 = lambda n, s, d: p0.enter_context(nc.sbuf_tensor(n, s, d))
                trn = trnA
                qtmp = [sb0(f"qtmp{i}", [128, I], F32) for i in range(2)]
                q8w = [sb0(f"q8w{i}", [128, I], I8) for i in range(2)]
                recb = sb0("recb", [128, I], F32)
                sccol = sb0("sccol", [128, IC], F32)
                lastw = [None, None]
                tg = [None, None]
                for (key, wn, wt, wq_out, n_nat, n_tr, nat_w, tr_w, sN) in WPREP:
                    b.wait("vector", "v")
                    b.wait("vector", "dg")
                    b.op("vector", lambda e, sN=sN, n=n_nat:
                         e.reciprocal(sccol[:, 0:n], sN[:, 0:n]), "v")
                    b.wait("vector", "v")
                    b.op("vector", lambda e, n=n_nat:
                         e.tensor_scalar_mul(sccol[:, 0:n], sccol[:, 0:n], 127.0), "v")
                    b.op("vector", lambda e, sN=sN, n=n_nat:
                         e.tensor_scalar_mul(sN[:, 0:n], sN[:, 0:n], 1.0 / 127.0), "v")
                    b.wait("gpsimd", "v")
                    for ch in range(n_nat):
                        b.dma("gpsimd", srow[0:1, 128 * ch:128 * (ch + 1)],
                              sccol[:, ch:ch + 1])
                    b.wait("gpsimd", "dg")
                    b.dma("gpsimd", recb[:, 0:128 * n_nat],
                          srow[0:1, 0:128 * n_nat].partition_broadcast(128))
                    for ch in range(n_tr):
                        sl = ch % 2
                        if tg[sl] is not None:
                            b.wait("sync", "v", tg[sl])
                        b.dma("sync", trn[sl][:, 0:tr_w], wt[128 * ch:128 * (ch + 1), :],
                              sem=f"wt{sl}")
                        b.wait("vector", f"wt{sl}")
                        b.wait("vector", "dg")
                        b.wait("vector", "s")
                        tg[sl] = b.op("vector", lambda e, sl=sl, tr_w=tr_w:
                                      e.tensor_mul(qtmp[sl][:, 0:tr_w], trn[sl][:, 0:tr_w],
                                                   recb[:, 0:tr_w]), "v")
                        if lastw[sl] is not None:
                            b.wait("scalar", lastw[sl])
                        b.wait("scalar", "v")
                        b.op("scalar", lambda e, sl=sl, tr_w=tr_w:
                             e.activation(q8w[sl][:, 0:tr_w], qtmp[sl][:, 0:tr_w], ACT.Copy), "s")
                        b.wait("gpsimd", "s")
                        b.dma("gpsimd", wq_out[128 * ch:128 * (ch + 1), :],
                              q8w[sl][:, 0:tr_w], sem=f"wq8{key}{sl}")
                        lastw[sl] = f"wq8{key}{sl}"
                    wp_marks[key] = (b.cur(f"wq8{key}0"), b.cur(f"wq8{key}1"))


        def emit_global_scale(ncols, sig_col, cm_in_ap, cm_out_ap):
            b.wait("vector", "v")
            b.op("vector", lambda e, n=ncols:
                 e.reduce_max(colsc[:], amaxA[:, 0:n], AX), "v")
            b.wait("gpsimd", "v")
            b.dma("gpsimd", rrow[0:1, 0:128], colsc[:, 0:1])
            b.wait("vector", "dg")
            b.op("vector", lambda e: e.reduce_max(sc1[0:1, 2:3], rrow[0:1, 0:128], AX), "v")
            b.wait("gpsimd", "v")
            b.dma("gpsimd", cm_in_ap[0:1, 0:1], sc1[0:1, 2:3])
            b.wait("gpsimd", "dg")
            b.op("gpsimd", lambda e, ci=cm_in_ap, co=cm_out_ap: e.collective_compute(
                "AllReduce", ALU.max, replica_groups=[list(range(NCORE))],
                ins=[ci], outs=[co]), "cc")
            b.wait("gpsimd", "cc")
            b.dma("gpsimd", sc1[0:1, 3:4], cm_out_ap[0:1, 0:1])
            b.wait("vector", "dg")
            b.op("vector", lambda e, sc=sig_col: e.tensor_mul(
                sc1[0:1, 4:5], sc1[0:1, 3:4], sigx[0:1, sc:sc + 1]), "v")
            b.wait("vector", "v")
            b.op("vector", lambda e: e.tensor_scalar_mul(sc1[0:1, 6:7], sc1[0:1, 4:5], 1.0 / 127.0), "v")
            b.wait("gpsimd", "v")
            b.dma("gpsimd", stage[0:1, 0:2], sc1[0:1, 4:6])
            b.wait("gpsimd", "dg")
            b.dma("gpsimd", stage[0:1, 2:4], sc1[0:1, 6:8])
            b.wait("gpsimd", "dg")
            b.dma("gpsimd", colsc[:], stage[0:1, 0:1].partition_broadcast(128))
            b.dma("gpsimd", scb[:], stage[0:1, 2:3].partition_broadcast(128))
            b.wait("vector", "dg")
            b.op("vector", lambda e: e.reciprocal(invb[:], colsc[:]), "v")
            b.wait("vector", "v")
            b.op("vector", lambda e: e.tensor_scalar_mul(invb[:], invb[:], 127.0), "v")
            b.wait("scalar", "v")
            b.wait("scalar", "dg")

        def emit_stats(src_emit, m_b, r_b, tmp, xb, sqb):
            ps_s = [banks.acquire(b, i) for i in range(4)]
            ps_q = [banks.acquire(b, i + 4) for i in range(4)]
            g = [None, None]
            for c in range(CC):
                sl = c % 2
                if g[sl] is not None:
                    b.wait("scalar", "t", g[sl])
                src_emit(c, sl)
                b.op("scalar", lambda e, sl=sl: e.activation(sqb[sl][:], xb[sl][:], ACT.Square), "s")
                b.wait("tensor", "s")
                for tt in range(NT):
                    b.op("tensor", lambda e, sl=sl, tt=tt, c=c: e.matmul(
                        ps_s[tt][:], onesb[:], xb[sl][:, TT * tt:TT * (tt + 1)],
                        start=(c == 0), stop=(c == CC - 1)), None)
                for tt in range(NT):
                    g[sl] = b.op("tensor", lambda e, sl=sl, tt=tt, c=c: e.matmul(
                        ps_q[tt][:], onesb[:], sqb[sl][:, TT * tt:TT * (tt + 1)],
                        start=(c == 0), stop=(c == CC - 1)), "t")
            b.wait("scalar", "t")
            for tt in range(NT):
                sl_ = slice(TT * tt, TT * (tt + 1))
                b.op("scalar", lambda e, tt=tt, sl_=sl_: e.activation(
                    m_b[:, sl_], ps_s[tt][:], ACT.Copy, scale=1.0 / C), "s")
                b.op("scalar", lambda e, tt=tt, sl_=sl_: e.activation(
                    r_b[:, sl_], ps_q[tt][:], ACT.Copy, scale=1.0 / C), "s")
            b.wait("vector", "s")
            for tt in range(NT):
                sl_ = slice(TT * tt, TT * (tt + 1))
                b.op("vector", lambda e, sl_=sl_: e.tensor_mul(tmp[:], m_b[:, sl_], m_b[:, sl_]), None)
                b.op("vector", lambda e, sl_=sl_: e.tensor_sub(r_b[:, sl_], r_b[:, sl_], tmp[:]), "v")
            b.wait("scalar", "v")
            b.op("scalar", lambda e: e.activation(r_b[:], r_b[:], ACT.Ln, bias=epsb[:]), None)
            b.op("scalar", lambda e: e.activation(r_b[:], r_b[:], ACT.Exp, scale=-0.5), "s")
            for i in range(8):
                banks.release(i, "s", b.cur("s"))
            b.wait("vector", "s")

        def emit_ln_quant(src_chunk, xq_bf, quant_cm_idx, ln_name, mid=None):
            with contextlib.ExitStack() as pl:
                sbl = lambda n, s, d: pl.enter_context(nc.sbuf_tensor(n, s, d))
                ln1 = [sbl(f"{ln_name}_ln{c}", [128, TOK], F32) for c in range(CC)]
                q8 = [sbl(f"{ln_name}_q8{i}", [128, TOK], I8) for i in range(2)]
                with contextlib.ExitStack() as pxs:
                    sbx = lambda n, s, d: pxs.enter_context(nc.sbuf_tensor(n, s, d))
                    xb = [sbx(f"{ln_name}_xb{i}", [128, TOK], BF16) for i in range(2)]
                    sqb = [sbx(f"{ln_name}_sq{i}", [128, TOK], BF16) for i in range(2)]
                    m_b = sbx(f"{ln_name}_m", [128, TOK], F32)
                    r_b = sbx(f"{ln_name}_r", [128, TOK], F32)
                    tmp = sbx(f"{ln_name}_t", [128, TT], F32)

                    def src_stats(c, sl):
                        apc = src_chunk(c, sl, ("scalar",))
                        b.op("scalar", lambda e, sl=sl, apc=apc: e.activation(
                            xb[sl][:], apc, ACT.Copy), None)

                    emit_stats(src_stats, m_b, r_b, tmp, xb, sqb)
                    apcs = [None] * CC
                    apcs[0] = src_chunk(0, 0, ())
                    for c in range(CC):
                        if c + 1 < CC:
                            apcs[c + 1] = src_chunk(c + 1, (c + 1) % 2, ())
                        b.wait("vector", f"xs{c % 2}") if src_chunk is not None else None
                        apc = apcs[c]
                        b.op("vector", lambda e, c=c, apc=apc: e.tensor_sub(ln1[c][:], apc, m_b[:]), None)
                        b.op("vector", lambda e, c=c: e.tensor_mul(ln1[c][:], ln1[c][:], r_b[:]), None)
                        b.op("vector", lambda e, c=c: e.reduce_max(
                            amaxA[:, c:c + 1], ln1[c][:], AX, apply_absolute_value=True), "v")
                    dmb = dump_tensor(ln_name + "_mb", [128, TOK])
                    if dmb is not None:
                        b.wait("sync", "v")
                        b.dma("sync", dmb, m_b[:])
                        b.barrier()
                    drb = dump_tensor(ln_name + "_rb", [128, TOK])
                    if drb is not None:
                        b.wait("sync", "v")
                        b.dma("sync", drb, r_b[:])
                        b.barrier()
                dam = dump_tensor(ln_name + "_amax", [128, 32])
                if dam is not None:
                    b.wait("sync", "v")
                    b.dma("sync", dam, amaxA[:])
                    b.barrier()
                dln = dump_tensor(ln_name, [C, TOK])
                if dln is not None:
                    b.wait("sync", "v")
                    for c in range(CC):
                        b.dma("sync", dln[128 * c:128 * (c + 1), :], ln1[c][:])
                emit_global_scale(CC, 0, cm_i[quant_cm_idx], cm_o[quant_cm_idx])
                if mid is not None:
                    mid()
                qgv = [None, None]
                for c in range(CC):
                    sl = c % 2
                    if qgv[sl] is not None:
                        b.wait("scalar", f"xqc{sl}")
                    b.op("scalar", lambda e, c=c, sl=sl: e.activation(
                        q8[sl][:], ln1[c][:], ACT.Copy, scale=invb[:]), "s")
                    b.wait("gpsimd", "s")
                    qgv[sl] = b.dma("gpsimd", xq_bf[c][:], q8[sl][:], sem=f"xqc{sl}")
                b.barrier()

        def emit_ffn_gate_up(xq_bf, tag, dbufw=True, wait_marks=None):
            b.op("vector", lambda e: e.tensor_scalar(sgC[:], sgN[:, 0:IC], scb[:], None, ALU.mult), None)
            b.op("vector", lambda e: e.tensor_scalar(suC[:], suN[:, 0:IC], scb[:], None, ALU.mult), "v")
            with contextlib.ExitStack() as pf:
                sbf = lambda n, s, d: pf.enter_context(nc.sbuf_tensor(n, s, d))
                HW = I // 2
                nws = 2 if dbufw else 1
                wgw_ = [[sbf(f"{tag}gw{h}_{c}", [128, HW], BF16) for c in range(CC)]
                        for h in range(nws)]
                wuw_ = [[sbf(f"{tag}uw{h}_{c}", [128, HW], BF16) for c in range(CC)]
                        for h in range(nws)]
                wgw = [wgw_[h % nws] for h in range(2)]
                wuw = [wuw_[h % nws] for h in range(2)]
                silu = [sbf(f"{tag}sil{i}", [128, TOK], F32) for i in range(2)]
                tb = [sbf(f"{tag}tb{i}", [128, TOK], F32) for i in range(2)]
                tspg = [None, None]

                def load_g(h):
                    io = HW * h
                    for c in range(CC):
                        b.dma("gpsimd", wgw[h][c][:], wgq[128 * c:128 * (c + 1), io:io + HW],
                              sem=f"wg{h}")

                def load_u(h):
                    io = HW * h
                    for c in range(CC):
                        b.dma("gpsimd", wuw[h][c][:], wuq[128 * c:128 * (c + 1), io:io + HW],
                              sem=f"wu{h}")

                if wait_marks is not None:
                    b.wait("gpsimd", "wq8g0", wait_marks["g"][0])
                    b.wait("gpsimd", "wq8g1", wait_marks["g"][1])
                load_g(0)
                if dbufw:
                    load_g(1)
                if wait_marks is not None:
                    b.wait("gpsimd", "wq8u0", wait_marks["u"][0])
                    b.wait("gpsimd", "wq8u1", wait_marks["u"][1])
                load_u(0)
                if dbufw:
                    load_u(1)
                b.wait("tensor", "xqc0")
                b.wait("tensor", "xqc1")
                for half in range(2):
                    if not dbufw and half == 1:
                        b.wait("gpsimd", "t")
                        load_g(1)
                        load_u(1)
                    b.wait("tensor", f"wg{half}")
                    for ici in range(IC // 2):
                        ic = half * (IC // 2) + ici
                        sl = ic % 2
                        for i in range(4):
                            banks.acquire(b, i)
                        for c in range(CC):
                            for tt in range(NT):
                                gmm = b.op("tensor", lambda e, c=c, tt=tt, ici=ici, half=half: e.matmul(
                                    banks.t[tt][:],
                                    wgw[half][c][:, 128 * ici:128 * (ici + 1)],
                                    xq_bf[c][:, TT * tt:TT * (tt + 1)],
                                    start=(c == 0), stop=(c == CC - 1)),
                                    "t" if (c == CC - 1 and tt == NT - 1) else None)
                        b.wait("scalar", "t", gmm)
                        if tspg[sl] is not None:
                            b.wait("scalar", f"tsp{sl}")
                        for tt in range(NT):
                            sv = b.op("scalar", lambda e, tt=tt, ic=ic, sl=sl: e.activation(
                                silu[sl][:, TT * tt:TT * (tt + 1)], banks.t[tt][:],
                                ACT.Silu, scale=sgC[:, ic:ic + 1]), "s")
                        for i in range(4):
                            banks.release(i, "s", sv)
                        for i in range(4, 8):
                            banks.acquire(b, i)
                        if ici == 0:
                            b.wait("tensor", f"wu{half}")
                        for c in range(CC):
                            for tt in range(NT):
                                umm = b.op("tensor", lambda e, c=c, tt=tt, ici=ici, half=half: e.matmul(
                                    banks.t[tt + 4][:],
                                    wuw[half][c][:, 128 * ici:128 * (ici + 1)],
                                    xq_bf[c][:, TT * tt:TT * (tt + 1)],
                                    start=(c == 0), stop=(c == CC - 1)),
                                    "t" if (c == CC - 1 and tt == NT - 1) else None)
                        b.wait("vector", "t", umm)
                        b.wait("vector", "s", sv)
                        if tspg[sl] is not None:
                            b.wait("vector", f"tsp{sl}")
                        for tt in range(NT):
                            uv = b.op("vector", lambda e, tt=tt, ic=ic, sl=sl: e.scalar_tensor_tensor(
                                tb[sl][:, TT * tt:TT * (tt + 1)], banks.t[tt + 4][:],
                                suC[:, ic:ic + 1], silu[sl][:, TT * tt:TT * (tt + 1)],
                                ALU.mult, ALU.mult), "v")
                        for i in range(4, 8):
                            banks.release(i, "v", uv)
                        b.op("vector", lambda e, ic=ic, sl=sl: e.reduce_max(
                            amaxA[:, ic:ic + 1], tb[sl][:], AX,
                            apply_absolute_value=True), "v")
                        b.wait("sync", "v")
                        tspg[sl] = b.dma("sync", t_sp[128 * ic:128 * (ic + 1), :], tb[sl][:],
                                         sem=f"tsp{sl}")
                b.barrier()

        def emit_ffn_down(resid_src, store_out, tag, wait_marks=None):
            b.op("vector", lambda e: e.tensor_scalar(
                sdC[:], shN[:, 0:CC], scb[:], 0.5, ALU.mult, ALU.mult), "v")
            if wait_marks is not None:
                b.wait("gpsimd", "wq8d0", wait_marks["d"][0])
                b.wait("gpsimd", "wq8d1", wait_marks["d"][1])
            with contextlib.ExitStack() as pd:
                sbd = lambda n, s, d: pd.enter_context(nc.sbuf_tensor(n, s, d))
                qhw = [sbd(f"{tag}qh{ic}", [128, 1024], BF16) for ic in range(IC)]
                wdw = [sbd(f"{tag}dw{ic}", [128, 1024], BF16) for ic in range(IC)]
                tin = [sbd(f"{tag}ti{i}", [128, 1024], F32) for i in range(4)]
                q8h = [sbd(f"{tag}q8h{i}", [128, 1024], I8) for i in range(4)]
                for half in range(2):
                    to = 1024 * half
                    b.wait("gpsimd", "t")
                    qge = [None, None, None, None]
                    for ic in range(IC):
                        sl = ic % 4
                        if half == 0:
                            b.dma("gpsimd", wdw[ic][:], wdq[128 * ic:128 * (ic + 1), :],
                                  sem="wdw")
                        if qge[sl] is not None:
                            b.wait("sync", f"qc{sl}")
                        b.dma("sync", tin[sl][:], t_sp[128 * ic:128 * (ic + 1), to:to + 1024],
                              sem=f"ti{sl}")
                        b.wait("scalar", f"ti{sl}")
                        if qge[sl] is not None:
                            b.wait("scalar", f"qc{sl}")
                        b.op("scalar", lambda e, sl=sl: e.activation(
                            q8h[sl][:], tin[sl][:], ACT.Copy, scale=invb[:]), "s")
                        b.wait("gpsimd", "s")
                        qge[sl] = b.dma("gpsimd", qhw[ic][:], q8h[sl][:], sem=f"qc{sl}")
                    b.wait("tensor", "wdw")
                    for qs in ("qc0", "qc1", "qc2", "qc3"):
                        b.wait("tensor", qs)
                    stg = [None, None]
                    for hc in range(CC):
                        sl = hc % 2
                        for i in range(2):
                            banks.acquire(b, i)
                        for ic in range(IC):
                            for t2 in range(2):
                                dmm = b.op("tensor", lambda e, ic=ic, t2=t2, hc=hc: e.matmul(
                                    banks.t[t2][:],
                                    wdw[ic][:, 128 * hc:128 * (hc + 1)],
                                    qhw[ic][:, TT * t2:TT * (t2 + 1)],
                                    start=(ic == 0), stop=(ic == IC - 1)),
                                    "t" if (ic == IC - 1 and t2 == 1) else None)
                        src = resid_src(hc, half, sl)
                        b.wait("vector", "t", dmm)
                        for t2 in range(2):
                            vv = b.op("vector", lambda e, t2=t2, hc=hc, src=src, to=to:
                                      e.scalar_tensor_tensor(
                                          XR[hc][:, to + TT * t2:to + TT * (t2 + 1)],
                                          banks.t[t2][:], sdC[:, hc:hc + 1],
                                          src[:, TT * t2:TT * (t2 + 1)],
                                          ALU.mult, ALU.add), "v")
                        for i in range(2):
                            banks.release(i, "v", vv)
                        if store_out:
                            b.wait("sync", "v", vv)
                            stg[sl] = b.dma("sync", oxT[128 * hc:128 * (hc + 1), to:to + 1024],
                                            XR[hc][:, to:to + 1024])
                b.barrier()

        with contextlib.ExitStack() as pA:
            sbA = lambda n, s, d: pA.enter_context(nc.sbuf_tensor(n, s, d))
            xq_bf = [sbA(f"xqb{c}", [128, TOK], BF16) for c in range(CC)]
            natA.extend(sbA(f"natA{i}", [128, I], F32) for i in range(2))
            trnA.extend(sbA(f"trnA{i}", [128, I], F32) for i in range(2))
            xs = []

            def p1_chunk(c, sl, engs):
                b.wait("gpsimd", "v")
                b.wait("gpsimd", "s")
                b.dma("gpsimd", xs[sl][:], xT[128 * c:128 * (c + 1), :], sem=f"xs{sl}")
                for e_ in engs:
                    b.wait(e_, f"xs{sl}")
                return xs[sl][:]

            with contextlib.ExitStack() as pXS:
                xs.extend(pXS.enter_context(nc.sbuf_tensor(f"xs{i}", [128, TOK], F32))
                          for i in range(2))
                emit_weight_prep_a()
                emit_ln_quant(p1_chunk, xq_bf, 0, "ln1", mid=emit_weight_prep_b)
            dsn = dump_tensor("sgN", [128, IC])
            if dsn is not None:
                b.wait("sync", "v")
                b.dma("sync", dsn, sgN[:])
                dsu = dump_tensor("suN", [128, IC])
                b.dma("sync", dsu, suN[:])
                b.barrier()
            dwq = dump_tensor("wgq", [C, I], I8)
            if dwq is not None:
                for s_ in ("wq8g0", "wq8g1", "wq8u0", "wq8u1", "wq8d0", "wq8d1"):
                    b.wait("sync", s_)
                b.dma("sync", dwq, wgq)
                duq = dump_tensor("wuq", [C, I], I8)
                b.dma("sync", duq, wuq)
                b.barrier()
            dsc = dump_tensor("sc1", [1, 8])
            if dsc is not None:
                b.wait("sync", "v")
                b.wait("sync", "dg")
                b.dma("sync", dsc, sc1[:])
                b.barrier()
            dinv = dump_tensor("invb", [128, 2])
            if dinv is not None:
                b.wait("sync", "v")
                b.dma("sync", dinv[:, 0:1], invb[:], allow_slow_non_contiguous=True)
                b.dma("sync", dinv[:, 1:2], scb[:], allow_slow_non_contiguous=True)
                b.barrier()
            dxq = dump_tensor("xq", [C, TOK], BF16)
            if dxq is not None:
                for c in range(CC):
                    b.dma("sync", dxq[128 * c:128 * (c + 1), :], xq_bf[c][:])
                b.barrier()
            if _STOP_AFTER != "p1":
                emit_ffn_gate_up(xq_bf, "f1", wait_marks=wp_marks)
                dsg = dump_tensor("sgC", [128, 2 * IC])
                if dsg is not None:
                    b.wait("sync", "v")
                    b.dma("sync", dsg[:, 0:IC], sgC[:])
                    b.dma("sync", dsg[:, IC:2 * IC], suC[:])
                    b.barrier()
                dt1 = dump_tensor("t1", [I, TOK])
                if dt1 is not None:
                    b.wait("sync", "tsp0")
                    b.wait("sync", "tsp1")
                    b.dma("sync", dt1, t_sp)
                    b.barrier()
                emit_global_scale(IC, 1, cm_i[1], cm_o[1])

        if _STOP_AFTER in ("p1", "p2"):
            b.barrier()
            return _emit(nc, es, b)

        XR.extend(sbuf(f"xr{c}", [128, TOK], F32) for c in range(CC))
        with contextlib.ExitStack() as p3s:
            xs3 = [p3s.enter_context(nc.sbuf_tensor(f"xs3{i}", [128, 1024], F32))
                   for i in range(2)]

            def p3_src(hc, half, sl):
                b.wait("sync", "v")
                b.dma("sync", xs3[sl][:],
                      xT[128 * hc:128 * (hc + 1), 1024 * half:1024 * (half + 1)])
                b.wait("vector", "ds")
                return xs3[sl][:]

            emit_ffn_down(p3_src, False, "f1", wait_marks=wp_marks)
        dx1 = dump_tensor("x1", [C, TOK])
        if dx1 is not None:
            for c in range(CC):
                b.dma("sync", dx1[128 * c:128 * (c + 1), :], XR[c][:])
            b.barrier()
        if _STOP_AFTER == "p3":
            b.barrier()
            return _emit(nc, es, b)

        m2s_dbg = []
        with contextlib.ExitStack() as pT:
            sbT = lambda n, s, d: pT.enter_context(nc.sbuf_tensor(n, s, d))
            h_bf = [sbT(f"hbf{c}", [128, TOK], BF16) for c in range(CC)]
            V_bf = [sbT(f"vbf{c}", [128, TOK], BF16) for c in range(CC)]
            wsl = [sbT(f"wsl{i}", [128, C], BF16) for i in range(2)]

            with contextlib.ExitStack() as sA:
                sba = lambda n, s, d: sA.enter_context(nc.sbuf_tensor(n, s, d))
                xb = [sba(f"axb{i}", [128, TOK], BF16) for i in range(2)]
                sqb = [sba(f"asq{i}", [128, TOK], BF16) for i in range(2)]
                m2 = sba("m2", [128, TOK], F32)
                r2 = sba("r2", [128, TOK], F32)
                m2s_dbg.extend([m2, r2])
                tmp = sba("atmp", [128, TT], F32)
                xaf = sba("xaf", [128, TOK], F32)
                xaw = [sba(f"xaw{i}", [128, TOK], BF16) for i in range(2)]

                def srcA(c, sl):
                    b.op("scalar", lambda e, sl=sl, c=c: e.activation(
                        xb[sl][:], XR[c][:], ACT.Copy), None)

                emit_stats(srcA, m2, r2, tmp, xb, sqb)
                xag = [None, None]
                for c in range(CC):
                    sl = c % 2
                    b.op("vector", lambda e, c=c: e.tensor_sub(xaf[:], XR[c][:], m2[:]), None)
                    if xag[sl] is not None:
                        b.wait("vector", f"xa{sl}")
                    b.op("vector", lambda e, sl=sl: e.tensor_mul(xaw[sl][:], xaf[:], r2[:]), "v")
                    b.wait("sync", "v")
                    xag[sl] = b.dma("sync", xa_sp[128 * c:128 * (c + 1), :], xaw[sl][:],
                                    sem=f"xa{sl}")
                b.barrier()

            dm2 = dump_tensor("m2r2", [256, TOK])
            if dm2 is not None:
                b.dma("sync", dm2[0:128, :], m2s_dbg[0][:])
                b.dma("sync", dm2[128:256, :], m2s_dbg[1][:])
                b.barrier()
            with contextlib.ExitStack() as sB:
                sbb = lambda n, s, d: sB.enter_context(nc.sbuf_tensor(n, s, d))
                xb = [sbb(f"bxb{i}", [128, TOK], BF16) for i in range(2)]
                sqb = [sbb(f"bsq{i}", [128, TOK], BF16) for i in range(2)]
                m3 = sbb("m3", [128, TOK], F32)
                r3 = sbb("r3", [128, TOK], F32)
                tmp = sbb("btmp", [128, TT], F32)
                uf = sbb("uf", [128, TOK], F32)

                def srcB(c, sl):
                    b.wait("sync", "s")
                    b.dma("sync", xb[sl][:], xa_sp[128 * c:128 * (c + 1), :])
                    b.wait("scalar", "ds")

                ps_s = [banks.acquire(b, i) for i in range(4)]
                ps_q = [banks.acquire(b, i + 4) for i in range(4)]
                g = [None, None]
                for c in range(CC):
                    sl = c % 2
                    if g[sl] is not None:
                        b.wait("sync", "t", g[sl])
                    srcB(c, sl)
                    b.op("scalar", lambda e, sl=sl: e.activation(sqb[sl][:], xb[sl][:], ACT.Square), "s")
                    b.wait("tensor", "s")
                    for tt in range(NT):
                        b.op("tensor", lambda e, sl=sl, tt=tt, c=c: e.matmul(
                            ps_s[tt][:], onesb[:], xb[sl][:, TT * tt:TT * (tt + 1)],
                            start=(c == 0), stop=(c == CC - 1)), None)
                    for tt in range(NT):
                        g[sl] = b.op("tensor", lambda e, sl=sl, tt=tt, c=c: e.matmul(
                            ps_q[tt][:], onesb[:], sqb[sl][:, TT * tt:TT * (tt + 1)],
                            start=(c == 0), stop=(c == CC - 1)), "t")
                b.wait("scalar", "t")
                for tt in range(NT):
                    sl_ = slice(TT * tt, TT * (tt + 1))
                    b.op("scalar", lambda e, tt=tt, sl_=sl_: e.activation(
                        m3[:, sl_], ps_s[tt][:], ACT.Copy, scale=1.0 / C), "s")
                    b.op("scalar", lambda e, tt=tt, sl_=sl_: e.activation(
                        r3[:, sl_], ps_q[tt][:], ACT.Copy, scale=1.0 / C), "s")
                b.wait("vector", "s")
                for tt in range(NT):
                    sl_ = slice(TT * tt, TT * (tt + 1))
                    b.op("vector", lambda e, sl_=sl_: e.tensor_mul(tmp[:], m3[:, sl_], m3[:, sl_]), None)
                    b.op("vector", lambda e, sl_=sl_: e.tensor_sub(r3[:, sl_], r3[:, sl_], tmp[:]), "v")
                b.wait("scalar", "v")
                b.op("scalar", lambda e: e.activation(r3[:], r3[:], ACT.Ln, bias=epsb[:]), None)
                b.op("scalar", lambda e: e.activation(r3[:], r3[:], ACT.Exp, scale=-0.5), "s")
                for i in range(8):
                    banks.release(i, "s", b.cur("s"))
                b.wait("vector", "s")
                for c in range(CC):
                    sl = c % 2
                    b.wait("sync", "v")
                    b.wait("sync", "s")
                    b.dma("sync", xb[sl][:], xa_sp[128 * c:128 * (c + 1), :])
                    b.wait("vector", "ds")
                    b.op("vector", lambda e, sl=sl: e.tensor_sub(uf[:], xb[sl][:], m3[:]), None)
                    b.op("vector", lambda e: e.tensor_mul(uf[:], uf[:], r3[:]), "v")
                    b.wait("scalar", "v")
                    b.op("scalar", lambda e, c=c: e.activation(
                        h_bf[c][:], uf[:], ACT.Identity,
                        scale=gT[:, c:c + 1], bias=bT[:, c:c + 1]), "s")
                b.barrier()
            dxa = dump_tensor("xa", [C, TOK], BF16)
            if dxa is not None:
                b.wait("sync", "xa0")
                b.wait("sync", "xa1")
                b.dma("sync", dxa, xa_sp)
                b.barrier()
            dh = dump_tensor("h", [C, TOK], BF16)
            if dh is not None:
                for c in range(CC):
                    b.dma("sync", dh[128 * c:128 * (c + 1), :], h_bf[c][:])
                b.barrier()

            with contextlib.ExitStack() as sC:
                sbc = lambda n, s, d: sC.enter_context(nc.sbuf_tensor(n, s, d))
                K_bf = [sbc(f"kbf{c}", [128, TOK], BF16) for c in range(CC)]
                KA = sbc("KA", [128, TOK], BF16)
                onesf = sbc("onesf", [128, 2], F32)
                bsl = [sbc(f"bsl{i}", [128, 128], BF16) for i in range(2)]
                errt = sbc("errt", [128, TOK], F32)
                grdt = sbc("grdt", [128, TOK], F32)
                shft = sbc("shft", [128, TOK], F32)
                b.op("vector", lambda e: e.memset(onesf[:], 1.0), "v")

                slab = Slab(wsl, "ws")
                flat = [(wk_s[dc], K_bf, dc) for dc in range(CC)] + \
                       [(wv_s[dc], V_bf, dc) for dc in range(CC)] + [(a_s, None, 0)]
                slab.load(b, 0, flat[0][0])
                for it, (wap, dst, dc) in enumerate(flat):
                    if it + 1 < len(flat):
                        slab.load(b, it + 1, flat[it + 1][0])
                    w = slab.use(b, it)
                    rhs = h_bf if dst is not None else K_bf
                    if dst is None:
                        b.wait("tensor", "s")
                    for i in range(4):
                        banks.acquire(b, i)
                    for c in range(CC):
                        for tt in range(NT):
                            mm = b.op("tensor", lambda e, c=c, tt=tt, w=w, rhs=rhs: e.matmul(
                                banks.t[tt][:], w[:, 128 * c:128 * (c + 1)],
                                rhs[c][:, TT * tt:TT * (tt + 1)],
                                start=(c == 0), stop=(c == CC - 1)),
                                "t" if (c == CC - 1 and tt == NT - 1) else None)
                    slab.done(it, mm)
                    b.wait("scalar", "t", mm)
                    out_t = (dst[dc] if dst is not None else KA)
                    for tt in range(NT):
                        sv = b.op("scalar", lambda e, tt=tt, out_t=out_t: e.activation(
                            out_t[:, TT * tt:TT * (tt + 1)], banks.t[tt][:], ACT.Copy), "s")
                    for i in range(4):
                        banks.release(i, "s", sv)

                bslab = Slab(bsl, "bs")
                slab.load(b, 17, w0_s[0])
                bslab.load(b, 0, b_s[0])
                eg = None
                sg_ = None
                for dc in range(CC):
                    if dc + 1 < CC:
                        slab.load(b, 17 + dc + 1, w0_s[dc + 1])
                        bslab.load(b, dc + 1, b_s[dc + 1])
                    w = slab.use(b, 17 + dc)
                    bw = bslab.use(b, dc)
                    for i in range(4):
                        banks.acquire(b, i)
                    for c in range(CC):
                        for tt in range(NT):
                            b.op("tensor", lambda e, c=c, tt=tt, w=w: e.matmul(
                                banks.t[tt][:], w[:, 128 * c:128 * (c + 1)],
                                K_bf[c][:, TT * tt:TT * (tt + 1)],
                                start=(c == 0), stop=False), None)
                    for tt in range(NT):
                        mm = b.op("tensor", lambda e, tt=tt, bw=bw: e.matmul(
                            banks.t[tt][:], bw[:], KA[:, TT * tt:TT * (tt + 1)],
                            start=False, stop=True),
                            "t" if tt == NT - 1 else None)
                    slab.done(17 + dc, mm)
                    bslab.done(dc, mm)
                    b.wait("vector", "t", mm)
                    if eg is not None:
                        b.wait("vector", "es")
                    for tt in range(NT):
                        vv = b.op("vector", lambda e, tt=tt, dc=dc: e.tensor_sub(
                            errt[:, TT * tt:TT * (tt + 1)], banks.t[tt][:],
                            V_bf[dc][:, TT * tt:TT * (tt + 1)]), "v")
                    for i in range(4):
                        banks.release(i, "v", vv)
                    b.op("vector", lambda e, dc=dc: e.tensor_mul(
                        grdt[:], K_bf[dc][:], errt[:]), None)
                    b.op("vector", lambda e: e.memset(shft[:, 0:1], 0.0), None)
                    b.op("vector", lambda e: e.tensor_tensor_scan(
                        shft[:, 1:TOK], onesf[:, 0:1].broadcast_to([128, TOK - 1]),
                        grdt[:, 0:TOK - 1], 0.0, ALU.mult, ALU.add), "v")
                    b.wait("vector", "v")
                    b.op("vector", lambda e, dc=dc: e.tensor_add(
                        carry[:, dc:dc + 1], shft[:, TOK - 1:TOK], grdt[:, TOK - 1:TOK]), "v")
                    b.wait("sync", "v")
                    eg = b.dma("sync", err_sp[128 * dc:128 * (dc + 1), :], errt[:], sem="es")
                    sg_ = b.dma("sync", sh_sp[128 * dc:128 * (dc + 1), :], shft[:], sem="es")
                b.barrier()

            b.op("vector", lambda e: e.tensor_scalar(carry[:], carry[:], ieb[:], None, ALU.mult), "v")
            b.wait("gpsimd", "v")
            b.dma("gpsimd", cy_i, carry[:])
            b.wait("gpsimd", "dg")
            b.op("gpsimd", lambda e: e.collective_compute(
                "AllReduce", ALU.add, replica_groups=[[0, 1], [2, 3], [4, 5], [6, 7]],
                ins=[cy_i], outs=[cy_o]), "cc")
            b.wait("gpsimd", "cc")
            b.dma("gpsimd", carry[:], cy_o)
            b.wait("vector", "dg")
            b.op("vector", lambda e: e.tensor_scalar(carry[:], carry[:], iob[:], None, ALU.mult), "v")
            b.barrier()

            with contextlib.ExitStack() as sF:
                sbff = lambda n, s, d: sF.enter_context(nc.sbuf_tensor(n, s, d))
                psb = sbff("psb", [128, TOK], F32)
                qlr = sbff("qlr", [128, TOK], F32)
                c1t = sbff("c1t", [128, TOK], F32)
                errF = [sbff(f"errF{i}", [128, TOK], F32) for i in range(2)]
                shF = [sbff(f"shF{i}", [128, TOK], F32) for i in range(2)]
                opb = [sbff(f"opb{i}", [128, TOK], BF16) for i in range(2)]
                b.dma("gpsimd", psb[:], posr.partition_broadcast(128))
                og = [None, None]
                cons = [None, None]
                sld = [None, None]
                slab = Slab(wsl, "ws")
                slab.load(b, 0, wq_s[0])

                def ld_es(dc):
                    sl = dc % 2
                    if cons[sl] is not None:
                        b.wait("sync", "v", cons[sl])
                    b.dma("sync", errF[sl][:], err_sp[128 * dc:128 * (dc + 1), :],
                          sem=f"ef{sl}")
                    sld[sl] = b.dma("sync", shF[sl][:], sh_sp[128 * dc:128 * (dc + 1), :],
                                    sem=f"ef{sl}")

                ld_es(0)
                for dc in range(CC):
                    sl = dc % 2
                    if dc + 1 < CC:
                        slab.load(b, dc + 1, wq_s[dc + 1])
                    w = slab.use(b, dc)
                    myld = sld[sl]
                    for i in range(4):
                        banks.acquire(b, i)
                    for c in range(CC):
                        for tt in range(NT):
                            mm = b.op("tensor", lambda e, c=c, tt=tt, w=w: e.matmul(
                                banks.t[tt][:], w[:, 128 * c:128 * (c + 1)],
                                h_bf[c][:, TT * tt:TT * (tt + 1)],
                                start=(c == 0), stop=(c == CC - 1)),
                                "t" if (c == CC - 1 and tt == NT - 1) else None)
                    slab.done(dc, mm)
                    b.wait("vector", "t", mm)
                    b.wait("vector", f"ef{sl}")
                    b.wait("vector", "dg")
                    for tt in range(NT):
                        vv = b.op("vector", lambda e, tt=tt, dc=dc: e.scalar_tensor_tensor(
                            qlr[:, TT * tt:TT * (tt + 1)], banks.t[tt][:],
                            lrT[:, dc:dc + 1], psb[:, TT * tt:TT * (tt + 1)],
                            ALU.mult, ALU.mult), "v")
                    for i in range(4):
                        banks.release(i, "v", vv)
                    b.op("vector", lambda e, dc=dc, sl=sl: e.scalar_tensor_tensor(
                        c1t[:], shF[sl][:], carry[:, dc:dc + 1], qlr[:],
                        ALU.add, ALU.mult), None)
                    cv = b.op("vector", lambda e, sl=sl: e.tensor_sub(qlr[:], errF[sl][:], c1t[:]), "v")
                    cons[sl] = cv
                    if dc + 1 < CC:
                        ld_es(dc + 1)
                    if og[sl] is not None:
                        b.wait("vector", f"os{sl}")
                    b.op("vector", lambda e, dc=dc, sl=sl: e.tensor_add(
                        opb[sl][:], qlr[:], V_bf[dc][:]), "v")
                    b.wait("sync", "v")
                    og[sl] = b.dma("sync", op_sp[128 * dc:128 * (dc + 1), :], opb[sl][:],
                                   sem=f"os{sl}")
                b.barrier()

            with contextlib.ExitStack() as sG:
                sbg = lambda n, s, d: sG.enter_context(nc.sbuf_tensor(n, s, d))
                xaG = [sbg(f"xag{c}", [128, TOK], BF16) for c in range(CC)]
                gtb = [sbg(f"gtb{i}", [128, TOK], BF16) for i in range(2)]
                opG = [sbg(f"opG{i}", [128, TOK], BF16) for i in range(2)]
                x1t = sbg("x1t", [128, TOK], F32)
                og_bf = h_bf
                for c in range(CC):
                    b.dma("sync", xaG[c][:], xa_sp[128 * c:128 * (c + 1), :])
                b.wait("tensor", "ds")
                slab = Slab(wsl, "ws")
                slab.load(b, 0, wg_s[0])
                for dc in range(CC):
                    sl = dc % 2
                    if dc + 1 < CC:
                        slab.load(b, dc + 1, wg_s[dc + 1])
                    w = slab.use(b, dc)
                    b.wait("sync", "v")
                    opg = b.dma("sync", opG[sl][:], op_sp[128 * dc:128 * (dc + 1), :],
                                sem=f"op{sl}")
                    for i in range(4):
                        banks.acquire(b, i)
                    for c in range(CC):
                        for tt in range(NT):
                            mm = b.op("tensor", lambda e, c=c, tt=tt, w=w: e.matmul(
                                banks.t[tt][:], w[:, 128 * c:128 * (c + 1)],
                                xaG[c][:, TT * tt:TT * (tt + 1)],
                                start=(c == 0), stop=(c == CC - 1)),
                                "t" if (c == CC - 1 and tt == NT - 1) else None)
                    slab.done(dc, mm)
                    b.wait("scalar", "t", mm)
                    b.wait("scalar", "v")
                    for tt in range(NT):
                        sv = b.op("scalar", lambda e, tt=tt, sl=sl: e.activation(
                            gtb[sl][:, TT * tt:TT * (tt + 1)], banks.t[tt][:], ACT.Sigmoid), "s")
                    for i in range(4):
                        banks.release(i, "s", sv)
                    b.wait("vector", "s", sv)
                    b.wait("vector", f"op{sl}")
                    b.op("vector", lambda e, dc=dc, sl=sl: e.tensor_mul(
                        og_bf[dc][:], opG[sl][:], gtb[sl][:]), "v")
                b.barrier()

                slab2 = Slab(wsl, "ws")
                slab2.load(b, 0, wo_s[0])
                for hc in range(CC):
                    sl = hc % 2
                    if hc + 1 < CC:
                        slab2.load(b, hc + 1, wo_s[hc + 1])
                    w = slab2.use(b, hc)
                    for i in range(4):
                        banks.acquire(b, i)
                    for dc in range(CC):
                        for tt in range(NT):
                            mm = b.op("tensor", lambda e, dc=dc, tt=tt, w=w: e.matmul(
                                banks.t[tt][:], w[:, 128 * dc:128 * (dc + 1)],
                                og_bf[dc][:, TT * tt:TT * (tt + 1)],
                                start=(dc == 0), stop=(dc == CC - 1)),
                                "t" if (dc == CC - 1 and tt == NT - 1) else None)
                    slab2.done(hc, mm)
                    b.op("vector", lambda e, hc=hc: e.tensor_add(
                        x1t[:], XR[hc][:], xaG[hc][:]), None)
                    b.wait("vector", "t", mm)
                    for tt in range(NT):
                        vv = b.op("vector", lambda e, tt=tt, hc=hc: e.scalar_tensor_tensor(
                            XR[hc][:, TT * tt:TT * (tt + 1)], banks.t[tt][:],
                            lsT[:, hc:hc + 1], x1t[:, TT * tt:TT * (tt + 1)],
                            ALU.mult, ALU.add), "v")
                    for i in range(4):
                        banks.release(i, "v", vv)
                b.barrier()

        dx2 = dump_tensor("x2", [C, TOK])
        if dx2 is not None:
            for c in range(CC):
                b.dma("sync", dx2[128 * c:128 * (c + 1), :], XR[c][:])
            b.barrier()
        if _STOP_AFTER == "p4":
            b.barrier()
            return _emit(nc, es, b)

        with contextlib.ExitStack() as pB:
            sbB = lambda n, s, d: pB.enter_context(nc.sbuf_tensor(n, s, d))
            xq_bf = [sbB(f"xqc{c}", [128, TOK], BF16) for c in range(CC)]

            def p5_chunk(c, sl, engs):
                return XR[c][:]

            emit_ln_quant(p5_chunk, xq_bf, 2, "ln3")
            emit_ffn_gate_up(xq_bf, "f2", dbufw=False)
            emit_global_scale(IC, 1, cm_i[3], cm_o[3])

            def p5_src(hc, half, sl):
                return XR[hc][:, 1024 * half:1024 * (half + 1)]

            emit_ffn_down(p5_src, True, "f2")

        b.barrier()
        return _emit(nc, es, b)


def _emit(nc, es, b):
    block = es.enter_context(nc.Block())

    @block.sync
    def _(e):
        for fn in b.q["sync"]:
            fn(e)

    @block.gpsimd
    def _(e):
        for fn in b.q["gpsimd"]:
            fn(e)

    @block.vector
    def _(e):
        for fn in b.q["vector"]:
            fn(e)

    @block.scalar
    def _(e):
        for fn in b.q["scalar"]:
            fn(e)

    @block.tensor
    def _(e):
        for fn in b.q["tensor"]:
            fn(e)

    return nc


def kernel(**inputs):
    f4 = np.float32
    x = np.asarray(inputs["L0_attn_x"], f4).reshape(NCORE * TOK, C)
    Wq = np.asarray(inputs["L0_attn_Wq"], f4)
    Wk = np.asarray(inputs["L0_attn_Wk"], f4)
    Wv = np.asarray(inputs["L0_attn_Wv"], f4)
    Wo = np.asarray(inputs["L0_attn_Wo"], f4)
    Wg = np.asarray(inputs["L0_attn_Wg"], f4)
    W0 = np.asarray(inputs["L0_attn_W0"], f4)
    A_init = np.asarray(inputs["L0_attn_A_init"], f4)
    B_init = np.asarray(inputs["L0_attn_B_init"], f4)

    wq_s = _slab(np.ascontiguousarray(Wq.T), CC)
    wk_s = _slab(np.ascontiguousarray(Wk.T), CC)
    wv_s = _slab(np.ascontiguousarray(Wv.T), CC)
    wg_s = _slab(np.ascontiguousarray(Wg.T), CC)
    wo_s = _slab(np.ascontiguousarray(Wo.T), CC)
    w0_s = _slab(np.ascontiguousarray(W0), CC)
    A_pad = np.zeros((C, 128), f4)
    A_pad[:, :8] = A_init
    a_s = _slab(A_pad, 1).reshape(128, C)
    B_pad = np.zeros((128, C), f4)
    B_pad[:8, :] = B_init
    b_s = np.ascontiguousarray(B_pad.reshape(128, CC, 128).transpose(1, 0, 2))

    pos = np.arange(1, 4097, dtype=f4)
    pos_scale = (f4(1.0) / (f4(1.0) + f4(0.1) * np.log(pos))).astype(f4)

    common = dict(
        wq_s=wq_s, wk_s=wk_s, wv_s=wv_s, wg_s=wg_s, wo_s=wo_s, w0_s=w0_s,
        a_s=a_s, b_s=b_s,
        wgT=np.ascontiguousarray(np.asarray(inputs["L0_ffn_w_gate"], f4).T),
        wuT=np.ascontiguousarray(np.asarray(inputs["L0_ffn_w_up"], f4).T),
        wdT=np.ascontiguousarray(np.asarray(inputs["L0_ffn_w_down"], f4).T),
        wgN=np.ascontiguousarray(np.asarray(inputs["L0_ffn_w_gate"], f4)),
        wuN=np.ascontiguousarray(np.asarray(inputs["L0_ffn_w_up"], f4)),
        wdN=np.ascontiguousarray(np.asarray(inputs["L0_ffn_w_down"], f4)),
        logr=_vec8(inputs["L0_attn_log_lr"]), lng=_vec8(inputs["L0_attn_ln_g"]),
        lnb=_vec8(inputs["L0_attn_ln_b"]), lsg=_vec8(inputs["L0_attn_ls_g"]),
        clipx=np.asarray(inputs["L0_ffn_clip_x"], f4).reshape(1, 1),
        clipd=np.asarray(inputs["L0_ffn_clip_down"], f4).reshape(1, 1),
    )
    in_maps = []
    for k in range(NCORE):
        m = dict(common)
        m["xT"] = np.ascontiguousarray(x[TOK * k:TOK * (k + 1), :].T)
        seq_half = k % 2
        m["posr"] = np.ascontiguousarray(
            pos_scale[TOK * seq_half:TOK * (seq_half + 1)].reshape(1, TOK))
        m["iseven"] = np.asarray([[1.0 if k % 2 == 0 else 0.0]], f4)
        m["isodd"] = np.asarray([[1.0 if k % 2 == 1 else 0.0]], f4)
        in_maps.append(m)

    nc = build_program()
    res = run_bass_kernel_spmd(nc, in_maps, core_ids=list(range(NCORE)))
    globals()["_LAST_RESULTS"] = res
    out = np.empty((NCORE * TOK, C), f4)
    for k in range(NCORE):
        out[TOK * k:TOK * (k + 1), :] = res.results[k]["oxT"].T
    return out.reshape(4, 4096, C)


# version 08
# speedup vs baseline: 1.0183x; 1.0183x over previous
import contextlib
import numpy as np
import ml_dtypes
import concourse.bass as bass
import concourse.mybir as mybir
from concourse.bass_utils import run_bass_kernel_spmd

F32 = mybir.dt.float32
BF16 = mybir.dt.bfloat16
I8 = mybir.dt.int8
ALU = mybir.AluOpType
ACT = mybir.ActivationFunctionType
AX = mybir.AxisListType.X

NCORE = 8
TOK = 2048
C = 1024
CC = 8
I = 2816
IC = 22
NT = 4
TT = 512
EPS = 1e-5

ENGS = ("sync", "gpsimd", "vector", "scalar", "tensor")

_STOP_AFTER = None
_DUMPS = []


class B:

    def __init__(self, nc, es):
        self.nc = nc
        self.es = es
        self.q = {e: [] for e in ENGS}
        self.sems = {}

    def sem(self, name):
        if name not in self.sems:
            h = self.es.enter_context(self.nc.semaphore(name))
            self.sems[name] = [h, 0]
        return self.sems[name]

    def op(self, eng, fn, sem=None, k=1):
        if sem is None:
            self.q[eng].append(fn)
            return None
        rec = self.sem(sem)
        rec[1] += k
        h = rec[0]
        self.q[eng].append(lambda e, fn=fn, h=h, k=k: fn(e).then_inc(h, k))
        return rec[1]

    def dma(self, eng, out, in_, sem=None, **kw):
        if sem is None:
            sem = {"sync": "ds", "gpsimd": "dg", "scalar": "dsc"}[eng]
        return self.op(eng, lambda e: e.dma_start(out=out, in_=in_, **kw), sem, 16)

    def wait(self, eng, sem, val=None):
        rec = self.sem(sem)
        v = rec[1] if val is None else val
        if v is None or v <= 0:
            return
        h = rec[0]
        self.q[eng].append(lambda e, h=h, v=v: e.wait_ge(h, v))

    def cur(self, sem):
        return self.sem(sem)[1]

    def barrier(self, engs=ENGS):
        snap = [(name, rec[1]) for name, rec in self.sems.items() if rec[1] > 0]
        for e in engs:
            for name, v in snap:
                self.wait(e, name, v)


class Banks:
    def __init__(self, nc, es):
        self.t = [es.enter_context(nc.psum_tensor(f"pb{i}", [128, TT], F32))
                  for i in range(8)]
        self.guard = [None] * 8

    def acquire(self, b, i):
        if self.guard[i] is not None:
            b.wait("tensor", *self.guard[i])
            self.guard[i] = None
        return self.t[i]

    def release(self, i, sem, val):
        self.guard[i] = (sem, val)


class Slab:

    def __init__(self, tiles, name):
        self.tiles = tiles
        self.name = name
        self.guard = [None] * len(tiles)

    def load(self, b, i, src):
        sl = i % len(self.tiles)
        if self.guard[sl] is not None:
            b.wait("gpsimd", self.guard[sl][0], self.guard[sl][1])
            self.guard[sl] = None
        b.dma("gpsimd", self.tiles[sl][:], src, sem=f"{self.name}{sl}")

    def use(self, b, i):
        sl = i % len(self.tiles)
        b.wait("tensor", f"{self.name}{sl}")
        return self.tiles[sl]

    def done(self, i, mmval, sem="t"):
        self.guard[i % len(self.tiles)] = (sem, mmval)


def _slab(wT, n_out_chunks):
    K = wT.shape[0]
    kc = K // 128
    a = wT.reshape(kc, 128, n_out_chunks, 128)
    return np.ascontiguousarray(a.transpose(2, 1, 0, 3).reshape(n_out_chunks, 128, K))


def _vec8(v):
    return np.ascontiguousarray(np.asarray(v, np.float32).reshape(CC, 128).T)


def build_program():
    nc = bass.Bass("TRN2", target_bir_lowering=False)
    ap = lambda t: t.ap()

    xT = ap(nc.dram_tensor("xT", [C, TOK], F32, kind="ExternalInput"))
    wq_s = ap(nc.dram_tensor("wq_s", [CC, 128, C], F32, kind="ExternalInput"))
    wk_s = ap(nc.dram_tensor("wk_s", [CC, 128, C], F32, kind="ExternalInput"))
    wv_s = ap(nc.dram_tensor("wv_s", [CC, 128, C], F32, kind="ExternalInput"))
    wg_s = ap(nc.dram_tensor("wg_s", [CC, 128, C], F32, kind="ExternalInput"))
    wo_s = ap(nc.dram_tensor("wo_s", [CC, 128, C], F32, kind="ExternalInput"))
    w0_s = ap(nc.dram_tensor("w0_s", [CC, 128, C], F32, kind="ExternalInput"))
    a_s = ap(nc.dram_tensor("a_s", [128, C], F32, kind="ExternalInput"))
    b_s = ap(nc.dram_tensor("b_s", [CC, 128, 128], F32, kind="ExternalInput"))
    wgT = ap(nc.dram_tensor("wgT", [C, I], F32, kind="ExternalInput"))
    wuT = ap(nc.dram_tensor("wuT", [C, I], F32, kind="ExternalInput"))
    wdT = ap(nc.dram_tensor("wdT", [I, C], F32, kind="ExternalInput"))
    wgN = ap(nc.dram_tensor("wgN", [I, C], F32, kind="ExternalInput"))
    wuN = ap(nc.dram_tensor("wuN", [I, C], F32, kind="ExternalInput"))
    wdN = ap(nc.dram_tensor("wdN", [C, I], F32, kind="ExternalInput"))
    logr = ap(nc.dram_tensor("logr", [128, CC], F32, kind="ExternalInput"))
    lng = ap(nc.dram_tensor("lng", [128, CC], F32, kind="ExternalInput"))
    lnb = ap(nc.dram_tensor("lnb", [128, CC], F32, kind="ExternalInput"))
    lsg = ap(nc.dram_tensor("lsg", [128, CC], F32, kind="ExternalInput"))
    clipx = ap(nc.dram_tensor("clipx", [1, 1], F32, kind="ExternalInput"))
    clipd = ap(nc.dram_tensor("clipd", [1, 1], F32, kind="ExternalInput"))
    posr = ap(nc.dram_tensor("posr", [1, TOK], F32, kind="ExternalInput"))
    iseven = ap(nc.dram_tensor("iseven", [1, 1], F32, kind="ExternalInput"))
    isodd = ap(nc.dram_tensor("isodd", [1, 1], F32, kind="ExternalInput"))

    oxT = ap(nc.dram_tensor("oxT", [C, TOK], F32, kind="ExternalOutput"))

    wgq = ap(nc.dram_tensor("wgq", [C, I], I8, kind="Internal"))
    wuq = ap(nc.dram_tensor("wuq", [C, I], I8, kind="Internal"))
    wdq = ap(nc.dram_tensor("wdq", [I, C], I8, kind="Internal"))
    t_sp = ap(nc.dram_tensor("t_sp", [I, TOK], F32, kind="Internal"))
    xa_sp = ap(nc.dram_tensor("xa_sp", [C, TOK], BF16, kind="Internal"))
    err_sp = ap(nc.dram_tensor("err_sp", [C, TOK], F32, kind="Internal"))
    sh_sp = ap(nc.dram_tensor("sh_sp", [C, TOK], F32, kind="Internal"))
    op_sp = ap(nc.dram_tensor("op_sp", [C, TOK], BF16, kind="Internal"))
    srow = ap(nc.dram_tensor("srow", [1, max(I, C)], F32, kind="Internal"))
    stage = ap(nc.dram_tensor("stage", [1, 16], F32, kind="Internal"))
    cw_i = ap(nc.dram_tensor("cw_i", [1, 8], F32, kind="Internal"))
    cw_o = ap(nc.dram_tensor("cw_o", [1, 8], F32, kind="Internal"))
    cm_i = [ap(nc.dram_tensor(f"cm_i{k}", [1, 8], F32, kind="Internal")) for k in range(4)]
    cm_o = [ap(nc.dram_tensor(f"cm_o{k}", [1, 8], F32, kind="Internal")) for k in range(4)]
    cy_i = ap(nc.dram_tensor("cy_i", [128, 8], F32, kind="Internal"))
    cy_o = ap(nc.dram_tensor("cy_o", [128, 8], F32, kind="Internal"))

    dumps = {}

    def dump_tensor(name, shape, dtype=F32):
        if name in _DUMPS:
            dumps[name] = ap(nc.dram_tensor("dbg_" + name, shape, dtype,
                                            kind="ExternalOutput"))
            return dumps[name]
        return None

    es = contextlib.ExitStack()
    with es:
        sbuf = lambda n, s, d: es.enter_context(nc.sbuf_tensor(n, s, d))
        b = B(nc, es)
        banks = Banks(nc, es)

        XR = []
        lrT = sbuf("lrT", [128, CC], F32)
        gT = sbuf("gT", [128, CC], F32)
        bT = sbuf("bT", [128, CC], F32)
        lsT = sbuf("lsT", [128, CC], F32)
        sgN = sbuf("sgN", [128, IC], F32)
        suN = sbuf("suN", [128, IC], F32)
        shN = sbuf("shN", [128, CC], F32)
        sgC = sbuf("sgC", [128, IC], F32)
        suC = sbuf("suC", [128, IC], F32)
        sdC = sbuf("sdC", [128, CC], F32)
        onesb = sbuf("onesb", [128, 128], BF16)
        sigx = sbuf("sigx", [1, 2], F32)
        rrow = sbuf("rrow", [1, 128], F32)
        sc1 = sbuf("sc1", [1, 8], F32)
        invb = sbuf("invb", [128, 1], F32)
        scb = sbuf("scb", [128, 1], F32)
        amaxA = sbuf("amaxA", [128, 32], F32)
        colsc = sbuf("colsc", [128, 1], F32)
        ieb = sbuf("ieb", [128, 1], F32)
        iob = sbuf("iob", [128, 1], F32)
        carry = sbuf("carry", [128, CC], F32)
        epsb = sbuf("epsb", [128, 1], F32)

        b.dma("sync", lrT[:], logr)
        b.dma("sync", gT[:], lng)
        b.dma("sync", bT[:], lnb)
        b.dma("sync", lsT[:], lsg)
        b.dma("gpsimd", ieb[:], iseven.partition_broadcast(128))
        b.dma("gpsimd", iob[:], isodd.partition_broadcast(128))
        b.dma("gpsimd", sc1[0:1, 0:1], clipx)
        b.dma("gpsimd", sc1[0:1, 1:2], clipd)

        b.op("vector", lambda e: e.memset(rrow[:], 0.0), "v")
        b.op("vector", lambda e: e.memset(epsb[:], EPS), "v")
        b.op("vector", lambda e: e.memset(onesb[:], 1.0), "v")
        b.wait("gpsimd", "v")
        b.dma("gpsimd", cw_i, rrow[0:1, 0:8])
        b.wait("gpsimd", "dg")
        b.op("gpsimd", lambda e: e.collective_compute(
            "AllReduce", ALU.add, replica_groups=[list(range(NCORE))],
            ins=[cw_i], outs=[cw_o]), "cc")

        b.wait("scalar", "dg")
        b.op("scalar", lambda e: e.activation(sigx[0:1, 0:2], sc1[0:1, 0:2], ACT.Sigmoid), "s")
        b.wait("scalar", "ds")
        b.op("scalar", lambda e: e.activation(lrT[:], lrT[:], ACT.Exp), "s")
        b.wait("vector", "s")
        b.op("vector", lambda e: e.tensor_scalar_min(lrT[:], lrT[:], 1.0), "v")

        WPREP = [
            ("g", wgN, wgT, wgq, IC, CC, C, I, sgN),
            ("u", wuN, wuT, wuq, IC, CC, C, I, suN),
            ("d", wdN, wdT, wdq, CC, IC, I, C, shN),
        ]
        wp_marks = {}
        natA = []
        trnA = []

        wp_ng = {}

        def emit_wp_nat(nat, wn, n_nat, nat_w, sN, tagn):
            ns = len(nat)
            ng = wp_ng.setdefault(tagn, [None] * ns)
            for ch in range(n_nat):
                sl = ch % ns
                if ng[sl] is not None:
                    b.wait("sync", "v", ng[sl])
                b.dma("sync", nat[sl][:, 0:nat_w], wn[128 * ch:128 * (ch + 1), :],
                      sem=f"{tagn}{sl}")
                b.wait("vector", f"{tagn}{sl}")
                ng[sl] = b.op("vector", lambda e, sl=sl, ch=ch, nat_w=nat_w, sN=sN:
                              e.reduce_max(sN[:, ch:ch + 1], nat[sl][:, 0:nat_w], AX,
                                           apply_absolute_value=True), "v")

        def emit_weight_prep_a():
            emit_wp_nat(natA, wgN, IC, C, sgN, "wn")
            emit_wp_nat(natA, wuN, IC, C, suN, "wn")
            emit_wp_nat(natA, wdN, CC, I, shN, "wn")

        def emit_weight_prep_b():
            with contextlib.ExitStack() as p0:
                sb0 = lambda n, s, d: p0.enter_context(nc.sbuf_tensor(n, s, d))
                trn = trnA
                qtmp = [sb0(f"qtmp{i}", [128, I], F32) for i in range(2)]
                q8w = [sb0(f"q8w{i}", [128, I], I8) for i in range(2)]
                recb = sb0("recb", [128, I], F32)
                sccol = sb0("sccol", [128, IC], F32)
                lastw = [None, None]
                tg = [None, None]
                for (key, wn, wt, wq_out, n_nat, n_tr, nat_w, tr_w, sN) in WPREP:
                    b.wait("vector", "v")
                    b.wait("vector", "dg")
                    b.op("vector", lambda e, sN=sN, n=n_nat:
                         e.reciprocal(sccol[:, 0:n], sN[:, 0:n]), "v")
                    b.wait("vector", "v")
                    b.op("vector", lambda e, n=n_nat:
                         e.tensor_scalar_mul(sccol[:, 0:n], sccol[:, 0:n], 127.0), "v")
                    b.op("vector", lambda e, sN=sN, n=n_nat:
                         e.tensor_scalar_mul(sN[:, 0:n], sN[:, 0:n], 1.0 / 127.0), "v")
                    b.wait("gpsimd", "v")
                    for ch in range(n_nat):
                        b.dma("gpsimd", srow[0:1, 128 * ch:128 * (ch + 1)],
                              sccol[:, ch:ch + 1])
                    b.wait("gpsimd", "dg")
                    b.dma("gpsimd", recb[:, 0:128 * n_nat],
                          srow[0:1, 0:128 * n_nat].partition_broadcast(128))
                    for ch in range(n_tr):
                        sl = ch % 2
                        if tg[sl] is not None:
                            b.wait("sync", "v", tg[sl])
                        b.dma("sync", trn[sl][:, 0:tr_w], wt[128 * ch:128 * (ch + 1), :],
                              sem=f"wt{sl}")
                        b.wait("vector", f"wt{sl}")
                        b.wait("vector", "dg")
                        b.wait("vector", "s")
                        tg[sl] = b.op("vector", lambda e, sl=sl, tr_w=tr_w:
                                      e.tensor_mul(qtmp[sl][:, 0:tr_w], trn[sl][:, 0:tr_w],
                                                   recb[:, 0:tr_w]), "v")
                        if lastw[sl] is not None:
                            b.wait("scalar", lastw[sl])
                        b.wait("scalar", "v")
                        b.op("scalar", lambda e, sl=sl, tr_w=tr_w:
                             e.activation(q8w[sl][:, 0:tr_w], qtmp[sl][:, 0:tr_w], ACT.Copy), "s")
                        b.wait("gpsimd", "s")
                        b.dma("gpsimd", wq_out[128 * ch:128 * (ch + 1), :],
                              q8w[sl][:, 0:tr_w], sem=f"wq8{key}{sl}")
                        lastw[sl] = f"wq8{key}{sl}"
                    wp_marks[key] = (b.cur(f"wq8{key}0"), b.cur(f"wq8{key}1"))


        def emit_global_scale(ncols, sig_col, cm_in_ap, cm_out_ap):
            b.wait("vector", "v")
            b.op("vector", lambda e, n=ncols:
                 e.reduce_max(colsc[:], amaxA[:, 0:n], AX), "v")
            b.wait("gpsimd", "v")
            b.dma("gpsimd", rrow[0:1, 0:128], colsc[:, 0:1])
            b.wait("vector", "dg")
            b.op("vector", lambda e: e.reduce_max(sc1[0:1, 2:3], rrow[0:1, 0:128], AX), "v")
            b.wait("gpsimd", "v")
            b.dma("gpsimd", cm_in_ap[0:1, 0:1], sc1[0:1, 2:3])
            b.wait("gpsimd", "dg")
            b.op("gpsimd", lambda e, ci=cm_in_ap, co=cm_out_ap: e.collective_compute(
                "AllReduce", ALU.max, replica_groups=[list(range(NCORE))],
                ins=[ci], outs=[co]), "cc")
            b.wait("gpsimd", "cc")
            b.dma("gpsimd", sc1[0:1, 3:4], cm_out_ap[0:1, 0:1])
            b.wait("vector", "dg")
            b.op("vector", lambda e, sc=sig_col: e.tensor_mul(
                sc1[0:1, 4:5], sc1[0:1, 3:4], sigx[0:1, sc:sc + 1]), "v")
            b.wait("vector", "v")
            b.op("vector", lambda e: e.tensor_scalar_mul(sc1[0:1, 6:7], sc1[0:1, 4:5], 1.0 / 127.0), "v")
            b.wait("gpsimd", "v")
            b.dma("gpsimd", stage[0:1, 0:2], sc1[0:1, 4:6])
            b.wait("gpsimd", "dg")
            b.dma("gpsimd", stage[0:1, 2:4], sc1[0:1, 6:8])
            b.wait("gpsimd", "dg")
            b.dma("gpsimd", colsc[:], stage[0:1, 0:1].partition_broadcast(128))
            b.dma("gpsimd", scb[:], stage[0:1, 2:3].partition_broadcast(128))
            b.wait("vector", "dg")
            b.op("vector", lambda e: e.reciprocal(invb[:], colsc[:]), "v")
            b.wait("vector", "v")
            b.op("vector", lambda e: e.tensor_scalar_mul(invb[:], invb[:], 127.0), "v")
            b.wait("scalar", "v")
            b.wait("scalar", "dg")

        def emit_stats(src_emit, m_b, r_b, tmp, xb, sqb):
            ps_s = [banks.acquire(b, i) for i in range(4)]
            ps_q = [banks.acquire(b, i + 4) for i in range(4)]
            g = [None, None]
            for c in range(CC):
                sl = c % 2
                if g[sl] is not None:
                    b.wait("scalar", "t", g[sl])
                src_emit(c, sl)
                b.op("scalar", lambda e, sl=sl: e.activation(sqb[sl][:], xb[sl][:], ACT.Square), "s")
                b.wait("tensor", "s")
                for tt in range(NT):
                    b.op("tensor", lambda e, sl=sl, tt=tt, c=c: e.matmul(
                        ps_s[tt][:], onesb[:], xb[sl][:, TT * tt:TT * (tt + 1)],
                        start=(c == 0), stop=(c == CC - 1)), None)
                for tt in range(NT):
                    g[sl] = b.op("tensor", lambda e, sl=sl, tt=tt, c=c: e.matmul(
                        ps_q[tt][:], onesb[:], sqb[sl][:, TT * tt:TT * (tt + 1)],
                        start=(c == 0), stop=(c == CC - 1)), "t")
            b.wait("scalar", "t")
            for tt in range(NT):
                sl_ = slice(TT * tt, TT * (tt + 1))
                b.op("scalar", lambda e, tt=tt, sl_=sl_: e.activation(
                    m_b[:, sl_], ps_s[tt][:], ACT.Copy, scale=1.0 / C), "s")
                b.op("scalar", lambda e, tt=tt, sl_=sl_: e.activation(
                    r_b[:, sl_], ps_q[tt][:], ACT.Copy, scale=1.0 / C), "s")
            b.wait("vector", "s")
            for tt in range(NT):
                sl_ = slice(TT * tt, TT * (tt + 1))
                b.op("vector", lambda e, sl_=sl_: e.tensor_mul(tmp[:], m_b[:, sl_], m_b[:, sl_]), None)
                b.op("vector", lambda e, sl_=sl_: e.tensor_sub(r_b[:, sl_], r_b[:, sl_], tmp[:]), "v")
            b.wait("scalar", "v")
            b.op("scalar", lambda e: e.activation(r_b[:], r_b[:], ACT.Ln, bias=epsb[:]), None)
            b.op("scalar", lambda e: e.activation(r_b[:], r_b[:], ACT.Exp, scale=-0.5), "s")
            for i in range(8):
                banks.release(i, "s", b.cur("s"))
            b.wait("vector", "s")

        def emit_ln_quant(src_chunk, xq_bf, quant_cm_idx, ln_name, mid=None,
                          ln_bufs=None):
            with contextlib.ExitStack() as pl:
                sbl = lambda n, s, d: pl.enter_context(nc.sbuf_tensor(n, s, d))
                if ln_bufs is not None:
                    ln1 = ln_bufs
                else:
                    ln1 = [sbl(f"{ln_name}_ln{c}", [128, TOK], F32) for c in range(CC)]
                q8 = [sbl(f"{ln_name}_q8{i}", [128, TOK], I8) for i in range(2)]
                with contextlib.ExitStack() as pxs:
                    sbx = lambda n, s, d: pxs.enter_context(nc.sbuf_tensor(n, s, d))
                    xb = [sbx(f"{ln_name}_xb{i}", [128, TOK], BF16) for i in range(2)]
                    sqb = [sbx(f"{ln_name}_sq{i}", [128, TOK], BF16) for i in range(2)]
                    m_b = sbx(f"{ln_name}_m", [128, TOK], F32)
                    r_b = sbx(f"{ln_name}_r", [128, TOK], F32)
                    tmp = sbx(f"{ln_name}_t", [128, TT], F32)

                    def src_stats(c, sl):
                        apc = src_chunk(c, sl, ("scalar",))
                        b.op("scalar", lambda e, sl=sl, apc=apc: e.activation(
                            xb[sl][:], apc, ACT.Copy), None)

                    emit_stats(src_stats, m_b, r_b, tmp, xb, sqb)
                    apcs = [None] * CC
                    apcs[0] = src_chunk(0, 0, ())
                    for c in range(CC):
                        if c + 1 < CC:
                            apcs[c + 1] = src_chunk(c + 1, (c + 1) % 2, ())
                        b.wait("vector", f"xs{c % 2}") if src_chunk is not None else None
                        apc = apcs[c]
                        b.op("vector", lambda e, c=c, apc=apc: e.tensor_sub(ln1[c][:], apc, m_b[:]), None)
                        b.op("vector", lambda e, c=c: e.tensor_mul(ln1[c][:], ln1[c][:], r_b[:]), None)
                        b.op("vector", lambda e, c=c: e.reduce_max(
                            amaxA[:, c:c + 1], ln1[c][:], AX, apply_absolute_value=True), "v")
                    dmb = dump_tensor(ln_name + "_mb", [128, TOK])
                    if dmb is not None:
                        b.wait("sync", "v")
                        b.dma("sync", dmb, m_b[:])
                        b.barrier()
                    drb = dump_tensor(ln_name + "_rb", [128, TOK])
                    if drb is not None:
                        b.wait("sync", "v")
                        b.dma("sync", drb, r_b[:])
                        b.barrier()
                dam = dump_tensor(ln_name + "_amax", [128, 32])
                if dam is not None:
                    b.wait("sync", "v")
                    b.dma("sync", dam, amaxA[:])
                    b.barrier()
                dln = dump_tensor(ln_name, [C, TOK])
                if dln is not None:
                    b.wait("sync", "v")
                    for c in range(CC):
                        b.dma("sync", dln[128 * c:128 * (c + 1), :], ln1[c][:])
                emit_global_scale(CC, 0, cm_i[quant_cm_idx], cm_o[quant_cm_idx])
                if mid is not None:
                    mid()
                qgv = [None, None]
                for c in range(CC):
                    sl = c % 2
                    if qgv[sl] is not None:
                        b.wait("scalar", f"xqc{sl}")
                    b.op("scalar", lambda e, c=c, sl=sl: e.activation(
                        q8[sl][:], ln1[c][:], ACT.Copy, scale=invb[:]), "s")
                    b.wait("gpsimd", "s")
                    qgv[sl] = b.dma("gpsimd", xq_bf[c][:], q8[sl][:], sem=f"xqc{sl}")
                b.barrier()

        def emit_ffn_gate_up(xq_bf, tag, dbufw=True, wait_marks=None):
            b.op("vector", lambda e: e.tensor_scalar(sgC[:], sgN[:, 0:IC], scb[:], None, ALU.mult), None)
            b.op("vector", lambda e: e.tensor_scalar(suC[:], suN[:, 0:IC], scb[:], None, ALU.mult), "v")
            with contextlib.ExitStack() as pf:
                sbf = lambda n, s, d: pf.enter_context(nc.sbuf_tensor(n, s, d))
                HW = I // 2
                nws = 2 if dbufw else 1
                wgw_ = [[sbf(f"{tag}gw{h}_{c}", [128, HW], BF16) for c in range(CC)]
                        for h in range(nws)]
                wuw_ = [[sbf(f"{tag}uw{h}_{c}", [128, HW], BF16) for c in range(CC)]
                        for h in range(nws)]
                wgw = [wgw_[h % nws] for h in range(2)]
                wuw = [wuw_[h % nws] for h in range(2)]
                silu = [sbf(f"{tag}sil{i}", [128, TOK], F32) for i in range(2)]
                tb = [sbf(f"{tag}tb{i}", [128, TOK], F32) for i in range(2)]
                tspg = [None, None]

                def load_g(h):
                    io = HW * h
                    for c in range(CC):
                        b.dma("gpsimd", wgw[h][c][:], wgq[128 * c:128 * (c + 1), io:io + HW],
                              sem=f"wg{h}")

                def load_u(h):
                    io = HW * h
                    for c in range(CC):
                        b.dma("gpsimd", wuw[h][c][:], wuq[128 * c:128 * (c + 1), io:io + HW],
                              sem=f"wu{h}")

                if wait_marks is not None:
                    b.wait("gpsimd", "wq8g0", wait_marks["g"][0])
                    b.wait("gpsimd", "wq8g1", wait_marks["g"][1])
                load_g(0)
                if dbufw:
                    load_g(1)
                if wait_marks is not None:
                    b.wait("gpsimd", "wq8u0", wait_marks["u"][0])
                    b.wait("gpsimd", "wq8u1", wait_marks["u"][1])
                load_u(0)
                if dbufw:
                    load_u(1)
                b.wait("tensor", "xqc0")
                b.wait("tensor", "xqc1")
                for half in range(2):
                    if not dbufw and half == 1:
                        b.wait("gpsimd", "t")
                        load_g(1)
                        load_u(1)
                    b.wait("tensor", f"wg{half}")
                    for ici in range(IC // 2):
                        ic = half * (IC // 2) + ici
                        sl = ic % 2
                        for i in range(4):
                            banks.acquire(b, i)
                        for c in range(CC):
                            for tt in range(NT):
                                gmm = b.op("tensor", lambda e, c=c, tt=tt, ici=ici, half=half: e.matmul(
                                    banks.t[tt][:],
                                    wgw[half][c][:, 128 * ici:128 * (ici + 1)],
                                    xq_bf[c][:, TT * tt:TT * (tt + 1)],
                                    start=(c == 0), stop=(c == CC - 1)),
                                    "t" if (c == CC - 1 and tt == NT - 1) else None)
                        b.wait("scalar", "t", gmm)
                        if tspg[sl] is not None:
                            b.wait("scalar", f"tsp{sl}")
                        for tt in range(NT):
                            sv = b.op("scalar", lambda e, tt=tt, ic=ic, sl=sl: e.activation(
                                silu[sl][:, TT * tt:TT * (tt + 1)], banks.t[tt][:],
                                ACT.Silu, scale=sgC[:, ic:ic + 1]), "s")
                        for i in range(4):
                            banks.release(i, "s", sv)
                        for i in range(4, 8):
                            banks.acquire(b, i)
                        if ici == 0:
                            b.wait("tensor", f"wu{half}")
                        for c in range(CC):
                            for tt in range(NT):
                                umm = b.op("tensor", lambda e, c=c, tt=tt, ici=ici, half=half: e.matmul(
                                    banks.t[tt + 4][:],
                                    wuw[half][c][:, 128 * ici:128 * (ici + 1)],
                                    xq_bf[c][:, TT * tt:TT * (tt + 1)],
                                    start=(c == 0), stop=(c == CC - 1)),
                                    "t" if (c == CC - 1 and tt == NT - 1) else None)
                        b.wait("vector", "t", umm)
                        b.wait("vector", "s", sv)
                        if tspg[sl] is not None:
                            b.wait("vector", f"tsp{sl}")
                        for tt in range(NT):
                            uv = b.op("vector", lambda e, tt=tt, ic=ic, sl=sl: e.scalar_tensor_tensor(
                                tb[sl][:, TT * tt:TT * (tt + 1)], banks.t[tt + 4][:],
                                suC[:, ic:ic + 1], silu[sl][:, TT * tt:TT * (tt + 1)],
                                ALU.mult, ALU.mult), "v")
                        for i in range(4, 8):
                            banks.release(i, "v", uv)
                        b.op("vector", lambda e, ic=ic, sl=sl: e.reduce_max(
                            amaxA[:, ic:ic + 1], tb[sl][:], AX,
                            apply_absolute_value=True), "v")
                        b.wait("sync", "v")
                        tspg[sl] = b.dma("sync", t_sp[128 * ic:128 * (ic + 1), :], tb[sl][:],
                                         sem=f"tsp{sl}")
                b.barrier()

        def emit_ffn_down(resid_src, store_out, tag, wait_marks=None):
            b.op("vector", lambda e: e.tensor_scalar(
                sdC[:], shN[:, 0:CC], scb[:], 0.5, ALU.mult, ALU.mult), "v")
            if wait_marks is not None:
                b.wait("gpsimd", "wq8d0", wait_marks["d"][0])
                b.wait("gpsimd", "wq8d1", wait_marks["d"][1])
            with contextlib.ExitStack() as pd:
                sbd = lambda n, s, d: pd.enter_context(nc.sbuf_tensor(n, s, d))
                qhw = [sbd(f"{tag}qh{ic}", [128, 1024], BF16) for ic in range(IC)]
                wdw = [sbd(f"{tag}dw{ic}", [128, 1024], BF16) for ic in range(IC)]
                tin = [sbd(f"{tag}ti{i}", [128, 1024], F32) for i in range(4)]
                q8h = [sbd(f"{tag}q8h{i}", [128, 1024], I8) for i in range(4)]
                for half in range(2):
                    to = 1024 * half
                    b.wait("gpsimd", "t")
                    qge = [None, None, None, None]
                    for ic in range(IC):
                        sl = ic % 4
                        if half == 0:
                            b.dma("gpsimd", wdw[ic][:], wdq[128 * ic:128 * (ic + 1), :],
                                  sem="wdw")
                        if qge[sl] is not None:
                            b.wait("sync", f"qc{sl}")
                        b.dma("sync", tin[sl][:], t_sp[128 * ic:128 * (ic + 1), to:to + 1024],
                              sem=f"ti{sl}")
                        b.wait("scalar", f"ti{sl}")
                        if qge[sl] is not None:
                            b.wait("scalar", f"qc{sl}")
                        b.op("scalar", lambda e, sl=sl: e.activation(
                            q8h[sl][:], tin[sl][:], ACT.Copy, scale=invb[:]), "s")
                        b.wait("gpsimd", "s")
                        qge[sl] = b.dma("gpsimd", qhw[ic][:], q8h[sl][:], sem=f"qc{sl}")
                    b.wait("tensor", "wdw")
                    for qs in ("qc0", "qc1", "qc2", "qc3"):
                        b.wait("tensor", qs)
                    stg = [None, None]
                    for hc in range(CC):
                        sl = hc % 2
                        for i in range(2):
                            banks.acquire(b, i)
                        for ic in range(IC):
                            for t2 in range(2):
                                dmm = b.op("tensor", lambda e, ic=ic, t2=t2, hc=hc: e.matmul(
                                    banks.t[t2][:],
                                    wdw[ic][:, 128 * hc:128 * (hc + 1)],
                                    qhw[ic][:, TT * t2:TT * (t2 + 1)],
                                    start=(ic == 0), stop=(ic == IC - 1)),
                                    "t" if (ic == IC - 1 and t2 == 1) else None)
                        src = resid_src(hc, half, sl)
                        b.wait("vector", "t", dmm)
                        for t2 in range(2):
                            vv = b.op("vector", lambda e, t2=t2, hc=hc, src=src, to=to:
                                      e.scalar_tensor_tensor(
                                          XR[hc][:, to + TT * t2:to + TT * (t2 + 1)],
                                          banks.t[t2][:], sdC[:, hc:hc + 1],
                                          src[:, TT * t2:TT * (t2 + 1)],
                                          ALU.mult, ALU.add), "v")
                        for i in range(2):
                            banks.release(i, "v", vv)
                        if store_out:
                            b.wait("sync", "v", vv)
                            stg[sl] = b.dma("sync", oxT[128 * hc:128 * (hc + 1), to:to + 1024],
                                            XR[hc][:, to:to + 1024])
                b.barrier()

        with contextlib.ExitStack() as pA:
            sbA = lambda n, s, d: pA.enter_context(nc.sbuf_tensor(n, s, d))
            xq_bf = [sbA(f"xqb{c}", [128, TOK], BF16) for c in range(CC)]

            with contextlib.ExitStack() as pXS:
                sbX = lambda n, s, d: pXS.enter_context(nc.sbuf_tensor(n, s, d))
                ln1P = [sbX(f"ln1buf{c}", [128, TOK], F32) for c in range(CC)]
                natA.extend(sbX(f"natA{i}", [128, I], F32) for i in range(2))
                trnA.extend(sbX(f"trnA{i}", [128, I], F32) for i in range(2))
                for c in range(CC):
                    b.dma("sync", ln1P[c][:], xT[128 * c:128 * (c + 1), :], sem=f"xl{c}")

                def p1_chunk(c, sl, engs):
                    for e_ in engs:
                        b.wait(e_, f"xl{c}")
                    return ln1P[c][:]

                emit_weight_prep_a()
                emit_ln_quant(p1_chunk, xq_bf, 0, "ln1", mid=emit_weight_prep_b,
                              ln_bufs=ln1P)
            dsn = dump_tensor("sgN", [128, IC])
            if dsn is not None:
                b.wait("sync", "v")
                b.dma("sync", dsn, sgN[:])
                dsu = dump_tensor("suN", [128, IC])
                b.dma("sync", dsu, suN[:])
                b.barrier()
            dwq = dump_tensor("wgq", [C, I], I8)
            if dwq is not None:
                for s_ in ("wq8g0", "wq8g1", "wq8u0", "wq8u1", "wq8d0", "wq8d1"):
                    b.wait("sync", s_)
                b.dma("sync", dwq, wgq)
                duq = dump_tensor("wuq", [C, I], I8)
                b.dma("sync", duq, wuq)
                b.barrier()
            dsc = dump_tensor("sc1", [1, 8])
            if dsc is not None:
                b.wait("sync", "v")
                b.wait("sync", "dg")
                b.dma("sync", dsc, sc1[:])
                b.barrier()
            dinv = dump_tensor("invb", [128, 2])
            if dinv is not None:
                b.wait("sync", "v")
                b.dma("sync", dinv[:, 0:1], invb[:], allow_slow_non_contiguous=True)
                b.dma("sync", dinv[:, 1:2], scb[:], allow_slow_non_contiguous=True)
                b.barrier()
            dxq = dump_tensor("xq", [C, TOK], BF16)
            if dxq is not None:
                for c in range(CC):
                    b.dma("sync", dxq[128 * c:128 * (c + 1), :], xq_bf[c][:])
                b.barrier()
            if _STOP_AFTER != "p1":
                emit_ffn_gate_up(xq_bf, "f1", wait_marks=wp_marks)
                dsg = dump_tensor("sgC", [128, 2 * IC])
                if dsg is not None:
                    b.wait("sync", "v")
                    b.dma("sync", dsg[:, 0:IC], sgC[:])
                    b.dma("sync", dsg[:, IC:2 * IC], suC[:])
                    b.barrier()
                dt1 = dump_tensor("t1", [I, TOK])
                if dt1 is not None:
                    b.wait("sync", "tsp0")
                    b.wait("sync", "tsp1")
                    b.dma("sync", dt1, t_sp)
                    b.barrier()
                emit_global_scale(IC, 1, cm_i[1], cm_o[1])

        if _STOP_AFTER in ("p1", "p2"):
            b.barrier()
            return _emit(nc, es, b)

        XR.extend(sbuf(f"xr{c}", [128, TOK], F32) for c in range(CC))
        with contextlib.ExitStack() as p3s:
            xs3 = [p3s.enter_context(nc.sbuf_tensor(f"xs3{i}", [128, 1024], F32))
                   for i in range(2)]

            def p3_src(hc, half, sl):
                b.wait("sync", "v")
                b.dma("sync", xs3[sl][:],
                      xT[128 * hc:128 * (hc + 1), 1024 * half:1024 * (half + 1)])
                b.wait("vector", "ds")
                return xs3[sl][:]

            emit_ffn_down(p3_src, False, "f1", wait_marks=wp_marks)
        dx1 = dump_tensor("x1", [C, TOK])
        if dx1 is not None:
            for c in range(CC):
                b.dma("sync", dx1[128 * c:128 * (c + 1), :], XR[c][:])
            b.barrier()
        if _STOP_AFTER == "p3":
            b.barrier()
            return _emit(nc, es, b)

        m2s_dbg = []
        with contextlib.ExitStack() as pT:
            sbT = lambda n, s, d: pT.enter_context(nc.sbuf_tensor(n, s, d))
            h_bf = [sbT(f"hbf{c}", [128, TOK], BF16) for c in range(CC)]
            V_bf = [sbT(f"vbf{c}", [128, TOK], BF16) for c in range(CC)]
            wsl = [sbT(f"wsl{i}", [128, C], BF16) for i in range(2)]

            with contextlib.ExitStack() as sA:
                sba = lambda n, s, d: sA.enter_context(nc.sbuf_tensor(n, s, d))
                xb = [sba(f"axb{i}", [128, TOK], BF16) for i in range(2)]
                sqb = [sba(f"asq{i}", [128, TOK], BF16) for i in range(2)]
                m2 = sba("m2", [128, TOK], F32)
                r2 = sba("r2", [128, TOK], F32)
                m2s_dbg.extend([m2, r2])
                tmp = sba("atmp", [128, TT], F32)
                xaf = sba("xaf", [128, TOK], F32)
                xaw = [sba(f"xaw{i}", [128, TOK], BF16) for i in range(2)]

                def srcA(c, sl):
                    b.op("scalar", lambda e, sl=sl, c=c: e.activation(
                        xb[sl][:], XR[c][:], ACT.Copy), None)

                emit_stats(srcA, m2, r2, tmp, xb, sqb)
                xag = [None, None]
                for c in range(CC):
                    sl = c % 2
                    b.op("vector", lambda e, c=c: e.tensor_sub(xaf[:], XR[c][:], m2[:]), None)
                    if xag[sl] is not None:
                        b.wait("vector", f"xa{sl}")
                    b.op("vector", lambda e, sl=sl: e.tensor_mul(xaw[sl][:], xaf[:], r2[:]), "v")
                    b.wait("sync", "v")
                    xag[sl] = b.dma("sync", xa_sp[128 * c:128 * (c + 1), :], xaw[sl][:],
                                    sem=f"xa{sl}")
                b.barrier()

            dm2 = dump_tensor("m2r2", [256, TOK])
            if dm2 is not None:
                b.dma("sync", dm2[0:128, :], m2s_dbg[0][:])
                b.dma("sync", dm2[128:256, :], m2s_dbg[1][:])
                b.barrier()
            with contextlib.ExitStack() as sB:
                sbb = lambda n, s, d: sB.enter_context(nc.sbuf_tensor(n, s, d))
                xb = [sbb(f"bxb{i}", [128, TOK], BF16) for i in range(2)]
                sqb = [sbb(f"bsq{i}", [128, TOK], BF16) for i in range(2)]
                m3 = sbb("m3", [128, TOK], F32)
                r3 = sbb("r3", [128, TOK], F32)
                tmp = sbb("btmp", [128, TT], F32)
                uf = sbb("uf", [128, TOK], F32)

                def srcB(c, sl):
                    b.wait("sync", "s")
                    b.dma("sync", xb[sl][:], xa_sp[128 * c:128 * (c + 1), :])
                    b.wait("scalar", "ds")

                ps_s = [banks.acquire(b, i) for i in range(4)]
                ps_q = [banks.acquire(b, i + 4) for i in range(4)]
                g = [None, None]
                for c in range(CC):
                    sl = c % 2
                    if g[sl] is not None:
                        b.wait("sync", "t", g[sl])
                    srcB(c, sl)
                    b.op("scalar", lambda e, sl=sl: e.activation(sqb[sl][:], xb[sl][:], ACT.Square), "s")
                    b.wait("tensor", "s")
                    for tt in range(NT):
                        b.op("tensor", lambda e, sl=sl, tt=tt, c=c: e.matmul(
                            ps_s[tt][:], onesb[:], xb[sl][:, TT * tt:TT * (tt + 1)],
                            start=(c == 0), stop=(c == CC - 1)), None)
                    for tt in range(NT):
                        g[sl] = b.op("tensor", lambda e, sl=sl, tt=tt, c=c: e.matmul(
                            ps_q[tt][:], onesb[:], sqb[sl][:, TT * tt:TT * (tt + 1)],
                            start=(c == 0), stop=(c == CC - 1)), "t")
                b.wait("scalar", "t")
                for tt in range(NT):
                    sl_ = slice(TT * tt, TT * (tt + 1))
                    b.op("scalar", lambda e, tt=tt, sl_=sl_: e.activation(
                        m3[:, sl_], ps_s[tt][:], ACT.Copy, scale=1.0 / C), "s")
                    b.op("scalar", lambda e, tt=tt, sl_=sl_: e.activation(
                        r3[:, sl_], ps_q[tt][:], ACT.Copy, scale=1.0 / C), "s")
                b.wait("vector", "s")
                for tt in range(NT):
                    sl_ = slice(TT * tt, TT * (tt + 1))
                    b.op("vector", lambda e, sl_=sl_: e.tensor_mul(tmp[:], m3[:, sl_], m3[:, sl_]), None)
                    b.op("vector", lambda e, sl_=sl_: e.tensor_sub(r3[:, sl_], r3[:, sl_], tmp[:]), "v")
                b.wait("scalar", "v")
                b.op("scalar", lambda e: e.activation(r3[:], r3[:], ACT.Ln, bias=epsb[:]), None)
                b.op("scalar", lambda e: e.activation(r3[:], r3[:], ACT.Exp, scale=-0.5), "s")
                for i in range(8):
                    banks.release(i, "s", b.cur("s"))
                b.wait("vector", "s")
                for c in range(CC):
                    sl = c % 2
                    b.wait("sync", "v")
                    b.wait("sync", "s")
                    b.dma("sync", xb[sl][:], xa_sp[128 * c:128 * (c + 1), :])
                    b.wait("vector", "ds")
                    b.op("vector", lambda e, sl=sl: e.tensor_sub(uf[:], xb[sl][:], m3[:]), None)
                    b.op("vector", lambda e: e.tensor_mul(uf[:], uf[:], r3[:]), "v")
                    b.wait("scalar", "v")
                    b.op("scalar", lambda e, c=c: e.activation(
                        h_bf[c][:], uf[:], ACT.Identity,
                        scale=gT[:, c:c + 1], bias=bT[:, c:c + 1]), "s")
                b.barrier()
            dxa = dump_tensor("xa", [C, TOK], BF16)
            if dxa is not None:
                b.wait("sync", "xa0")
                b.wait("sync", "xa1")
                b.dma("sync", dxa, xa_sp)
                b.barrier()
            dh = dump_tensor("h", [C, TOK], BF16)
            if dh is not None:
                for c in range(CC):
                    b.dma("sync", dh[128 * c:128 * (c + 1), :], h_bf[c][:])
                b.barrier()

            with contextlib.ExitStack() as sC:
                sbc = lambda n, s, d: sC.enter_context(nc.sbuf_tensor(n, s, d))
                K_bf = [sbc(f"kbf{c}", [128, TOK], BF16) for c in range(CC)]
                KA = sbc("KA", [128, TOK], BF16)
                onesf = sbc("onesf", [128, 2], F32)
                bsl = [sbc(f"bsl{i}", [128, 128], BF16) for i in range(2)]
                errt = sbc("errt", [128, TOK], F32)
                grdt = sbc("grdt", [128, TOK], F32)
                shft = sbc("shft", [128, TOK], F32)
                b.op("vector", lambda e: e.memset(onesf[:], 1.0), "v")

                slab = Slab(wsl, "ws")
                flat = [(wk_s[dc], K_bf, dc) for dc in range(CC)] + \
                       [(wv_s[dc], V_bf, dc) for dc in range(CC)] + [(a_s, None, 0)]
                slab.load(b, 0, flat[0][0])
                for it, (wap, dst, dc) in enumerate(flat):
                    if it + 1 < len(flat):
                        slab.load(b, it + 1, flat[it + 1][0])
                    w = slab.use(b, it)
                    rhs = h_bf if dst is not None else K_bf
                    if dst is None:
                        b.wait("tensor", "s")
                    for i in range(4):
                        banks.acquire(b, i)
                    for c in range(CC):
                        for tt in range(NT):
                            mm = b.op("tensor", lambda e, c=c, tt=tt, w=w, rhs=rhs: e.matmul(
                                banks.t[tt][:], w[:, 128 * c:128 * (c + 1)],
                                rhs[c][:, TT * tt:TT * (tt + 1)],
                                start=(c == 0), stop=(c == CC - 1)),
                                "t" if (c == CC - 1 and tt == NT - 1) else None)
                    slab.done(it, mm)
                    b.wait("scalar", "t", mm)
                    out_t = (dst[dc] if dst is not None else KA)
                    for tt in range(NT):
                        sv = b.op("scalar", lambda e, tt=tt, out_t=out_t: e.activation(
                            out_t[:, TT * tt:TT * (tt + 1)], banks.t[tt][:], ACT.Copy), "s")
                    for i in range(4):
                        banks.release(i, "s", sv)

                bslab = Slab(bsl, "bs")
                slab.load(b, 17, w0_s[0])
                bslab.load(b, 0, b_s[0])
                eg = None
                sg_ = None
                for dc in range(CC):
                    if dc + 1 < CC:
                        slab.load(b, 17 + dc + 1, w0_s[dc + 1])
                        bslab.load(b, dc + 1, b_s[dc + 1])
                    w = slab.use(b, 17 + dc)
                    bw = bslab.use(b, dc)
                    for i in range(4):
                        banks.acquire(b, i)
                    for c in range(CC):
                        for tt in range(NT):
                            b.op("tensor", lambda e, c=c, tt=tt, w=w: e.matmul(
                                banks.t[tt][:], w[:, 128 * c:128 * (c + 1)],
                                K_bf[c][:, TT * tt:TT * (tt + 1)],
                                start=(c == 0), stop=False), None)
                    for tt in range(NT):
                        mm = b.op("tensor", lambda e, tt=tt, bw=bw: e.matmul(
                            banks.t[tt][:], bw[:], KA[:, TT * tt:TT * (tt + 1)],
                            start=False, stop=True),
                            "t" if tt == NT - 1 else None)
                    slab.done(17 + dc, mm)
                    bslab.done(dc, mm)
                    b.wait("vector", "t", mm)
                    if eg is not None:
                        b.wait("vector", "es")
                    for tt in range(NT):
                        vv = b.op("vector", lambda e, tt=tt, dc=dc: e.tensor_sub(
                            errt[:, TT * tt:TT * (tt + 1)], banks.t[tt][:],
                            V_bf[dc][:, TT * tt:TT * (tt + 1)]), "v")
                    for i in range(4):
                        banks.release(i, "v", vv)
                    b.op("vector", lambda e, dc=dc: e.tensor_mul(
                        grdt[:], K_bf[dc][:], errt[:]), None)
                    b.op("vector", lambda e: e.memset(shft[:, 0:1], 0.0), None)
                    b.op("vector", lambda e: e.tensor_tensor_scan(
                        shft[:, 1:TOK], onesf[:, 0:1].broadcast_to([128, TOK - 1]),
                        grdt[:, 0:TOK - 1], 0.0, ALU.mult, ALU.add), "v")
                    b.wait("vector", "v")
                    b.op("vector", lambda e, dc=dc: e.tensor_add(
                        carry[:, dc:dc + 1], shft[:, TOK - 1:TOK], grdt[:, TOK - 1:TOK]), "v")
                    b.wait("sync", "v")
                    eg = b.dma("sync", err_sp[128 * dc:128 * (dc + 1), :], errt[:], sem="es")
                    sg_ = b.dma("sync", sh_sp[128 * dc:128 * (dc + 1), :], shft[:], sem="es")
                b.barrier()

            b.op("vector", lambda e: e.tensor_scalar(carry[:], carry[:], ieb[:], None, ALU.mult), "v")
            b.wait("gpsimd", "v")
            b.dma("gpsimd", cy_i, carry[:])
            b.wait("gpsimd", "dg")
            b.op("gpsimd", lambda e: e.collective_compute(
                "AllReduce", ALU.add, replica_groups=[[0, 1], [2, 3], [4, 5], [6, 7]],
                ins=[cy_i], outs=[cy_o]), "cc")
            b.wait("gpsimd", "cc")
            b.dma("gpsimd", carry[:], cy_o)
            b.wait("vector", "dg")
            b.op("vector", lambda e: e.tensor_scalar(carry[:], carry[:], iob[:], None, ALU.mult), "v")
            b.barrier()

            with contextlib.ExitStack() as sF:
                sbff = lambda n, s, d: sF.enter_context(nc.sbuf_tensor(n, s, d))
                psb = sbff("psb", [128, TOK], F32)
                qlr = sbff("qlr", [128, TOK], F32)
                c1t = sbff("c1t", [128, TOK], F32)
                errF = [sbff(f"errF{i}", [128, TOK], F32) for i in range(2)]
                shF = [sbff(f"shF{i}", [128, TOK], F32) for i in range(2)]
                opb = [sbff(f"opb{i}", [128, TOK], BF16) for i in range(2)]
                b.dma("gpsimd", psb[:], posr.partition_broadcast(128))
                og = [None, None]
                cons = [None, None]
                sld = [None, None]
                slab = Slab(wsl, "ws")
                slab.load(b, 0, wq_s[0])

                def ld_es(dc):
                    sl = dc % 2
                    if cons[sl] is not None:
                        b.wait("sync", "v", cons[sl])
                    b.dma("sync", errF[sl][:], err_sp[128 * dc:128 * (dc + 1), :],
                          sem=f"ef{sl}")
                    sld[sl] = b.dma("sync", shF[sl][:], sh_sp[128 * dc:128 * (dc + 1), :],
                                    sem=f"ef{sl}")

                ld_es(0)
                for dc in range(CC):
                    sl = dc % 2
                    if dc + 1 < CC:
                        slab.load(b, dc + 1, wq_s[dc + 1])
                    w = slab.use(b, dc)
                    myld = sld[sl]
                    for i in range(4):
                        banks.acquire(b, i)
                    for c in range(CC):
                        for tt in range(NT):
                            mm = b.op("tensor", lambda e, c=c, tt=tt, w=w: e.matmul(
                                banks.t[tt][:], w[:, 128 * c:128 * (c + 1)],
                                h_bf[c][:, TT * tt:TT * (tt + 1)],
                                start=(c == 0), stop=(c == CC - 1)),
                                "t" if (c == CC - 1 and tt == NT - 1) else None)
                    slab.done(dc, mm)
                    b.wait("vector", "t", mm)
                    b.wait("vector", f"ef{sl}")
                    b.wait("vector", "dg")
                    for tt in range(NT):
                        vv = b.op("vector", lambda e, tt=tt, dc=dc: e.scalar_tensor_tensor(
                            qlr[:, TT * tt:TT * (tt + 1)], banks.t[tt][:],
                            lrT[:, dc:dc + 1], psb[:, TT * tt:TT * (tt + 1)],
                            ALU.mult, ALU.mult), "v")
                    for i in range(4):
                        banks.release(i, "v", vv)
                    b.op("vector", lambda e, dc=dc, sl=sl: e.scalar_tensor_tensor(
                        c1t[:], shF[sl][:], carry[:, dc:dc + 1], qlr[:],
                        ALU.add, ALU.mult), None)
                    cv = b.op("vector", lambda e, sl=sl: e.tensor_sub(qlr[:], errF[sl][:], c1t[:]), "v")
                    cons[sl] = cv
                    if dc + 1 < CC:
                        ld_es(dc + 1)
                    if og[sl] is not None:
                        b.wait("vector", f"os{sl}")
                    b.op("vector", lambda e, dc=dc, sl=sl: e.tensor_add(
                        opb[sl][:], qlr[:], V_bf[dc][:]), "v")
                    b.wait("sync", "v")
                    og[sl] = b.dma("sync", op_sp[128 * dc:128 * (dc + 1), :], opb[sl][:],
                                   sem=f"os{sl}")
                b.barrier()

            with contextlib.ExitStack() as sG:
                sbg = lambda n, s, d: sG.enter_context(nc.sbuf_tensor(n, s, d))
                xaG = [sbg(f"xag{c}", [128, TOK], BF16) for c in range(CC)]
                gtb = [sbg(f"gtb{i}", [128, TOK], BF16) for i in range(2)]
                opG = [sbg(f"opG{i}", [128, TOK], BF16) for i in range(2)]
                x1t = sbg("x1t", [128, TOK], F32)
                og_bf = h_bf
                for c in range(CC):
                    b.dma("sync", xaG[c][:], xa_sp[128 * c:128 * (c + 1), :])
                b.wait("tensor", "ds")
                slab = Slab(wsl, "ws")
                slab.load(b, 0, wg_s[0])
                for dc in range(CC):
                    sl = dc % 2
                    if dc + 1 < CC:
                        slab.load(b, dc + 1, wg_s[dc + 1])
                    w = slab.use(b, dc)
                    b.wait("sync", "v")
                    opg = b.dma("sync", opG[sl][:], op_sp[128 * dc:128 * (dc + 1), :],
                                sem=f"op{sl}")
                    for i in range(4):
                        banks.acquire(b, i)
                    for c in range(CC):
                        for tt in range(NT):
                            mm = b.op("tensor", lambda e, c=c, tt=tt, w=w: e.matmul(
                                banks.t[tt][:], w[:, 128 * c:128 * (c + 1)],
                                xaG[c][:, TT * tt:TT * (tt + 1)],
                                start=(c == 0), stop=(c == CC - 1)),
                                "t" if (c == CC - 1 and tt == NT - 1) else None)
                    slab.done(dc, mm)
                    b.wait("scalar", "t", mm)
                    b.wait("scalar", "v")
                    for tt in range(NT):
                        sv = b.op("scalar", lambda e, tt=tt, sl=sl: e.activation(
                            gtb[sl][:, TT * tt:TT * (tt + 1)], banks.t[tt][:], ACT.Sigmoid), "s")
                    for i in range(4):
                        banks.release(i, "s", sv)
                    b.wait("vector", "s", sv)
                    b.wait("vector", f"op{sl}")
                    b.op("vector", lambda e, dc=dc, sl=sl: e.tensor_mul(
                        og_bf[dc][:], opG[sl][:], gtb[sl][:]), "v")
                b.barrier()

                slab2 = Slab(wsl, "ws")
                slab2.load(b, 0, wo_s[0])
                for hc in range(CC):
                    sl = hc % 2
                    if hc + 1 < CC:
                        slab2.load(b, hc + 1, wo_s[hc + 1])
                    w = slab2.use(b, hc)
                    for i in range(4):
                        banks.acquire(b, i)
                    for dc in range(CC):
                        for tt in range(NT):
                            mm = b.op("tensor", lambda e, dc=dc, tt=tt, w=w: e.matmul(
                                banks.t[tt][:], w[:, 128 * dc:128 * (dc + 1)],
                                og_bf[dc][:, TT * tt:TT * (tt + 1)],
                                start=(dc == 0), stop=(dc == CC - 1)),
                                "t" if (dc == CC - 1 and tt == NT - 1) else None)
                    slab2.done(hc, mm)
                    b.op("vector", lambda e, hc=hc: e.tensor_add(
                        x1t[:], XR[hc][:], xaG[hc][:]), None)
                    b.wait("vector", "t", mm)
                    for tt in range(NT):
                        vv = b.op("vector", lambda e, tt=tt, hc=hc: e.scalar_tensor_tensor(
                            XR[hc][:, TT * tt:TT * (tt + 1)], banks.t[tt][:],
                            lsT[:, hc:hc + 1], x1t[:, TT * tt:TT * (tt + 1)],
                            ALU.mult, ALU.add), "v")
                    for i in range(4):
                        banks.release(i, "v", vv)
                b.barrier()

        dx2 = dump_tensor("x2", [C, TOK])
        if dx2 is not None:
            for c in range(CC):
                b.dma("sync", dx2[128 * c:128 * (c + 1), :], XR[c][:])
            b.barrier()
        if _STOP_AFTER == "p4":
            b.barrier()
            return _emit(nc, es, b)

        with contextlib.ExitStack() as pB:
            sbB = lambda n, s, d: pB.enter_context(nc.sbuf_tensor(n, s, d))
            xq_bf = [sbB(f"xqc{c}", [128, TOK], BF16) for c in range(CC)]

            def p5_chunk(c, sl, engs):
                return XR[c][:]

            emit_ln_quant(p5_chunk, xq_bf, 2, "ln3")
            emit_ffn_gate_up(xq_bf, "f2", dbufw=False)
            emit_global_scale(IC, 1, cm_i[3], cm_o[3])

            def p5_src(hc, half, sl):
                return XR[hc][:, 1024 * half:1024 * (half + 1)]

            emit_ffn_down(p5_src, True, "f2")

        b.barrier()
        return _emit(nc, es, b)


def _emit(nc, es, b):
    block = es.enter_context(nc.Block())

    @block.sync
    def _(e):
        for fn in b.q["sync"]:
            fn(e)

    @block.gpsimd
    def _(e):
        for fn in b.q["gpsimd"]:
            fn(e)

    @block.vector
    def _(e):
        for fn in b.q["vector"]:
            fn(e)

    @block.scalar
    def _(e):
        for fn in b.q["scalar"]:
            fn(e)

    @block.tensor
    def _(e):
        for fn in b.q["tensor"]:
            fn(e)

    return nc


def kernel(**inputs):
    f4 = np.float32
    x = np.asarray(inputs["L0_attn_x"], f4).reshape(NCORE * TOK, C)
    Wq = np.asarray(inputs["L0_attn_Wq"], f4)
    Wk = np.asarray(inputs["L0_attn_Wk"], f4)
    Wv = np.asarray(inputs["L0_attn_Wv"], f4)
    Wo = np.asarray(inputs["L0_attn_Wo"], f4)
    Wg = np.asarray(inputs["L0_attn_Wg"], f4)
    W0 = np.asarray(inputs["L0_attn_W0"], f4)
    A_init = np.asarray(inputs["L0_attn_A_init"], f4)
    B_init = np.asarray(inputs["L0_attn_B_init"], f4)

    wq_s = _slab(np.ascontiguousarray(Wq.T), CC)
    wk_s = _slab(np.ascontiguousarray(Wk.T), CC)
    wv_s = _slab(np.ascontiguousarray(Wv.T), CC)
    wg_s = _slab(np.ascontiguousarray(Wg.T), CC)
    wo_s = _slab(np.ascontiguousarray(Wo.T), CC)
    w0_s = _slab(np.ascontiguousarray(W0), CC)
    A_pad = np.zeros((C, 128), f4)
    A_pad[:, :8] = A_init
    a_s = _slab(A_pad, 1).reshape(128, C)
    B_pad = np.zeros((128, C), f4)
    B_pad[:8, :] = B_init
    b_s = np.ascontiguousarray(B_pad.reshape(128, CC, 128).transpose(1, 0, 2))

    pos = np.arange(1, 4097, dtype=f4)
    pos_scale = (f4(1.0) / (f4(1.0) + f4(0.1) * np.log(pos))).astype(f4)

    common = dict(
        wq_s=wq_s, wk_s=wk_s, wv_s=wv_s, wg_s=wg_s, wo_s=wo_s, w0_s=w0_s,
        a_s=a_s, b_s=b_s,
        wgT=np.ascontiguousarray(np.asarray(inputs["L0_ffn_w_gate"], f4).T),
        wuT=np.ascontiguousarray(np.asarray(inputs["L0_ffn_w_up"], f4).T),
        wdT=np.ascontiguousarray(np.asarray(inputs["L0_ffn_w_down"], f4).T),
        wgN=np.ascontiguousarray(np.asarray(inputs["L0_ffn_w_gate"], f4)),
        wuN=np.ascontiguousarray(np.asarray(inputs["L0_ffn_w_up"], f4)),
        wdN=np.ascontiguousarray(np.asarray(inputs["L0_ffn_w_down"], f4)),
        logr=_vec8(inputs["L0_attn_log_lr"]), lng=_vec8(inputs["L0_attn_ln_g"]),
        lnb=_vec8(inputs["L0_attn_ln_b"]), lsg=_vec8(inputs["L0_attn_ls_g"]),
        clipx=np.asarray(inputs["L0_ffn_clip_x"], f4).reshape(1, 1),
        clipd=np.asarray(inputs["L0_ffn_clip_down"], f4).reshape(1, 1),
    )
    in_maps = []
    for k in range(NCORE):
        m = dict(common)
        m["xT"] = np.ascontiguousarray(x[TOK * k:TOK * (k + 1), :].T)
        seq_half = k % 2
        m["posr"] = np.ascontiguousarray(
            pos_scale[TOK * seq_half:TOK * (seq_half + 1)].reshape(1, TOK))
        m["iseven"] = np.asarray([[1.0 if k % 2 == 0 else 0.0]], f4)
        m["isodd"] = np.asarray([[1.0 if k % 2 == 1 else 0.0]], f4)
        in_maps.append(m)

    nc = build_program()
    res = run_bass_kernel_spmd(nc, in_maps, core_ids=list(range(NCORE)))
    globals()["_LAST_RESULTS"] = res
    out = np.empty((NCORE * TOK, C), f4)
    for k in range(NCORE):
        out[TOK * k:TOK * (k + 1), :] = res.results[k]["oxT"].T
    return out.reshape(4, 4096, C)


# version 09
# speedup vs baseline: 1.0254x; 1.0070x over previous
import contextlib
import numpy as np
import ml_dtypes
import concourse.bass as bass
import concourse.mybir as mybir
from concourse.bass_utils import run_bass_kernel_spmd

F32 = mybir.dt.float32
BF16 = mybir.dt.bfloat16
I8 = mybir.dt.int8
ALU = mybir.AluOpType
ACT = mybir.ActivationFunctionType
AX = mybir.AxisListType.X

NCORE = 8
TOK = 2048
C = 1024
CC = 8
I = 2816
IC = 22
NT = 4
TT = 512
EPS = 1e-5

ENGS = ("sync", "gpsimd", "vector", "scalar", "tensor")

_STOP_AFTER = None
_DUMPS = []


class B:

    def __init__(self, nc, es):
        self.nc = nc
        self.es = es
        self.q = {e: [] for e in ENGS}
        self.sems = {}

    def sem(self, name):
        if name not in self.sems:
            h = self.es.enter_context(self.nc.semaphore(name))
            self.sems[name] = [h, 0]
        return self.sems[name]

    def op(self, eng, fn, sem=None, k=1):
        if sem is None:
            self.q[eng].append(fn)
            return None
        rec = self.sem(sem)
        rec[1] += k
        h = rec[0]
        self.q[eng].append(lambda e, fn=fn, h=h, k=k: fn(e).then_inc(h, k))
        return rec[1]

    def dma(self, eng, out, in_, sem=None, **kw):
        if sem is None:
            sem = {"sync": "ds", "gpsimd": "dg", "scalar": "dsc"}[eng]
        return self.op(eng, lambda e: e.dma_start(out=out, in_=in_, **kw), sem, 16)

    def wait(self, eng, sem, val=None):
        rec = self.sem(sem)
        v = rec[1] if val is None else val
        if v is None or v <= 0:
            return
        h = rec[0]
        self.q[eng].append(lambda e, h=h, v=v: e.wait_ge(h, v))

    def cur(self, sem):
        return self.sem(sem)[1]

    def barrier(self, engs=ENGS):
        snap = [(name, rec[1]) for name, rec in self.sems.items() if rec[1] > 0]
        for e in engs:
            for name, v in snap:
                self.wait(e, name, v)


class Banks:
    def __init__(self, nc, es):
        self.t = [es.enter_context(nc.psum_tensor(f"pb{i}", [128, TT], F32))
                  for i in range(8)]
        self.guard = [None] * 8

    def acquire(self, b, i):
        if self.guard[i] is not None:
            b.wait("tensor", *self.guard[i])
            self.guard[i] = None
        return self.t[i]

    def release(self, i, sem, val):
        self.guard[i] = (sem, val)


class Slab:

    def __init__(self, tiles, name):
        self.tiles = tiles
        self.name = name
        self.guard = [None] * len(tiles)

    def load(self, b, i, src):
        sl = i % len(self.tiles)
        if self.guard[sl] is not None:
            b.wait("gpsimd", self.guard[sl][0], self.guard[sl][1])
            self.guard[sl] = None
        b.dma("gpsimd", self.tiles[sl][:], src, sem=f"{self.name}{sl}")

    def use(self, b, i):
        sl = i % len(self.tiles)
        b.wait("tensor", f"{self.name}{sl}")
        return self.tiles[sl]

    def done(self, i, mmval, sem="t"):
        self.guard[i % len(self.tiles)] = (sem, mmval)


def _slab(wT, n_out_chunks):
    K = wT.shape[0]
    kc = K // 128
    a = wT.reshape(kc, 128, n_out_chunks, 128)
    return np.ascontiguousarray(a.transpose(2, 1, 0, 3).reshape(n_out_chunks, 128, K))


def _vec8(v):
    return np.ascontiguousarray(np.asarray(v, np.float32).reshape(CC, 128).T)


def build_program():
    nc = bass.Bass("TRN2", target_bir_lowering=False)
    ap = lambda t: t.ap()

    xT = ap(nc.dram_tensor("xT", [C, TOK], F32, kind="ExternalInput"))
    wq_s = ap(nc.dram_tensor("wq_s", [CC, 128, C], F32, kind="ExternalInput"))
    wk_s = ap(nc.dram_tensor("wk_s", [CC, 128, C], F32, kind="ExternalInput"))
    wv_s = ap(nc.dram_tensor("wv_s", [CC, 128, C], F32, kind="ExternalInput"))
    wg_s = ap(nc.dram_tensor("wg_s", [CC, 128, C], F32, kind="ExternalInput"))
    wo_s = ap(nc.dram_tensor("wo_s", [CC, 128, C], F32, kind="ExternalInput"))
    w0_s = ap(nc.dram_tensor("w0_s", [CC, 128, C], F32, kind="ExternalInput"))
    a_s = ap(nc.dram_tensor("a_s", [128, C], F32, kind="ExternalInput"))
    b_s = ap(nc.dram_tensor("b_s", [CC, 128, 128], F32, kind="ExternalInput"))
    wgT = ap(nc.dram_tensor("wgT", [C, I], F32, kind="ExternalInput"))
    wuT = ap(nc.dram_tensor("wuT", [C, I], F32, kind="ExternalInput"))
    wdT = ap(nc.dram_tensor("wdT", [I, C], F32, kind="ExternalInput"))
    wgN = ap(nc.dram_tensor("wgN", [I, C], F32, kind="ExternalInput"))
    wuN = ap(nc.dram_tensor("wuN", [I, C], F32, kind="ExternalInput"))
    wdN = ap(nc.dram_tensor("wdN", [C, I], F32, kind="ExternalInput"))
    logr = ap(nc.dram_tensor("logr", [128, CC], F32, kind="ExternalInput"))
    lng = ap(nc.dram_tensor("lng", [128, CC], F32, kind="ExternalInput"))
    lnb = ap(nc.dram_tensor("lnb", [128, CC], F32, kind="ExternalInput"))
    lsg = ap(nc.dram_tensor("lsg", [128, CC], F32, kind="ExternalInput"))
    clipx = ap(nc.dram_tensor("clipx", [1, 1], F32, kind="ExternalInput"))
    clipd = ap(nc.dram_tensor("clipd", [1, 1], F32, kind="ExternalInput"))
    posr = ap(nc.dram_tensor("posr", [1, TOK], F32, kind="ExternalInput"))
    iseven = ap(nc.dram_tensor("iseven", [1, 1], F32, kind="ExternalInput"))
    isodd = ap(nc.dram_tensor("isodd", [1, 1], F32, kind="ExternalInput"))

    oxT = ap(nc.dram_tensor("oxT", [C, TOK], F32, kind="ExternalOutput"))

    wgq = ap(nc.dram_tensor("wgq", [C, I], I8, kind="Internal"))
    wuq = ap(nc.dram_tensor("wuq", [C, I], I8, kind="Internal"))
    wdq = ap(nc.dram_tensor("wdq", [I, C], I8, kind="Internal"))
    t_sp = ap(nc.dram_tensor("t_sp", [I, TOK], F32, kind="Internal"))
    xa_sp = ap(nc.dram_tensor("xa_sp", [C, TOK], BF16, kind="Internal"))
    err_sp = ap(nc.dram_tensor("err_sp", [C, TOK], F32, kind="Internal"))
    sh_sp = ap(nc.dram_tensor("sh_sp", [C, TOK], F32, kind="Internal"))
    op_sp = ap(nc.dram_tensor("op_sp", [C, TOK], BF16, kind="Internal"))
    srow = ap(nc.dram_tensor("srow", [1, max(I, C)], F32, kind="Internal"))
    stage = ap(nc.dram_tensor("stage", [1, 16], F32, kind="Internal"))
    cw_i = ap(nc.dram_tensor("cw_i", [1, 8], F32, kind="Internal"))
    cw_o = ap(nc.dram_tensor("cw_o", [1, 8], F32, kind="Internal"))
    cm_i = [ap(nc.dram_tensor(f"cm_i{k}", [1, 8], F32, kind="Internal")) for k in range(4)]
    cm_o = [ap(nc.dram_tensor(f"cm_o{k}", [1, 8], F32, kind="Internal")) for k in range(4)]
    cy_i = ap(nc.dram_tensor("cy_i", [128, 8], F32, kind="Internal"))
    cy_o = ap(nc.dram_tensor("cy_o", [128, 8], F32, kind="Internal"))

    dumps = {}

    def dump_tensor(name, shape, dtype=F32):
        if name in _DUMPS:
            dumps[name] = ap(nc.dram_tensor("dbg_" + name, shape, dtype,
                                            kind="ExternalOutput"))
            return dumps[name]
        return None

    es = contextlib.ExitStack()
    with es:
        sbuf = lambda n, s, d: es.enter_context(nc.sbuf_tensor(n, s, d))
        b = B(nc, es)
        banks = Banks(nc, es)

        XR = []
        lrT = sbuf("lrT", [128, CC], F32)
        gT = sbuf("gT", [128, CC], F32)
        bT = sbuf("bT", [128, CC], F32)
        lsT = sbuf("lsT", [128, CC], F32)
        sgN = sbuf("sgN", [128, IC], F32)
        suN = sbuf("suN", [128, IC], F32)
        shN = sbuf("shN", [128, CC], F32)
        sgC = sbuf("sgC", [128, IC], F32)
        suC = sbuf("suC", [128, IC], F32)
        sdC = sbuf("sdC", [128, CC], F32)
        onesb = sbuf("onesb", [128, 128], BF16)
        sigx = sbuf("sigx", [1, 2], F32)
        rrow = sbuf("rrow", [1, 128], F32)
        sc1 = sbuf("sc1", [1, 8], F32)
        invb = sbuf("invb", [128, 1], F32)
        scb = sbuf("scb", [128, 1], F32)
        amaxA = sbuf("amaxA", [128, 32], F32)
        colsc = sbuf("colsc", [128, 1], F32)
        ieb = sbuf("ieb", [128, 1], F32)
        iob = sbuf("iob", [128, 1], F32)
        carry = sbuf("carry", [128, CC], F32)
        epsb = sbuf("epsb", [128, 1], F32)

        b.dma("sync", lrT[:], logr)
        b.dma("sync", gT[:], lng)
        b.dma("sync", bT[:], lnb)
        b.dma("sync", lsT[:], lsg)
        b.dma("gpsimd", ieb[:], iseven.partition_broadcast(128))
        b.dma("gpsimd", iob[:], isodd.partition_broadcast(128))
        b.dma("gpsimd", sc1[0:1, 0:1], clipx)
        b.dma("gpsimd", sc1[0:1, 1:2], clipd)

        b.op("vector", lambda e: e.memset(rrow[:], 0.0), "v")
        b.op("vector", lambda e: e.memset(epsb[:], EPS), "v")
        b.op("vector", lambda e: e.memset(onesb[:], 1.0), "v")
        b.wait("gpsimd", "v")
        b.dma("gpsimd", cw_i, rrow[0:1, 0:8])
        b.wait("gpsimd", "dg")
        b.op("gpsimd", lambda e: e.collective_compute(
            "AllReduce", ALU.add, replica_groups=[list(range(NCORE))],
            ins=[cw_i], outs=[cw_o]), "cc")

        b.wait("scalar", "dg")
        b.op("scalar", lambda e: e.activation(sigx[0:1, 0:2], sc1[0:1, 0:2], ACT.Sigmoid), "s")
        b.wait("scalar", "ds")
        b.op("scalar", lambda e: e.activation(lrT[:], lrT[:], ACT.Exp), "s")
        b.wait("vector", "s")
        b.op("vector", lambda e: e.tensor_scalar_min(lrT[:], lrT[:], 1.0), "v")

        WPREP = [
            ("g", wgN, wgT, wgq, IC, CC, C, I, sgN),
            ("u", wuN, wuT, wuq, IC, CC, C, I, suN),
            ("d", wdN, wdT, wdq, CC, IC, I, C, shN),
        ]
        wp_marks = {}
        natA = []
        trnA = []

        wp_ng = {}

        def emit_wp_nat(nat, wn, n_nat, nat_w, sN, tagn):
            ns = len(nat)
            ng = wp_ng.setdefault(tagn, [None] * ns)
            for ch in range(n_nat):
                sl = ch % ns
                if ng[sl] is not None:
                    b.wait("sync", "v", ng[sl])
                b.dma("sync", nat[sl][:, 0:nat_w], wn[128 * ch:128 * (ch + 1), :],
                      sem=f"{tagn}{sl}")
                b.wait("vector", f"{tagn}{sl}")
                ng[sl] = b.op("vector", lambda e, sl=sl, ch=ch, nat_w=nat_w, sN=sN:
                              e.reduce_max(sN[:, ch:ch + 1], nat[sl][:, 0:nat_w], AX,
                                           apply_absolute_value=True), "v")

        def emit_weight_prep_a():
            emit_wp_nat(natA, wgN, IC, C, sgN, "wn")
            emit_wp_nat(natA, wuN, IC, C, suN, "wn")
            emit_wp_nat(natA, wdN, CC, I, shN, "wn")

        def emit_weight_prep_b():
            with contextlib.ExitStack() as p0:
                sb0 = lambda n, s, d: p0.enter_context(nc.sbuf_tensor(n, s, d))
                trn = trnA
                qtmp = [sb0(f"qtmp{i}", [128, I], F32) for i in range(2)]
                q8w = [sb0(f"q8w{i}", [128, I], I8) for i in range(2)]
                recb = sb0("recb", [128, I], F32)
                sccol = sb0("sccol", [128, IC], F32)
                lastw = [None, None]
                tg = [None, None]
                for (key, wn, wt, wq_out, n_nat, n_tr, nat_w, tr_w, sN) in WPREP:
                    b.wait("vector", "v")
                    b.wait("vector", "dg")
                    b.op("vector", lambda e, sN=sN, n=n_nat:
                         e.reciprocal(sccol[:, 0:n], sN[:, 0:n]), "v")
                    b.wait("vector", "v")
                    b.op("vector", lambda e, n=n_nat:
                         e.tensor_scalar_mul(sccol[:, 0:n], sccol[:, 0:n], 127.0), "v")
                    b.op("vector", lambda e, sN=sN, n=n_nat:
                         e.tensor_scalar_mul(sN[:, 0:n], sN[:, 0:n], 1.0 / 127.0), "v")
                    b.wait("gpsimd", "v")
                    for ch in range(n_nat):
                        b.dma("gpsimd", srow[0:1, 128 * ch:128 * (ch + 1)],
                              sccol[:, ch:ch + 1])
                    b.wait("gpsimd", "dg")
                    b.dma("gpsimd", recb[:, 0:128 * n_nat],
                          srow[0:1, 0:128 * n_nat].partition_broadcast(128))
                    for ch in range(n_tr):
                        sl = ch % 2
                        if tg[sl] is not None:
                            b.wait("sync", "v", tg[sl])
                        b.dma("sync", trn[sl][:, 0:tr_w], wt[128 * ch:128 * (ch + 1), :],
                              sem=f"wt{sl}")
                        b.wait("vector", f"wt{sl}")
                        b.wait("vector", "dg")
                        b.wait("vector", "s")
                        tg[sl] = b.op("vector", lambda e, sl=sl, tr_w=tr_w:
                                      e.tensor_mul(qtmp[sl][:, 0:tr_w], trn[sl][:, 0:tr_w],
                                                   recb[:, 0:tr_w]), "v")
                        if lastw[sl] is not None:
                            b.wait("scalar", lastw[sl])
                        b.wait("scalar", "v")
                        b.op("scalar", lambda e, sl=sl, tr_w=tr_w:
                             e.activation(q8w[sl][:, 0:tr_w], qtmp[sl][:, 0:tr_w], ACT.Copy), "s")
                        b.wait("gpsimd", "s")
                        b.dma("gpsimd", wq_out[128 * ch:128 * (ch + 1), :],
                              q8w[sl][:, 0:tr_w], sem=f"wq8{key}{sl}")
                        lastw[sl] = f"wq8{key}{sl}"
                    wp_marks[key] = (b.cur(f"wq8{key}0"), b.cur(f"wq8{key}1"))


        def emit_global_scale(ncols, sig_col, cm_in_ap, cm_out_ap):
            b.wait("vector", "v")
            b.op("vector", lambda e, n=ncols:
                 e.reduce_max(colsc[:], amaxA[:, 0:n], AX), "v")
            b.wait("gpsimd", "v")
            b.dma("gpsimd", rrow[0:1, 0:128], colsc[:, 0:1])
            b.wait("vector", "dg")
            b.op("vector", lambda e: e.reduce_max(sc1[0:1, 2:3], rrow[0:1, 0:128], AX), "v")
            b.wait("gpsimd", "v")
            b.dma("gpsimd", cm_in_ap[0:1, 0:1], sc1[0:1, 2:3])
            b.wait("gpsimd", "dg")
            b.op("gpsimd", lambda e, ci=cm_in_ap, co=cm_out_ap: e.collective_compute(
                "AllReduce", ALU.max, replica_groups=[list(range(NCORE))],
                ins=[ci], outs=[co]), "cc")
            b.wait("gpsimd", "cc")
            b.dma("gpsimd", sc1[0:1, 3:4], cm_out_ap[0:1, 0:1])
            b.wait("vector", "dg")
            b.op("vector", lambda e, sc=sig_col: e.tensor_mul(
                sc1[0:1, 4:5], sc1[0:1, 3:4], sigx[0:1, sc:sc + 1]), "v")
            b.wait("vector", "v")
            b.op("vector", lambda e: e.tensor_scalar_mul(sc1[0:1, 6:7], sc1[0:1, 4:5], 1.0 / 127.0), "v")
            b.wait("gpsimd", "v")
            b.dma("gpsimd", stage[0:1, 0:2], sc1[0:1, 4:6])
            b.wait("gpsimd", "dg")
            b.dma("gpsimd", stage[0:1, 2:4], sc1[0:1, 6:8])
            b.wait("gpsimd", "dg")
            b.dma("gpsimd", colsc[:], stage[0:1, 0:1].partition_broadcast(128))
            b.dma("gpsimd", scb[:], stage[0:1, 2:3].partition_broadcast(128))
            b.wait("vector", "dg")
            b.op("vector", lambda e: e.reciprocal(invb[:], colsc[:]), "v")
            b.wait("vector", "v")
            b.op("vector", lambda e: e.tensor_scalar_mul(invb[:], invb[:], 127.0), "v")
            b.wait("scalar", "v")
            b.wait("scalar", "dg")

        def emit_stats(src_emit, m_b, r_b, tmp, xb, sqb):
            ps_s = [banks.acquire(b, i) for i in range(4)]
            ps_q = [banks.acquire(b, i + 4) for i in range(4)]
            g = [None, None]
            for c in range(CC):
                sl = c % 2
                if g[sl] is not None:
                    b.wait("scalar", "t", g[sl])
                src_emit(c, sl)
                b.op("scalar", lambda e, sl=sl: e.activation(sqb[sl][:], xb[sl][:], ACT.Square), "s")
                b.wait("tensor", "s")
                for tt in range(NT):
                    b.op("tensor", lambda e, sl=sl, tt=tt, c=c: e.matmul(
                        ps_s[tt][:], onesb[:], xb[sl][:, TT * tt:TT * (tt + 1)],
                        start=(c == 0), stop=(c == CC - 1)), None)
                for tt in range(NT):
                    g[sl] = b.op("tensor", lambda e, sl=sl, tt=tt, c=c: e.matmul(
                        ps_q[tt][:], onesb[:], sqb[sl][:, TT * tt:TT * (tt + 1)],
                        start=(c == 0), stop=(c == CC - 1)), "t")
            b.wait("scalar", "t")
            for tt in range(NT):
                sl_ = slice(TT * tt, TT * (tt + 1))
                b.op("scalar", lambda e, tt=tt, sl_=sl_: e.activation(
                    m_b[:, sl_], ps_s[tt][:], ACT.Copy, scale=1.0 / C), "s")
                b.op("scalar", lambda e, tt=tt, sl_=sl_: e.activation(
                    r_b[:, sl_], ps_q[tt][:], ACT.Copy, scale=1.0 / C), "s")
            b.wait("vector", "s")
            for tt in range(NT):
                sl_ = slice(TT * tt, TT * (tt + 1))
                b.op("vector", lambda e, sl_=sl_: e.tensor_mul(tmp[:], m_b[:, sl_], m_b[:, sl_]), None)
                b.op("vector", lambda e, sl_=sl_: e.tensor_sub(r_b[:, sl_], r_b[:, sl_], tmp[:]), "v")
            b.wait("scalar", "v")
            b.op("scalar", lambda e: e.activation(r_b[:], r_b[:], ACT.Ln, bias=epsb[:]), None)
            b.op("scalar", lambda e: e.activation(r_b[:], r_b[:], ACT.Exp, scale=-0.5), "s")
            for i in range(8):
                banks.release(i, "s", b.cur("s"))
            b.wait("vector", "s")

        def emit_ln_quant(src_chunk, xq_bf, quant_cm_idx, ln_name, mid=None,
                          ln_bufs=None):
            with contextlib.ExitStack() as pl:
                sbl = lambda n, s, d: pl.enter_context(nc.sbuf_tensor(n, s, d))
                if ln_bufs is not None:
                    ln1 = ln_bufs
                else:
                    ln1 = [sbl(f"{ln_name}_ln{c}", [128, TOK], F32) for c in range(CC)]
                q8 = [sbl(f"{ln_name}_q8{i}", [128, TOK], I8) for i in range(2)]
                with contextlib.ExitStack() as pxs:
                    sbx = lambda n, s, d: pxs.enter_context(nc.sbuf_tensor(n, s, d))
                    xb = [sbx(f"{ln_name}_xb{i}", [128, TOK], BF16) for i in range(2)]
                    sqb = [sbx(f"{ln_name}_sq{i}", [128, TOK], BF16) for i in range(2)]
                    m_b = sbx(f"{ln_name}_m", [128, TOK], F32)
                    r_b = sbx(f"{ln_name}_r", [128, TOK], F32)
                    tmp = sbx(f"{ln_name}_t", [128, TT], F32)

                    def src_stats(c, sl):
                        apc = src_chunk(c, sl, ("scalar",))
                        b.op("scalar", lambda e, sl=sl, apc=apc: e.activation(
                            xb[sl][:], apc, ACT.Copy), None)

                    emit_stats(src_stats, m_b, r_b, tmp, xb, sqb)
                    apcs = [None] * CC
                    apcs[0] = src_chunk(0, 0, ())
                    for c in range(CC):
                        if c + 1 < CC:
                            apcs[c + 1] = src_chunk(c + 1, (c + 1) % 2, ())
                        b.wait("vector", f"xs{c % 2}") if src_chunk is not None else None
                        apc = apcs[c]
                        b.op("vector", lambda e, c=c, apc=apc: e.tensor_sub(ln1[c][:], apc, m_b[:]), None)
                        b.op("vector", lambda e, c=c: e.tensor_mul(ln1[c][:], ln1[c][:], r_b[:]), None)
                        b.op("vector", lambda e, c=c: e.reduce_max(
                            amaxA[:, c:c + 1], ln1[c][:], AX, apply_absolute_value=True), "v")
                    dmb = dump_tensor(ln_name + "_mb", [128, TOK])
                    if dmb is not None:
                        b.wait("sync", "v")
                        b.dma("sync", dmb, m_b[:])
                        b.barrier()
                    drb = dump_tensor(ln_name + "_rb", [128, TOK])
                    if drb is not None:
                        b.wait("sync", "v")
                        b.dma("sync", drb, r_b[:])
                        b.barrier()
                dam = dump_tensor(ln_name + "_amax", [128, 32])
                if dam is not None:
                    b.wait("sync", "v")
                    b.dma("sync", dam, amaxA[:])
                    b.barrier()
                dln = dump_tensor(ln_name, [C, TOK])
                if dln is not None:
                    b.wait("sync", "v")
                    for c in range(CC):
                        b.dma("sync", dln[128 * c:128 * (c + 1), :], ln1[c][:])
                if mid is not None:
                    mid()
                emit_global_scale(CC, 0, cm_i[quant_cm_idx], cm_o[quant_cm_idx])
                qgv = [None, None]
                for c in range(CC):
                    sl = c % 2
                    if qgv[sl] is not None:
                        b.wait("scalar", f"xqc{sl}")
                    b.op("scalar", lambda e, c=c, sl=sl: e.activation(
                        q8[sl][:], ln1[c][:], ACT.Copy, scale=invb[:]), "s")
                    b.wait("gpsimd", "s")
                    qgv[sl] = b.dma("gpsimd", xq_bf[c][:], q8[sl][:], sem=f"xqc{sl}")
                b.barrier()

        def emit_ffn_gate_up(xq_bf, tag, dbufw=True, wait_marks=None):
            b.op("vector", lambda e: e.tensor_scalar(sgC[:], sgN[:, 0:IC], scb[:], None, ALU.mult), None)
            b.op("vector", lambda e: e.tensor_scalar(suC[:], suN[:, 0:IC], scb[:], None, ALU.mult), "v")
            with contextlib.ExitStack() as pf:
                sbf = lambda n, s, d: pf.enter_context(nc.sbuf_tensor(n, s, d))
                HW = I // 2
                nws = 2 if dbufw else 1
                wgw_ = [[sbf(f"{tag}gw{h}_{c}", [128, HW], BF16) for c in range(CC)]
                        for h in range(nws)]
                wuw_ = [[sbf(f"{tag}uw{h}_{c}", [128, HW], BF16) for c in range(CC)]
                        for h in range(nws)]
                wgw = [wgw_[h % nws] for h in range(2)]
                wuw = [wuw_[h % nws] for h in range(2)]
                silu = [sbf(f"{tag}sil{i}", [128, TOK], F32) for i in range(2)]
                tb = [sbf(f"{tag}tb{i}", [128, TOK], F32) for i in range(2)]
                tspg = [None, None]

                def load_g(h):
                    io = HW * h
                    for c in range(CC):
                        b.dma("gpsimd", wgw[h][c][:], wgq[128 * c:128 * (c + 1), io:io + HW],
                              sem=f"wg{h}")

                def load_u(h):
                    io = HW * h
                    for c in range(CC):
                        b.dma("gpsimd", wuw[h][c][:], wuq[128 * c:128 * (c + 1), io:io + HW],
                              sem=f"wu{h}")

                if wait_marks is not None:
                    b.wait("gpsimd", "wq8g0", wait_marks["g"][0])
                    b.wait("gpsimd", "wq8g1", wait_marks["g"][1])
                load_g(0)
                if dbufw:
                    load_g(1)
                if wait_marks is not None:
                    b.wait("gpsimd", "wq8u0", wait_marks["u"][0])
                    b.wait("gpsimd", "wq8u1", wait_marks["u"][1])
                load_u(0)
                if dbufw:
                    load_u(1)
                b.wait("tensor", "xqc0")
                b.wait("tensor", "xqc1")
                for half in range(2):
                    if not dbufw and half == 1:
                        b.wait("gpsimd", "t")
                        load_g(1)
                        load_u(1)
                    b.wait("tensor", f"wg{half}")
                    for ici in range(IC // 2):
                        ic = half * (IC // 2) + ici
                        sl = ic % 2
                        for i in range(4):
                            banks.acquire(b, i)
                        for c in range(CC):
                            for tt in range(NT):
                                gmm = b.op("tensor", lambda e, c=c, tt=tt, ici=ici, half=half: e.matmul(
                                    banks.t[tt][:],
                                    wgw[half][c][:, 128 * ici:128 * (ici + 1)],
                                    xq_bf[c][:, TT * tt:TT * (tt + 1)],
                                    start=(c == 0), stop=(c == CC - 1)),
                                    "t" if (c == CC - 1 and tt == NT - 1) else None)
                        b.wait("scalar", "t", gmm)
                        if tspg[sl] is not None:
                            b.wait("scalar", f"tsp{sl}")
                        for tt in range(NT):
                            sv = b.op("scalar", lambda e, tt=tt, ic=ic, sl=sl: e.activation(
                                silu[sl][:, TT * tt:TT * (tt + 1)], banks.t[tt][:],
                                ACT.Silu, scale=sgC[:, ic:ic + 1]), "s")
                        for i in range(4):
                            banks.release(i, "s", sv)
                        for i in range(4, 8):
                            banks.acquire(b, i)
                        if ici == 0:
                            b.wait("tensor", f"wu{half}")
                        for c in range(CC):
                            for tt in range(NT):
                                umm = b.op("tensor", lambda e, c=c, tt=tt, ici=ici, half=half: e.matmul(
                                    banks.t[tt + 4][:],
                                    wuw[half][c][:, 128 * ici:128 * (ici + 1)],
                                    xq_bf[c][:, TT * tt:TT * (tt + 1)],
                                    start=(c == 0), stop=(c == CC - 1)),
                                    "t" if (c == CC - 1 and tt == NT - 1) else None)
                        b.wait("vector", "t", umm)
                        b.wait("vector", "s", sv)
                        if tspg[sl] is not None:
                            b.wait("vector", f"tsp{sl}")
                        for tt in range(NT):
                            uv = b.op("vector", lambda e, tt=tt, ic=ic, sl=sl: e.scalar_tensor_tensor(
                                tb[sl][:, TT * tt:TT * (tt + 1)], banks.t[tt + 4][:],
                                suC[:, ic:ic + 1], silu[sl][:, TT * tt:TT * (tt + 1)],
                                ALU.mult, ALU.mult), "v")
                        for i in range(4, 8):
                            banks.release(i, "v", uv)
                        b.op("vector", lambda e, ic=ic, sl=sl: e.reduce_max(
                            amaxA[:, ic:ic + 1], tb[sl][:], AX,
                            apply_absolute_value=True), "v")
                        b.wait("sync", "v")
                        tspg[sl] = b.dma("sync", t_sp[128 * ic:128 * (ic + 1), :], tb[sl][:],
                                         sem=f"tsp{sl}")
                b.barrier()

        def emit_ffn_down(resid_src, store_out, tag, wait_marks=None):
            b.op("vector", lambda e: e.tensor_scalar(
                sdC[:], shN[:, 0:CC], scb[:], 0.5, ALU.mult, ALU.mult), "v")
            if wait_marks is not None:
                b.wait("gpsimd", "wq8d0", wait_marks["d"][0])
                b.wait("gpsimd", "wq8d1", wait_marks["d"][1])
            with contextlib.ExitStack() as pd:
                sbd = lambda n, s, d: pd.enter_context(nc.sbuf_tensor(n, s, d))
                qhw = [sbd(f"{tag}qh{ic}", [128, 1024], BF16) for ic in range(IC)]
                wdw = [sbd(f"{tag}dw{ic}", [128, 1024], BF16) for ic in range(IC)]
                tin = [sbd(f"{tag}ti{i}", [128, 1024], F32) for i in range(4)]
                q8h = [sbd(f"{tag}q8h{i}", [128, 1024], I8) for i in range(4)]
                for half in range(2):
                    to = 1024 * half
                    b.wait("gpsimd", "t")
                    qge = [None, None, None, None]
                    for ic in range(IC):
                        sl = ic % 4
                        if half == 0:
                            b.dma("gpsimd", wdw[ic][:], wdq[128 * ic:128 * (ic + 1), :],
                                  sem="wdw")
                        if qge[sl] is not None:
                            b.wait("sync", f"qc{sl}")
                        b.dma("sync", tin[sl][:], t_sp[128 * ic:128 * (ic + 1), to:to + 1024],
                              sem=f"ti{sl}")
                        b.wait("scalar", f"ti{sl}")
                        if qge[sl] is not None:
                            b.wait("scalar", f"qc{sl}")
                        b.op("scalar", lambda e, sl=sl: e.activation(
                            q8h[sl][:], tin[sl][:], ACT.Copy, scale=invb[:]), "s")
                        b.wait("gpsimd", "s")
                        qge[sl] = b.dma("gpsimd", qhw[ic][:], q8h[sl][:], sem=f"qc{sl}")
                    b.wait("tensor", "wdw")
                    for qs in ("qc0", "qc1", "qc2", "qc3"):
                        b.wait("tensor", qs)
                    stg = [None, None]
                    for hc in range(CC):
                        sl = hc % 2
                        for i in range(2):
                            banks.acquire(b, i)
                        for ic in range(IC):
                            for t2 in range(2):
                                dmm = b.op("tensor", lambda e, ic=ic, t2=t2, hc=hc: e.matmul(
                                    banks.t[t2][:],
                                    wdw[ic][:, 128 * hc:128 * (hc + 1)],
                                    qhw[ic][:, TT * t2:TT * (t2 + 1)],
                                    start=(ic == 0), stop=(ic == IC - 1)),
                                    "t" if (ic == IC - 1 and t2 == 1) else None)
                        src = resid_src(hc, half, sl)
                        b.wait("vector", "t", dmm)
                        for t2 in range(2):
                            vv = b.op("vector", lambda e, t2=t2, hc=hc, src=src, to=to:
                                      e.scalar_tensor_tensor(
                                          XR[hc][:, to + TT * t2:to + TT * (t2 + 1)],
                                          banks.t[t2][:], sdC[:, hc:hc + 1],
                                          src[:, TT * t2:TT * (t2 + 1)],
                                          ALU.mult, ALU.add), "v")
                        for i in range(2):
                            banks.release(i, "v", vv)
                        if store_out:
                            b.wait("sync", "v", vv)
                            stg[sl] = b.dma("sync", oxT[128 * hc:128 * (hc + 1), to:to + 1024],
                                            XR[hc][:, to:to + 1024])
                b.barrier()

        with contextlib.ExitStack() as pA:
            sbA = lambda n, s, d: pA.enter_context(nc.sbuf_tensor(n, s, d))
            xq_bf = [sbA(f"xqb{c}", [128, TOK], BF16) for c in range(CC)]

            with contextlib.ExitStack() as pXS:
                sbX = lambda n, s, d: pXS.enter_context(nc.sbuf_tensor(n, s, d))
                ln1P = [sbX(f"ln1buf{c}", [128, TOK], F32) for c in range(CC)]
                natA.extend(sbX(f"natA{i}", [128, I], F32) for i in range(2))
                trnA.extend(sbX(f"trnA{i}", [128, I], F32) for i in range(2))
                for c in range(CC):
                    b.dma("sync", ln1P[c][:], xT[128 * c:128 * (c + 1), :], sem=f"xl{c}")

                def p1_chunk(c, sl, engs):
                    for e_ in engs:
                        b.wait(e_, f"xl{c}")
                    return ln1P[c][:]

                emit_weight_prep_a()
                emit_ln_quant(p1_chunk, xq_bf, 0, "ln1", mid=emit_weight_prep_b,
                              ln_bufs=ln1P)
            dsn = dump_tensor("sgN", [128, IC])
            if dsn is not None:
                b.wait("sync", "v")
                b.dma("sync", dsn, sgN[:])
                dsu = dump_tensor("suN", [128, IC])
                b.dma("sync", dsu, suN[:])
                b.barrier()
            dwq = dump_tensor("wgq", [C, I], I8)
            if dwq is not None:
                for s_ in ("wq8g0", "wq8g1", "wq8u0", "wq8u1", "wq8d0", "wq8d1"):
                    b.wait("sync", s_)
                b.dma("sync", dwq, wgq)
                duq = dump_tensor("wuq", [C, I], I8)
                b.dma("sync", duq, wuq)
                b.barrier()
            dsc = dump_tensor("sc1", [1, 8])
            if dsc is not None:
                b.wait("sync", "v")
                b.wait("sync", "dg")
                b.dma("sync", dsc, sc1[:])
                b.barrier()
            dinv = dump_tensor("invb", [128, 2])
            if dinv is not None:
                b.wait("sync", "v")
                b.dma("sync", dinv[:, 0:1], invb[:], allow_slow_non_contiguous=True)
                b.dma("sync", dinv[:, 1:2], scb[:], allow_slow_non_contiguous=True)
                b.barrier()
            dxq = dump_tensor("xq", [C, TOK], BF16)
            if dxq is not None:
                for c in range(CC):
                    b.dma("sync", dxq[128 * c:128 * (c + 1), :], xq_bf[c][:])
                b.barrier()
            if _STOP_AFTER != "p1":
                emit_ffn_gate_up(xq_bf, "f1", wait_marks=wp_marks)
                dsg = dump_tensor("sgC", [128, 2 * IC])
                if dsg is not None:
                    b.wait("sync", "v")
                    b.dma("sync", dsg[:, 0:IC], sgC[:])
                    b.dma("sync", dsg[:, IC:2 * IC], suC[:])
                    b.barrier()
                dt1 = dump_tensor("t1", [I, TOK])
                if dt1 is not None:
                    b.wait("sync", "tsp0")
                    b.wait("sync", "tsp1")
                    b.dma("sync", dt1, t_sp)
                    b.barrier()
                emit_global_scale(IC, 1, cm_i[1], cm_o[1])

        if _STOP_AFTER in ("p1", "p2"):
            b.barrier()
            return _emit(nc, es, b)

        XR.extend(sbuf(f"xr{c}", [128, TOK], F32) for c in range(CC))
        with contextlib.ExitStack() as p3s:
            xs3 = [p3s.enter_context(nc.sbuf_tensor(f"xs3{i}", [128, 1024], F32))
                   for i in range(2)]

            def p3_src(hc, half, sl):
                b.wait("sync", "v")
                b.dma("sync", xs3[sl][:],
                      xT[128 * hc:128 * (hc + 1), 1024 * half:1024 * (half + 1)])
                b.wait("vector", "ds")
                return xs3[sl][:]

            emit_ffn_down(p3_src, False, "f1", wait_marks=wp_marks)
        dx1 = dump_tensor("x1", [C, TOK])
        if dx1 is not None:
            for c in range(CC):
                b.dma("sync", dx1[128 * c:128 * (c + 1), :], XR[c][:])
            b.barrier()
        if _STOP_AFTER == "p3":
            b.barrier()
            return _emit(nc, es, b)

        m2s_dbg = []
        with contextlib.ExitStack() as pT:
            sbT = lambda n, s, d: pT.enter_context(nc.sbuf_tensor(n, s, d))
            h_bf = [sbT(f"hbf{c}", [128, TOK], BF16) for c in range(CC)]
            V_bf = [sbT(f"vbf{c}", [128, TOK], BF16) for c in range(CC)]
            wsl = [sbT(f"wsl{i}", [128, C], BF16) for i in range(2)]

            with contextlib.ExitStack() as sA:
                sba = lambda n, s, d: sA.enter_context(nc.sbuf_tensor(n, s, d))
                xb = [sba(f"axb{i}", [128, TOK], BF16) for i in range(2)]
                sqb = [sba(f"asq{i}", [128, TOK], BF16) for i in range(2)]
                m2 = sba("m2", [128, TOK], F32)
                r2 = sba("r2", [128, TOK], F32)
                m2s_dbg.extend([m2, r2])
                tmp = sba("atmp", [128, TT], F32)
                xaf = sba("xaf", [128, TOK], F32)
                xaw = [sba(f"xaw{i}", [128, TOK], BF16) for i in range(2)]

                def srcA(c, sl):
                    b.op("scalar", lambda e, sl=sl, c=c: e.activation(
                        xb[sl][:], XR[c][:], ACT.Copy), None)

                emit_stats(srcA, m2, r2, tmp, xb, sqb)
                xag = [None, None]
                for c in range(CC):
                    sl = c % 2
                    b.op("vector", lambda e, c=c: e.tensor_sub(xaf[:], XR[c][:], m2[:]), None)
                    if xag[sl] is not None:
                        b.wait("vector", f"xa{sl}")
                    b.op("vector", lambda e, sl=sl: e.tensor_mul(xaw[sl][:], xaf[:], r2[:]), "v")
                    b.wait("sync", "v")
                    xag[sl] = b.dma("sync", xa_sp[128 * c:128 * (c + 1), :], xaw[sl][:],
                                    sem=f"xa{sl}")
                b.barrier()

            dm2 = dump_tensor("m2r2", [256, TOK])
            if dm2 is not None:
                b.dma("sync", dm2[0:128, :], m2s_dbg[0][:])
                b.dma("sync", dm2[128:256, :], m2s_dbg[1][:])
                b.barrier()
            with contextlib.ExitStack() as sB:
                sbb = lambda n, s, d: sB.enter_context(nc.sbuf_tensor(n, s, d))
                xb = [sbb(f"bxb{i}", [128, TOK], BF16) for i in range(2)]
                sqb = [sbb(f"bsq{i}", [128, TOK], BF16) for i in range(2)]
                m3 = sbb("m3", [128, TOK], F32)
                r3 = sbb("r3", [128, TOK], F32)
                tmp = sbb("btmp", [128, TT], F32)
                uf = sbb("uf", [128, TOK], F32)

                def srcB(c, sl):
                    b.wait("sync", "s")
                    b.dma("sync", xb[sl][:], xa_sp[128 * c:128 * (c + 1), :])
                    b.wait("scalar", "ds")

                ps_s = [banks.acquire(b, i) for i in range(4)]
                ps_q = [banks.acquire(b, i + 4) for i in range(4)]
                g = [None, None]
                for c in range(CC):
                    sl = c % 2
                    if g[sl] is not None:
                        b.wait("sync", "t", g[sl])
                    srcB(c, sl)
                    b.op("scalar", lambda e, sl=sl: e.activation(sqb[sl][:], xb[sl][:], ACT.Square), "s")
                    b.wait("tensor", "s")
                    for tt in range(NT):
                        b.op("tensor", lambda e, sl=sl, tt=tt, c=c: e.matmul(
                            ps_s[tt][:], onesb[:], xb[sl][:, TT * tt:TT * (tt + 1)],
                            start=(c == 0), stop=(c == CC - 1)), None)
                    for tt in range(NT):
                        g[sl] = b.op("tensor", lambda e, sl=sl, tt=tt, c=c: e.matmul(
                            ps_q[tt][:], onesb[:], sqb[sl][:, TT * tt:TT * (tt + 1)],
                            start=(c == 0), stop=(c == CC - 1)), "t")
                b.wait("scalar", "t")
                for tt in range(NT):
                    sl_ = slice(TT * tt, TT * (tt + 1))
                    b.op("scalar", lambda e, tt=tt, sl_=sl_: e.activation(
                        m3[:, sl_], ps_s[tt][:], ACT.Copy, scale=1.0 / C), "s")
                    b.op("scalar", lambda e, tt=tt, sl_=sl_: e.activation(
                        r3[:, sl_], ps_q[tt][:], ACT.Copy, scale=1.0 / C), "s")
                b.wait("vector", "s")
                for tt in range(NT):
                    sl_ = slice(TT * tt, TT * (tt + 1))
                    b.op("vector", lambda e, sl_=sl_: e.tensor_mul(tmp[:], m3[:, sl_], m3[:, sl_]), None)
                    b.op("vector", lambda e, sl_=sl_: e.tensor_sub(r3[:, sl_], r3[:, sl_], tmp[:]), "v")
                b.wait("scalar", "v")
                b.op("scalar", lambda e: e.activation(r3[:], r3[:], ACT.Ln, bias=epsb[:]), None)
                b.op("scalar", lambda e: e.activation(r3[:], r3[:], ACT.Exp, scale=-0.5), "s")
                for i in range(8):
                    banks.release(i, "s", b.cur("s"))
                b.wait("vector", "s")
                for c in range(CC):
                    sl = c % 2
                    b.wait("sync", "v")
                    b.wait("sync", "s")
                    b.dma("sync", xb[sl][:], xa_sp[128 * c:128 * (c + 1), :])
                    b.wait("vector", "ds")
                    b.op("vector", lambda e, sl=sl: e.tensor_sub(uf[:], xb[sl][:], m3[:]), None)
                    b.op("vector", lambda e: e.tensor_mul(uf[:], uf[:], r3[:]), "v")
                    b.wait("scalar", "v")
                    b.op("scalar", lambda e, c=c: e.activation(
                        h_bf[c][:], uf[:], ACT.Identity,
                        scale=gT[:, c:c + 1], bias=bT[:, c:c + 1]), "s")
                b.barrier()
            dxa = dump_tensor("xa", [C, TOK], BF16)
            if dxa is not None:
                b.wait("sync", "xa0")
                b.wait("sync", "xa1")
                b.dma("sync", dxa, xa_sp)
                b.barrier()
            dh = dump_tensor("h", [C, TOK], BF16)
            if dh is not None:
                for c in range(CC):
                    b.dma("sync", dh[128 * c:128 * (c + 1), :], h_bf[c][:])
                b.barrier()

            with contextlib.ExitStack() as sC:
                sbc = lambda n, s, d: sC.enter_context(nc.sbuf_tensor(n, s, d))
                K_bf = [sbc(f"kbf{c}", [128, TOK], BF16) for c in range(CC)]
                KA = sbc("KA", [128, TOK], BF16)
                onesf = sbc("onesf", [128, 2], F32)
                bsl = [sbc(f"bsl{i}", [128, 128], BF16) for i in range(2)]
                errt = sbc("errt", [128, TOK], F32)
                grdt = sbc("grdt", [128, TOK], F32)
                shft = sbc("shft", [128, TOK], F32)
                b.op("vector", lambda e: e.memset(onesf[:], 1.0), "v")

                slab = Slab(wsl, "ws")
                flat = [(wk_s[dc], K_bf, dc) for dc in range(CC)] + \
                       [(wv_s[dc], V_bf, dc) for dc in range(CC)] + [(a_s, None, 0)]
                slab.load(b, 0, flat[0][0])
                for it, (wap, dst, dc) in enumerate(flat):
                    if it + 1 < len(flat):
                        slab.load(b, it + 1, flat[it + 1][0])
                    w = slab.use(b, it)
                    rhs = h_bf if dst is not None else K_bf
                    if dst is None:
                        b.wait("tensor", "s")
                    for i in range(4):
                        banks.acquire(b, i)
                    for c in range(CC):
                        for tt in range(NT):
                            mm = b.op("tensor", lambda e, c=c, tt=tt, w=w, rhs=rhs: e.matmul(
                                banks.t[tt][:], w[:, 128 * c:128 * (c + 1)],
                                rhs[c][:, TT * tt:TT * (tt + 1)],
                                start=(c == 0), stop=(c == CC - 1)),
                                "t" if (c == CC - 1 and tt == NT - 1) else None)
                    slab.done(it, mm)
                    b.wait("scalar", "t", mm)
                    out_t = (dst[dc] if dst is not None else KA)
                    for tt in range(NT):
                        sv = b.op("scalar", lambda e, tt=tt, out_t=out_t: e.activation(
                            out_t[:, TT * tt:TT * (tt + 1)], banks.t[tt][:], ACT.Copy), "s")
                    for i in range(4):
                        banks.release(i, "s", sv)

                bslab = Slab(bsl, "bs")
                slab.load(b, 17, w0_s[0])
                bslab.load(b, 0, b_s[0])
                eg = None
                sg_ = None
                for dc in range(CC):
                    if dc + 1 < CC:
                        slab.load(b, 17 + dc + 1, w0_s[dc + 1])
                        bslab.load(b, dc + 1, b_s[dc + 1])
                    w = slab.use(b, 17 + dc)
                    bw = bslab.use(b, dc)
                    for i in range(4):
                        banks.acquire(b, i)
                    for c in range(CC):
                        for tt in range(NT):
                            b.op("tensor", lambda e, c=c, tt=tt, w=w: e.matmul(
                                banks.t[tt][:], w[:, 128 * c:128 * (c + 1)],
                                K_bf[c][:, TT * tt:TT * (tt + 1)],
                                start=(c == 0), stop=False), None)
                    for tt in range(NT):
                        mm = b.op("tensor", lambda e, tt=tt, bw=bw: e.matmul(
                            banks.t[tt][:], bw[:], KA[:, TT * tt:TT * (tt + 1)],
                            start=False, stop=True),
                            "t" if tt == NT - 1 else None)
                    slab.done(17 + dc, mm)
                    bslab.done(dc, mm)
                    b.wait("vector", "t", mm)
                    if eg is not None:
                        b.wait("vector", "es")
                    for tt in range(NT):
                        vv = b.op("vector", lambda e, tt=tt, dc=dc: e.tensor_sub(
                            errt[:, TT * tt:TT * (tt + 1)], banks.t[tt][:],
                            V_bf[dc][:, TT * tt:TT * (tt + 1)]), "v")
                    for i in range(4):
                        banks.release(i, "v", vv)
                    b.op("vector", lambda e, dc=dc: e.tensor_mul(
                        grdt[:], K_bf[dc][:], errt[:]), None)
                    b.op("vector", lambda e: e.memset(shft[:, 0:1], 0.0), None)
                    b.op("vector", lambda e: e.tensor_tensor_scan(
                        shft[:, 1:TOK], onesf[:, 0:1].broadcast_to([128, TOK - 1]),
                        grdt[:, 0:TOK - 1], 0.0, ALU.mult, ALU.add), "v")
                    b.wait("vector", "v")
                    b.op("vector", lambda e, dc=dc: e.tensor_add(
                        carry[:, dc:dc + 1], shft[:, TOK - 1:TOK], grdt[:, TOK - 1:TOK]), "v")
                    b.wait("sync", "v")
                    eg = b.dma("sync", err_sp[128 * dc:128 * (dc + 1), :], errt[:], sem="es")
                    sg_ = b.dma("sync", sh_sp[128 * dc:128 * (dc + 1), :], shft[:], sem="es")
                b.barrier()

            b.op("vector", lambda e: e.tensor_scalar(carry[:], carry[:], ieb[:], None, ALU.mult), "v")
            b.wait("gpsimd", "v")
            b.dma("gpsimd", cy_i, carry[:])
            b.wait("gpsimd", "dg")
            b.op("gpsimd", lambda e: e.collective_compute(
                "AllReduce", ALU.add, replica_groups=[[0, 1], [2, 3], [4, 5], [6, 7]],
                ins=[cy_i], outs=[cy_o]), "cc")
            b.wait("gpsimd", "cc")
            b.dma("gpsimd", carry[:], cy_o)
            b.wait("vector", "dg")
            b.op("vector", lambda e: e.tensor_scalar(carry[:], carry[:], iob[:], None, ALU.mult), "v")
            b.barrier()

            with contextlib.ExitStack() as sF:
                sbff = lambda n, s, d: sF.enter_context(nc.sbuf_tensor(n, s, d))
                psb = sbff("psb", [128, TOK], F32)
                qlr = sbff("qlr", [128, TOK], F32)
                c1t = sbff("c1t", [128, TOK], F32)
                errF = [sbff(f"errF{i}", [128, TOK], F32) for i in range(2)]
                shF = [sbff(f"shF{i}", [128, TOK], F32) for i in range(2)]
                opb = [sbff(f"opb{i}", [128, TOK], BF16) for i in range(2)]
                b.dma("gpsimd", psb[:], posr.partition_broadcast(128))
                og = [None, None]
                cons = [None, None]
                sld = [None, None]
                slab = Slab(wsl, "ws")
                slab.load(b, 0, wq_s[0])

                def ld_es(dc):
                    sl = dc % 2
                    if cons[sl] is not None:
                        b.wait("sync", "v", cons[sl])
                    b.dma("sync", errF[sl][:], err_sp[128 * dc:128 * (dc + 1), :],
                          sem=f"ef{sl}")
                    sld[sl] = b.dma("sync", shF[sl][:], sh_sp[128 * dc:128 * (dc + 1), :],
                                    sem=f"ef{sl}")

                ld_es(0)
                for dc in range(CC):
                    sl = dc % 2
                    if dc + 1 < CC:
                        slab.load(b, dc + 1, wq_s[dc + 1])
                    w = slab.use(b, dc)
                    myld = sld[sl]
                    for i in range(4):
                        banks.acquire(b, i)
                    for c in range(CC):
                        for tt in range(NT):
                            mm = b.op("tensor", lambda e, c=c, tt=tt, w=w: e.matmul(
                                banks.t[tt][:], w[:, 128 * c:128 * (c + 1)],
                                h_bf[c][:, TT * tt:TT * (tt + 1)],
                                start=(c == 0), stop=(c == CC - 1)),
                                "t" if (c == CC - 1 and tt == NT - 1) else None)
                    slab.done(dc, mm)
                    b.wait("vector", "t", mm)
                    b.wait("vector", f"ef{sl}")
                    b.wait("vector", "dg")
                    for tt in range(NT):
                        vv = b.op("vector", lambda e, tt=tt, dc=dc: e.scalar_tensor_tensor(
                            qlr[:, TT * tt:TT * (tt + 1)], banks.t[tt][:],
                            lrT[:, dc:dc + 1], psb[:, TT * tt:TT * (tt + 1)],
                            ALU.mult, ALU.mult), "v")
                    for i in range(4):
                        banks.release(i, "v", vv)
                    b.op("vector", lambda e, dc=dc, sl=sl: e.scalar_tensor_tensor(
                        c1t[:], shF[sl][:], carry[:, dc:dc + 1], qlr[:],
                        ALU.add, ALU.mult), None)
                    cv = b.op("vector", lambda e, sl=sl: e.tensor_sub(qlr[:], errF[sl][:], c1t[:]), "v")
                    cons[sl] = cv
                    if dc + 1 < CC:
                        ld_es(dc + 1)
                    if og[sl] is not None:
                        b.wait("vector", f"os{sl}")
                    b.op("vector", lambda e, dc=dc, sl=sl: e.tensor_add(
                        opb[sl][:], qlr[:], V_bf[dc][:]), "v")
                    b.wait("sync", "v")
                    og[sl] = b.dma("sync", op_sp[128 * dc:128 * (dc + 1), :], opb[sl][:],
                                   sem=f"os{sl}")
                b.barrier()

            with contextlib.ExitStack() as sG:
                sbg = lambda n, s, d: sG.enter_context(nc.sbuf_tensor(n, s, d))
                xaG = [sbg(f"xag{c}", [128, TOK], BF16) for c in range(CC)]
                gtb = [sbg(f"gtb{i}", [128, TOK], BF16) for i in range(2)]
                opG = [sbg(f"opG{i}", [128, TOK], BF16) for i in range(2)]
                x1t = sbg("x1t", [128, TOK], F32)
                og_bf = h_bf
                for c in range(CC):
                    b.dma("sync", xaG[c][:], xa_sp[128 * c:128 * (c + 1), :])
                b.wait("tensor", "ds")
                slab = Slab(wsl, "ws")
                slab.load(b, 0, wg_s[0])
                for dc in range(CC):
                    sl = dc % 2
                    if dc + 1 < CC:
                        slab.load(b, dc + 1, wg_s[dc + 1])
                    w = slab.use(b, dc)
                    b.wait("sync", "v")
                    opg = b.dma("sync", opG[sl][:], op_sp[128 * dc:128 * (dc + 1), :],
                                sem=f"op{sl}")
                    for i in range(4):
                        banks.acquire(b, i)
                    for c in range(CC):
                        for tt in range(NT):
                            mm = b.op("tensor", lambda e, c=c, tt=tt, w=w: e.matmul(
                                banks.t[tt][:], w[:, 128 * c:128 * (c + 1)],
                                xaG[c][:, TT * tt:TT * (tt + 1)],
                                start=(c == 0), stop=(c == CC - 1)),
                                "t" if (c == CC - 1 and tt == NT - 1) else None)
                    slab.done(dc, mm)
                    b.wait("scalar", "t", mm)
                    b.wait("scalar", "v")
                    for tt in range(NT):
                        sv = b.op("scalar", lambda e, tt=tt, sl=sl: e.activation(
                            gtb[sl][:, TT * tt:TT * (tt + 1)], banks.t[tt][:], ACT.Sigmoid), "s")
                    for i in range(4):
                        banks.release(i, "s", sv)
                    b.wait("vector", "s", sv)
                    b.wait("vector", f"op{sl}")
                    b.op("vector", lambda e, dc=dc, sl=sl: e.tensor_mul(
                        og_bf[dc][:], opG[sl][:], gtb[sl][:]), "v")
                b.barrier()

                slab2 = Slab(wsl, "ws")
                slab2.load(b, 0, wo_s[0])
                for hc in range(CC):
                    sl = hc % 2
                    if hc + 1 < CC:
                        slab2.load(b, hc + 1, wo_s[hc + 1])
                    w = slab2.use(b, hc)
                    for i in range(4):
                        banks.acquire(b, i)
                    for dc in range(CC):
                        for tt in range(NT):
                            mm = b.op("tensor", lambda e, dc=dc, tt=tt, w=w: e.matmul(
                                banks.t[tt][:], w[:, 128 * dc:128 * (dc + 1)],
                                og_bf[dc][:, TT * tt:TT * (tt + 1)],
                                start=(dc == 0), stop=(dc == CC - 1)),
                                "t" if (dc == CC - 1 and tt == NT - 1) else None)
                    slab2.done(hc, mm)
                    b.op("vector", lambda e, hc=hc: e.tensor_add(
                        x1t[:], XR[hc][:], xaG[hc][:]), None)
                    b.wait("vector", "t", mm)
                    for tt in range(NT):
                        vv = b.op("vector", lambda e, tt=tt, hc=hc: e.scalar_tensor_tensor(
                            XR[hc][:, TT * tt:TT * (tt + 1)], banks.t[tt][:],
                            lsT[:, hc:hc + 1], x1t[:, TT * tt:TT * (tt + 1)],
                            ALU.mult, ALU.add), "v")
                    for i in range(4):
                        banks.release(i, "v", vv)
                b.barrier()

        dx2 = dump_tensor("x2", [C, TOK])
        if dx2 is not None:
            for c in range(CC):
                b.dma("sync", dx2[128 * c:128 * (c + 1), :], XR[c][:])
            b.barrier()
        if _STOP_AFTER == "p4":
            b.barrier()
            return _emit(nc, es, b)

        with contextlib.ExitStack() as pB:
            sbB = lambda n, s, d: pB.enter_context(nc.sbuf_tensor(n, s, d))
            xq_bf = [sbB(f"xqc{c}", [128, TOK], BF16) for c in range(CC)]

            def p5_chunk(c, sl, engs):
                return XR[c][:]

            emit_ln_quant(p5_chunk, xq_bf, 2, "ln3")
            emit_ffn_gate_up(xq_bf, "f2", dbufw=False)
            emit_global_scale(IC, 1, cm_i[3], cm_o[3])

            def p5_src(hc, half, sl):
                return XR[hc][:, 1024 * half:1024 * (half + 1)]

            emit_ffn_down(p5_src, True, "f2")

        b.barrier()
        return _emit(nc, es, b)


def _emit(nc, es, b):
    block = es.enter_context(nc.Block())

    @block.sync
    def _(e):
        for fn in b.q["sync"]:
            fn(e)

    @block.gpsimd
    def _(e):
        for fn in b.q["gpsimd"]:
            fn(e)

    @block.vector
    def _(e):
        for fn in b.q["vector"]:
            fn(e)

    @block.scalar
    def _(e):
        for fn in b.q["scalar"]:
            fn(e)

    @block.tensor
    def _(e):
        for fn in b.q["tensor"]:
            fn(e)

    return nc


def kernel(**inputs):
    f4 = np.float32
    x = np.asarray(inputs["L0_attn_x"], f4).reshape(NCORE * TOK, C)
    Wq = np.asarray(inputs["L0_attn_Wq"], f4)
    Wk = np.asarray(inputs["L0_attn_Wk"], f4)
    Wv = np.asarray(inputs["L0_attn_Wv"], f4)
    Wo = np.asarray(inputs["L0_attn_Wo"], f4)
    Wg = np.asarray(inputs["L0_attn_Wg"], f4)
    W0 = np.asarray(inputs["L0_attn_W0"], f4)
    A_init = np.asarray(inputs["L0_attn_A_init"], f4)
    B_init = np.asarray(inputs["L0_attn_B_init"], f4)

    wq_s = _slab(np.ascontiguousarray(Wq.T), CC)
    wk_s = _slab(np.ascontiguousarray(Wk.T), CC)
    wv_s = _slab(np.ascontiguousarray(Wv.T), CC)
    wg_s = _slab(np.ascontiguousarray(Wg.T), CC)
    wo_s = _slab(np.ascontiguousarray(Wo.T), CC)
    w0_s = _slab(np.ascontiguousarray(W0), CC)
    A_pad = np.zeros((C, 128), f4)
    A_pad[:, :8] = A_init
    a_s = _slab(A_pad, 1).reshape(128, C)
    B_pad = np.zeros((128, C), f4)
    B_pad[:8, :] = B_init
    b_s = np.ascontiguousarray(B_pad.reshape(128, CC, 128).transpose(1, 0, 2))

    pos = np.arange(1, 4097, dtype=f4)
    pos_scale = (f4(1.0) / (f4(1.0) + f4(0.1) * np.log(pos))).astype(f4)

    common = dict(
        wq_s=wq_s, wk_s=wk_s, wv_s=wv_s, wg_s=wg_s, wo_s=wo_s, w0_s=w0_s,
        a_s=a_s, b_s=b_s,
        wgT=np.ascontiguousarray(np.asarray(inputs["L0_ffn_w_gate"], f4).T),
        wuT=np.ascontiguousarray(np.asarray(inputs["L0_ffn_w_up"], f4).T),
        wdT=np.ascontiguousarray(np.asarray(inputs["L0_ffn_w_down"], f4).T),
        wgN=np.ascontiguousarray(np.asarray(inputs["L0_ffn_w_gate"], f4)),
        wuN=np.ascontiguousarray(np.asarray(inputs["L0_ffn_w_up"], f4)),
        wdN=np.ascontiguousarray(np.asarray(inputs["L0_ffn_w_down"], f4)),
        logr=_vec8(inputs["L0_attn_log_lr"]), lng=_vec8(inputs["L0_attn_ln_g"]),
        lnb=_vec8(inputs["L0_attn_ln_b"]), lsg=_vec8(inputs["L0_attn_ls_g"]),
        clipx=np.asarray(inputs["L0_ffn_clip_x"], f4).reshape(1, 1),
        clipd=np.asarray(inputs["L0_ffn_clip_down"], f4).reshape(1, 1),
    )
    in_maps = []
    for k in range(NCORE):
        m = dict(common)
        m["xT"] = np.ascontiguousarray(x[TOK * k:TOK * (k + 1), :].T)
        seq_half = k % 2
        m["posr"] = np.ascontiguousarray(
            pos_scale[TOK * seq_half:TOK * (seq_half + 1)].reshape(1, TOK))
        m["iseven"] = np.asarray([[1.0 if k % 2 == 0 else 0.0]], f4)
        m["isodd"] = np.asarray([[1.0 if k % 2 == 1 else 0.0]], f4)
        in_maps.append(m)

    nc = build_program()
    res = run_bass_kernel_spmd(nc, in_maps, core_ids=list(range(NCORE)))
    globals()["_LAST_RESULTS"] = res
    out = np.empty((NCORE * TOK, C), f4)
    for k in range(NCORE):
        out[TOK * k:TOK * (k + 1), :] = res.results[k]["oxT"].T
    return out.reshape(4, 4096, C)


# version 10
# speedup vs baseline: 1.0363x; 1.0107x over previous
import contextlib
import numpy as np
import ml_dtypes
import concourse.bass as bass
import concourse.mybir as mybir
from concourse.bass_utils import run_bass_kernel_spmd

F32 = mybir.dt.float32
BF16 = mybir.dt.bfloat16
I8 = mybir.dt.int8
ALU = mybir.AluOpType
ACT = mybir.ActivationFunctionType
AX = mybir.AxisListType.X

NCORE = 8
TOK = 2048
C = 1024
CC = 8
I = 2816
IC = 22
NT = 4
TT = 512
EPS = 1e-5

ENGS = ("sync", "gpsimd", "vector", "scalar", "tensor")

_STOP_AFTER = None
_DUMPS = []


class B:

    def __init__(self, nc, es):
        self.nc = nc
        self.es = es
        self.q = {e: [] for e in ENGS}
        self.sems = {}

    def sem(self, name):
        if name not in self.sems:
            h = self.es.enter_context(self.nc.semaphore(name))
            self.sems[name] = [h, 0]
        return self.sems[name]

    def op(self, eng, fn, sem=None, k=1):
        if sem is None:
            self.q[eng].append(fn)
            return None
        rec = self.sem(sem)
        rec[1] += k
        h = rec[0]
        self.q[eng].append(lambda e, fn=fn, h=h, k=k: fn(e).then_inc(h, k))
        return rec[1]

    def dma(self, eng, out, in_, sem=None, **kw):
        if sem is None:
            sem = {"sync": "ds", "gpsimd": "dg", "scalar": "dsc"}[eng]
        return self.op(eng, lambda e: e.dma_start(out=out, in_=in_, **kw), sem, 16)

    def wait(self, eng, sem, val=None):
        rec = self.sem(sem)
        v = rec[1] if val is None else val
        if v is None or v <= 0:
            return
        h = rec[0]
        self.q[eng].append(lambda e, h=h, v=v: e.wait_ge(h, v))

    def cur(self, sem):
        return self.sem(sem)[1]

    def barrier(self, engs=ENGS):
        snap = [(name, rec[1]) for name, rec in self.sems.items() if rec[1] > 0]
        for e in engs:
            for name, v in snap:
                self.wait(e, name, v)


class Banks:
    def __init__(self, nc, es):
        self.t = [es.enter_context(nc.psum_tensor(f"pb{i}", [128, TT], F32))
                  for i in range(8)]
        self.guard = [None] * 8

    def acquire(self, b, i):
        if self.guard[i] is not None:
            b.wait("tensor", *self.guard[i])
            self.guard[i] = None
        return self.t[i]

    def release(self, i, sem, val):
        self.guard[i] = (sem, val)


class Slab:

    def __init__(self, tiles, name):
        self.tiles = tiles
        self.name = name
        self.guard = [None] * len(tiles)

    def load(self, b, i, src):
        sl = i % len(self.tiles)
        if self.guard[sl] is not None:
            b.wait("gpsimd", self.guard[sl][0], self.guard[sl][1])
            self.guard[sl] = None
        b.dma("gpsimd", self.tiles[sl][:], src, sem=f"{self.name}{sl}")

    def use(self, b, i):
        sl = i % len(self.tiles)
        b.wait("tensor", f"{self.name}{sl}")
        return self.tiles[sl]

    def done(self, i, mmval, sem="t"):
        self.guard[i % len(self.tiles)] = (sem, mmval)


def _slab(wT, n_out_chunks):
    K = wT.shape[0]
    kc = K // 128
    a = wT.reshape(kc, 128, n_out_chunks, 128)
    return np.ascontiguousarray(a.transpose(2, 1, 0, 3).reshape(n_out_chunks, 128, K))


def _vec8(v):
    return np.ascontiguousarray(np.asarray(v, np.float32).reshape(CC, 128).T)


def build_program():
    nc = bass.Bass("TRN2", target_bir_lowering=False)
    ap = lambda t: t.ap()

    xT = ap(nc.dram_tensor("xT", [C, TOK], F32, kind="ExternalInput"))
    wq_s = ap(nc.dram_tensor("wq_s", [CC, 128, C], F32, kind="ExternalInput"))
    wk_s = ap(nc.dram_tensor("wk_s", [CC, 128, C], F32, kind="ExternalInput"))
    wv_s = ap(nc.dram_tensor("wv_s", [CC, 128, C], F32, kind="ExternalInput"))
    wg_s = ap(nc.dram_tensor("wg_s", [CC, 128, C], F32, kind="ExternalInput"))
    wo_s = ap(nc.dram_tensor("wo_s", [CC, 128, C], F32, kind="ExternalInput"))
    w0_s = ap(nc.dram_tensor("w0_s", [CC, 128, C], F32, kind="ExternalInput"))
    a_s = ap(nc.dram_tensor("a_s", [128, C], F32, kind="ExternalInput"))
    b_s = ap(nc.dram_tensor("b_s", [CC, 128, 128], F32, kind="ExternalInput"))
    wgT = ap(nc.dram_tensor("wgT", [C, I], F32, kind="ExternalInput"))
    wuT = ap(nc.dram_tensor("wuT", [C, I], F32, kind="ExternalInput"))
    wdT = ap(nc.dram_tensor("wdT", [I, C], F32, kind="ExternalInput"))
    wgN = ap(nc.dram_tensor("wgN", [I, C], F32, kind="ExternalInput"))
    wuN = ap(nc.dram_tensor("wuN", [I, C], F32, kind="ExternalInput"))
    wdN = ap(nc.dram_tensor("wdN", [C, I], F32, kind="ExternalInput"))
    logr = ap(nc.dram_tensor("logr", [128, CC], F32, kind="ExternalInput"))
    lng = ap(nc.dram_tensor("lng", [128, CC], F32, kind="ExternalInput"))
    lnb = ap(nc.dram_tensor("lnb", [128, CC], F32, kind="ExternalInput"))
    lsg = ap(nc.dram_tensor("lsg", [128, CC], F32, kind="ExternalInput"))
    clipx = ap(nc.dram_tensor("clipx", [1, 1], F32, kind="ExternalInput"))
    clipd = ap(nc.dram_tensor("clipd", [1, 1], F32, kind="ExternalInput"))
    posr = ap(nc.dram_tensor("posr", [1, TOK], F32, kind="ExternalInput"))
    iseven = ap(nc.dram_tensor("iseven", [1, 1], F32, kind="ExternalInput"))
    isodd = ap(nc.dram_tensor("isodd", [1, 1], F32, kind="ExternalInput"))

    oxT = ap(nc.dram_tensor("oxT", [C, TOK], F32, kind="ExternalOutput"))

    wgq = ap(nc.dram_tensor("wgq", [C, I], I8, kind="Internal"))
    wuq = ap(nc.dram_tensor("wuq", [C, I], I8, kind="Internal"))
    wdq = ap(nc.dram_tensor("wdq", [I, C], I8, kind="Internal"))
    t_sp = ap(nc.dram_tensor("t_sp", [I, TOK], F32, kind="Internal"))
    xa_sp = ap(nc.dram_tensor("xa_sp", [C, TOK], BF16, kind="Internal"))
    err_sp = ap(nc.dram_tensor("err_sp", [C, TOK], F32, kind="Internal"))
    sh_sp = ap(nc.dram_tensor("sh_sp", [C, TOK], F32, kind="Internal"))
    op_sp = ap(nc.dram_tensor("op_sp", [C, TOK], BF16, kind="Internal"))
    srow = ap(nc.dram_tensor("srow", [1, max(I, C)], F32, kind="Internal"))
    stage = ap(nc.dram_tensor("stage", [1, 16], F32, kind="Internal"))
    cw_i = ap(nc.dram_tensor("cw_i", [1, 8], F32, kind="Internal"))
    cw_o = ap(nc.dram_tensor("cw_o", [1, 8], F32, kind="Internal"))
    cm_i = [ap(nc.dram_tensor(f"cm_i{k}", [1, 8], F32, kind="Internal")) for k in range(4)]
    cm_o = [ap(nc.dram_tensor(f"cm_o{k}", [1, 8], F32, kind="Internal")) for k in range(4)]
    cy_i = ap(nc.dram_tensor("cy_i", [128, 8], F32, kind="Internal"))
    cy_o = ap(nc.dram_tensor("cy_o", [128, 8], F32, kind="Internal"))

    dumps = {}

    def dump_tensor(name, shape, dtype=F32):
        if name in _DUMPS:
            dumps[name] = ap(nc.dram_tensor("dbg_" + name, shape, dtype,
                                            kind="ExternalOutput"))
            return dumps[name]
        return None

    es = contextlib.ExitStack()
    with es:
        sbuf = lambda n, s, d: es.enter_context(nc.sbuf_tensor(n, s, d))
        b = B(nc, es)
        banks = Banks(nc, es)

        XR = []
        lrT = sbuf("lrT", [128, CC], F32)
        gT = sbuf("gT", [128, CC], F32)
        bT = sbuf("bT", [128, CC], F32)
        lsT = sbuf("lsT", [128, CC], F32)
        sgN = sbuf("sgN", [128, IC], F32)
        suN = sbuf("suN", [128, IC], F32)
        shN = sbuf("shN", [128, CC], F32)
        sgC = sbuf("sgC", [128, IC], F32)
        suC = sbuf("suC", [128, IC], F32)
        sdC = sbuf("sdC", [128, CC], F32)
        onesb = sbuf("onesb", [128, 128], BF16)
        sigx = sbuf("sigx", [1, 2], F32)
        rrow = sbuf("rrow", [1, 128], F32)
        sc1 = sbuf("sc1", [1, 8], F32)
        invb = sbuf("invb", [128, 1], F32)
        scb = sbuf("scb", [128, 1], F32)
        amaxA = sbuf("amaxA", [128, 32], F32)
        colsc = sbuf("colsc", [128, 1], F32)
        ieb = sbuf("ieb", [128, 1], F32)
        iob = sbuf("iob", [128, 1], F32)
        carry = sbuf("carry", [128, CC], F32)
        epsb = sbuf("epsb", [128, 1], F32)

        b.dma("sync", lrT[:], logr)
        b.dma("sync", gT[:], lng)
        b.dma("sync", bT[:], lnb)
        b.dma("sync", lsT[:], lsg)
        b.dma("gpsimd", ieb[:], iseven.partition_broadcast(128))
        b.dma("gpsimd", iob[:], isodd.partition_broadcast(128))
        b.dma("gpsimd", sc1[0:1, 0:1], clipx)
        b.dma("gpsimd", sc1[0:1, 1:2], clipd)

        b.op("vector", lambda e: e.memset(rrow[:], 0.0), "v")
        b.op("vector", lambda e: e.memset(epsb[:], EPS), "v")
        b.op("vector", lambda e: e.memset(onesb[:], 1.0), "v")
        b.wait("gpsimd", "v")
        b.dma("gpsimd", cw_i, rrow[0:1, 0:8])
        b.wait("gpsimd", "dg")
        b.op("gpsimd", lambda e: e.collective_compute(
            "AllReduce", ALU.add, replica_groups=[list(range(NCORE))],
            ins=[cw_i], outs=[cw_o]), "cc")

        b.wait("scalar", "dg")
        b.op("scalar", lambda e: e.activation(sigx[0:1, 0:2], sc1[0:1, 0:2], ACT.Sigmoid), "s")
        b.wait("scalar", "ds")
        b.op("scalar", lambda e: e.activation(lrT[:], lrT[:], ACT.Exp), "s")
        b.wait("vector", "s")
        b.op("vector", lambda e: e.tensor_scalar_min(lrT[:], lrT[:], 1.0), "v")

        WPREP = [
            ("g", wgN, wgT, wgq, IC, CC, C, I, sgN),
            ("u", wuN, wuT, wuq, IC, CC, C, I, suN),
            ("d", wdN, wdT, wdq, CC, IC, I, C, shN),
        ]
        wp_marks = {}
        natA = []
        trnA = []

        wp_ng = {}

        def emit_wp_nat(nat, wn, n_nat, nat_w, sN, tagn):
            ns = len(nat)
            ng = wp_ng.setdefault(tagn, [None] * ns)
            for ch in range(n_nat):
                sl = ch % ns
                if ng[sl] is not None:
                    b.wait("sync", "v", ng[sl])
                b.dma("sync", nat[sl][:, 0:nat_w], wn[128 * ch:128 * (ch + 1), :],
                      sem=f"{tagn}{sl}")
                b.wait("vector", f"{tagn}{sl}")
                ng[sl] = b.op("vector", lambda e, sl=sl, ch=ch, nat_w=nat_w, sN=sN:
                              e.reduce_max(sN[:, ch:ch + 1], nat[sl][:, 0:nat_w], AX,
                                           apply_absolute_value=True), "v")

        def emit_weight_prep_a():
            emit_wp_nat(natA, wgN, IC, C, sgN, "wn")
            emit_wp_nat(natA, wuN, IC, C, suN, "wn")
            emit_wp_nat(natA, wdN, CC, I, shN, "wn")

        def emit_weight_prep_b():
            with contextlib.ExitStack() as p0:
                sb0 = lambda n, s, d: p0.enter_context(nc.sbuf_tensor(n, s, d))
                trn = trnA
                qtmp = [sb0(f"qtmp{i}", [128, I], F32) for i in range(2)]
                q8w = [sb0(f"q8w{i}", [128, I], I8) for i in range(2)]
                recb = sb0("recb", [128, I], F32)
                sccol = sb0("sccol", [128, IC], F32)
                lastw = [None, None]
                tg = [None, None]
                for (key, wn, wt, wq_out, n_nat, n_tr, nat_w, tr_w, sN) in WPREP:
                    b.wait("vector", "v")
                    b.wait("vector", "dg")
                    b.op("vector", lambda e, sN=sN, n=n_nat:
                         e.reciprocal(sccol[:, 0:n], sN[:, 0:n]), "v")
                    b.wait("vector", "v")
                    b.op("vector", lambda e, n=n_nat:
                         e.tensor_scalar_mul(sccol[:, 0:n], sccol[:, 0:n], 127.0), "v")
                    b.op("vector", lambda e, sN=sN, n=n_nat:
                         e.tensor_scalar_mul(sN[:, 0:n], sN[:, 0:n], 1.0 / 127.0), "v")
                    b.wait("gpsimd", "v")
                    for ch in range(n_nat):
                        b.dma("gpsimd", srow[0:1, 128 * ch:128 * (ch + 1)],
                              sccol[:, ch:ch + 1])
                    b.wait("gpsimd", "dg")
                    b.dma("gpsimd", recb[:, 0:128 * n_nat],
                          srow[0:1, 0:128 * n_nat].partition_broadcast(128))
                    for ch in range(n_tr):
                        sl = ch % 2
                        if tg[sl] is not None:
                            b.wait("sync", "v", tg[sl])
                        b.dma("sync", trn[sl][:, 0:tr_w], wt[128 * ch:128 * (ch + 1), :],
                              sem=f"wt{sl}")
                        b.wait("vector", f"wt{sl}")
                        b.wait("vector", "dg")
                        b.wait("vector", "s")
                        tg[sl] = b.op("vector", lambda e, sl=sl, tr_w=tr_w:
                                      e.tensor_mul(qtmp[sl][:, 0:tr_w], trn[sl][:, 0:tr_w],
                                                   recb[:, 0:tr_w]), "v")
                        if lastw[sl] is not None:
                            b.wait("scalar", lastw[sl])
                        b.wait("scalar", "v")
                        b.op("scalar", lambda e, sl=sl, tr_w=tr_w:
                             e.activation(q8w[sl][:, 0:tr_w], qtmp[sl][:, 0:tr_w], ACT.Copy), "s")
                        b.wait("gpsimd", "s")
                        b.dma("gpsimd", wq_out[128 * ch:128 * (ch + 1), :],
                              q8w[sl][:, 0:tr_w], sem=f"wq8{key}{sl}")
                        lastw[sl] = f"wq8{key}{sl}"
                    wp_marks[key] = (b.cur(f"wq8{key}0"), b.cur(f"wq8{key}1"))


        def emit_global_scale(ncols, sig_col, cm_in_ap, cm_out_ap):
            b.wait("vector", "v")
            b.op("vector", lambda e, n=ncols:
                 e.reduce_max(colsc[:], amaxA[:, 0:n], AX), "v")
            b.wait("gpsimd", "v")
            b.dma("gpsimd", rrow[0:1, 0:128], colsc[:, 0:1])
            b.wait("vector", "dg")
            b.op("vector", lambda e: e.reduce_max(sc1[0:1, 2:3], rrow[0:1, 0:128], AX), "v")
            b.wait("gpsimd", "v")
            b.dma("gpsimd", cm_in_ap[0:1, 0:1], sc1[0:1, 2:3])
            b.wait("gpsimd", "dg")
            b.op("gpsimd", lambda e, ci=cm_in_ap, co=cm_out_ap: e.collective_compute(
                "AllReduce", ALU.max, replica_groups=[list(range(NCORE))],
                ins=[ci], outs=[co]), "cc")
            b.wait("gpsimd", "cc")
            b.dma("gpsimd", sc1[0:1, 3:4], cm_out_ap[0:1, 0:1])
            b.wait("vector", "dg")
            b.op("vector", lambda e, sc=sig_col: e.tensor_mul(
                sc1[0:1, 4:5], sc1[0:1, 3:4], sigx[0:1, sc:sc + 1]), "v")
            b.wait("vector", "v")
            b.op("vector", lambda e: e.tensor_scalar_mul(sc1[0:1, 6:7], sc1[0:1, 4:5], 1.0 / 127.0), "v")
            b.wait("gpsimd", "v")
            b.dma("gpsimd", stage[0:1, 0:2], sc1[0:1, 4:6])
            b.wait("gpsimd", "dg")
            b.dma("gpsimd", stage[0:1, 2:4], sc1[0:1, 6:8])
            b.wait("gpsimd", "dg")
            b.dma("gpsimd", colsc[:], stage[0:1, 0:1].partition_broadcast(128))
            b.dma("gpsimd", scb[:], stage[0:1, 2:3].partition_broadcast(128))
            b.wait("vector", "dg")
            b.op("vector", lambda e: e.reciprocal(invb[:], colsc[:]), "v")
            b.wait("vector", "v")
            b.op("vector", lambda e: e.tensor_scalar_mul(invb[:], invb[:], 127.0), "v")
            b.wait("scalar", "v")
            b.wait("scalar", "dg")

        def emit_stats(src_emit, m_b, r_b, tmp, xb, sqb):
            ps_s = [banks.acquire(b, i) for i in range(4)]
            ps_q = [banks.acquire(b, i + 4) for i in range(4)]
            g = [None, None]
            for c in range(CC):
                sl = c % 2
                if g[sl] is not None:
                    b.wait("scalar", "t", g[sl])
                src_emit(c, sl)
                b.op("scalar", lambda e, sl=sl: e.activation(sqb[sl][:], xb[sl][:], ACT.Square), "s")
                b.wait("tensor", "s")
                for tt in range(NT):
                    b.op("tensor", lambda e, sl=sl, tt=tt, c=c: e.matmul(
                        ps_s[tt][:], onesb[:], xb[sl][:, TT * tt:TT * (tt + 1)],
                        start=(c == 0), stop=(c == CC - 1)), None)
                for tt in range(NT):
                    g[sl] = b.op("tensor", lambda e, sl=sl, tt=tt, c=c: e.matmul(
                        ps_q[tt][:], onesb[:], sqb[sl][:, TT * tt:TT * (tt + 1)],
                        start=(c == 0), stop=(c == CC - 1)), "t")
            b.wait("scalar", "t")
            for tt in range(NT):
                sl_ = slice(TT * tt, TT * (tt + 1))
                b.op("scalar", lambda e, tt=tt, sl_=sl_: e.activation(
                    m_b[:, sl_], ps_s[tt][:], ACT.Copy, scale=1.0 / C), "s")
                b.op("scalar", lambda e, tt=tt, sl_=sl_: e.activation(
                    r_b[:, sl_], ps_q[tt][:], ACT.Copy, scale=1.0 / C), "s")
            b.wait("vector", "s")
            for tt in range(NT):
                sl_ = slice(TT * tt, TT * (tt + 1))
                b.op("vector", lambda e, sl_=sl_: e.tensor_mul(tmp[:], m_b[:, sl_], m_b[:, sl_]), None)
                b.op("vector", lambda e, sl_=sl_: e.tensor_sub(r_b[:, sl_], r_b[:, sl_], tmp[:]), "v")
            b.wait("scalar", "v")
            b.op("scalar", lambda e: e.activation(r_b[:], r_b[:], ACT.Ln, bias=epsb[:]), None)
            b.op("scalar", lambda e: e.activation(r_b[:], r_b[:], ACT.Exp, scale=-0.5), "s")
            for i in range(8):
                banks.release(i, "s", b.cur("s"))
            b.wait("vector", "s")

        def emit_ln_quant(src_chunk, xq_bf, quant_cm_idx, ln_name, mid=None,
                          ln_bufs=None):
            with contextlib.ExitStack() as pl:
                sbl = lambda n, s, d: pl.enter_context(nc.sbuf_tensor(n, s, d))
                if ln_bufs is not None:
                    ln1 = ln_bufs
                else:
                    ln1 = [sbl(f"{ln_name}_ln{c}", [128, TOK], F32) for c in range(CC)]
                q8 = [sbl(f"{ln_name}_q8{i}", [128, TOK], I8) for i in range(2)]
                with contextlib.ExitStack() as pxs:
                    sbx = lambda n, s, d: pxs.enter_context(nc.sbuf_tensor(n, s, d))
                    xb = [sbx(f"{ln_name}_xb{i}", [128, TOK], BF16) for i in range(2)]
                    sqb = [sbx(f"{ln_name}_sq{i}", [128, TOK], BF16) for i in range(2)]
                    m_b = sbx(f"{ln_name}_m", [128, TOK], F32)
                    r_b = sbx(f"{ln_name}_r", [128, TOK], F32)
                    tmp = sbx(f"{ln_name}_t", [128, TT], F32)

                    def src_stats(c, sl):
                        apc = src_chunk(c, sl, ("scalar",))
                        b.op("scalar", lambda e, sl=sl, apc=apc: e.activation(
                            xb[sl][:], apc, ACT.Copy), None)

                    emit_stats(src_stats, m_b, r_b, tmp, xb, sqb)
                    apcs = [None] * CC
                    apcs[0] = src_chunk(0, 0, ())
                    for c in range(CC):
                        if c + 1 < CC:
                            apcs[c + 1] = src_chunk(c + 1, (c + 1) % 2, ())
                        b.wait("vector", f"xs{c % 2}") if src_chunk is not None else None
                        apc = apcs[c]
                        b.op("vector", lambda e, c=c, apc=apc: e.tensor_sub(ln1[c][:], apc, m_b[:]), None)
                        b.op("vector", lambda e, c=c: e.tensor_mul(ln1[c][:], ln1[c][:], r_b[:]), None)
                        b.op("vector", lambda e, c=c: e.reduce_max(
                            amaxA[:, c:c + 1], ln1[c][:], AX, apply_absolute_value=True), "v")
                    dmb = dump_tensor(ln_name + "_mb", [128, TOK])
                    if dmb is not None:
                        b.wait("sync", "v")
                        b.dma("sync", dmb, m_b[:])
                        b.barrier()
                    drb = dump_tensor(ln_name + "_rb", [128, TOK])
                    if drb is not None:
                        b.wait("sync", "v")
                        b.dma("sync", drb, r_b[:])
                        b.barrier()
                dam = dump_tensor(ln_name + "_amax", [128, 32])
                if dam is not None:
                    b.wait("sync", "v")
                    b.dma("sync", dam, amaxA[:])
                    b.barrier()
                dln = dump_tensor(ln_name, [C, TOK])
                if dln is not None:
                    b.wait("sync", "v")
                    for c in range(CC):
                        b.dma("sync", dln[128 * c:128 * (c + 1), :], ln1[c][:])
                if mid is not None:
                    mid()
                emit_global_scale(CC, 0, cm_i[quant_cm_idx], cm_o[quant_cm_idx])
                qgv = [None, None]
                for c in range(CC):
                    sl = c % 2
                    if qgv[sl] is not None:
                        b.wait("scalar", f"xqc{sl}")
                    b.op("scalar", lambda e, c=c, sl=sl: e.activation(
                        q8[sl][:], ln1[c][:], ACT.Copy, scale=invb[:]), "s")
                    b.wait("gpsimd", "s")
                    qgv[sl] = b.dma("gpsimd", xq_bf[c][:], q8[sl][:], sem=f"xqc{sl}")
                b.barrier()

        def emit_ffn_gate_up(xq_bf, tag, dbufw=True, wait_marks=None):
            b.op("vector", lambda e: e.tensor_scalar(sgC[:], sgN[:, 0:IC], scb[:], None, ALU.mult), None)
            b.op("vector", lambda e: e.tensor_scalar(suC[:], suN[:, 0:IC], scb[:], None, ALU.mult), "v")
            with contextlib.ExitStack() as pf:
                sbf = lambda n, s, d: pf.enter_context(nc.sbuf_tensor(n, s, d))
                HW = I // 2
                nws = 2 if dbufw else 1
                wgw_ = [[sbf(f"{tag}gw{h}_{c}", [128, HW], BF16) for c in range(CC)]
                        for h in range(nws)]
                wuw_ = [[sbf(f"{tag}uw{h}_{c}", [128, HW], BF16) for c in range(CC)]
                        for h in range(nws)]
                wgw = [wgw_[h % nws] for h in range(2)]
                wuw = [wuw_[h % nws] for h in range(2)]
                silu = [sbf(f"{tag}sil{i}", [128, TOK], F32) for i in range(2)]
                tb = [sbf(f"{tag}tb{i}", [128, TOK], F32) for i in range(2)]
                tspg = [None, None]

                def load_g(h):
                    io = HW * h
                    for c in range(CC):
                        b.dma("gpsimd", wgw[h][c][:], wgq[128 * c:128 * (c + 1), io:io + HW],
                              sem=f"wg{h}")

                def load_u(h):
                    io = HW * h
                    for c in range(CC):
                        b.dma("gpsimd", wuw[h][c][:], wuq[128 * c:128 * (c + 1), io:io + HW],
                              sem=f"wu{h}")

                if wait_marks is not None:
                    b.wait("gpsimd", "wq8g0", wait_marks["g"][0])
                    b.wait("gpsimd", "wq8g1", wait_marks["g"][1])
                load_g(0)
                if dbufw:
                    load_g(1)
                if wait_marks is not None:
                    b.wait("gpsimd", "wq8u0", wait_marks["u"][0])
                    b.wait("gpsimd", "wq8u1", wait_marks["u"][1])
                load_u(0)
                if dbufw:
                    load_u(1)
                b.wait("tensor", "xqc0")
                b.wait("tensor", "xqc1")
                for half in range(2):
                    if not dbufw and half == 1:
                        b.wait("gpsimd", "t")
                        load_g(1)
                        load_u(1)
                    b.wait("tensor", f"wg{half}")
                    for ici in range(IC // 2):
                        ic = half * (IC // 2) + ici
                        sl = ic % 2
                        for i in range(4):
                            banks.acquire(b, i)
                        for c in range(CC):
                            for tt in range(NT):
                                gmm = b.op("tensor", lambda e, c=c, tt=tt, ici=ici, half=half: e.matmul(
                                    banks.t[tt][:],
                                    wgw[half][c][:, 128 * ici:128 * (ici + 1)],
                                    xq_bf[c][:, TT * tt:TT * (tt + 1)],
                                    start=(c == 0), stop=(c == CC - 1)),
                                    "t" if (c == CC - 1 and tt == NT - 1) else None)
                        b.wait("scalar", "t", gmm)
                        if tspg[sl] is not None:
                            b.wait("scalar", f"tsp{sl}")
                        for tt in range(NT):
                            sv = b.op("scalar", lambda e, tt=tt, ic=ic, sl=sl: e.activation(
                                silu[sl][:, TT * tt:TT * (tt + 1)], banks.t[tt][:],
                                ACT.Silu, scale=sgC[:, ic:ic + 1]), "s")
                        for i in range(4):
                            banks.release(i, "s", sv)
                        for i in range(4, 8):
                            banks.acquire(b, i)
                        if ici == 0:
                            b.wait("tensor", f"wu{half}")
                        for c in range(CC):
                            for tt in range(NT):
                                umm = b.op("tensor", lambda e, c=c, tt=tt, ici=ici, half=half: e.matmul(
                                    banks.t[tt + 4][:],
                                    wuw[half][c][:, 128 * ici:128 * (ici + 1)],
                                    xq_bf[c][:, TT * tt:TT * (tt + 1)],
                                    start=(c == 0), stop=(c == CC - 1)),
                                    "t" if (c == CC - 1 and tt == NT - 1) else None)
                        b.wait("vector", "t", umm)
                        b.wait("vector", "s", sv)
                        if tspg[sl] is not None:
                            b.wait("vector", f"tsp{sl}")
                        for tt in range(NT):
                            uv = b.op("vector", lambda e, tt=tt, ic=ic, sl=sl: e.scalar_tensor_tensor(
                                tb[sl][:, TT * tt:TT * (tt + 1)], banks.t[tt + 4][:],
                                suC[:, ic:ic + 1], silu[sl][:, TT * tt:TT * (tt + 1)],
                                ALU.mult, ALU.mult), "v")
                        for i in range(4, 8):
                            banks.release(i, "v", uv)
                        b.op("vector", lambda e, ic=ic, sl=sl: e.reduce_max(
                            amaxA[:, ic:ic + 1], tb[sl][:], AX,
                            apply_absolute_value=True), "v")
                        b.wait("sync", "v")
                        tspg[sl] = b.dma("sync", t_sp[128 * ic:128 * (ic + 1), :], tb[sl][:],
                                         sem=f"tsp{sl}")
                b.barrier()

        def emit_ffn_down(resid_src, store_out, tag, wait_marks=None):
            b.op("vector", lambda e: e.tensor_scalar(
                sdC[:], shN[:, 0:CC], scb[:], 0.5, ALU.mult, ALU.mult), "v")
            if wait_marks is not None:
                b.wait("gpsimd", "wq8d0", wait_marks["d"][0])
                b.wait("gpsimd", "wq8d1", wait_marks["d"][1])
            with contextlib.ExitStack() as pd:
                sbd = lambda n, s, d: pd.enter_context(nc.sbuf_tensor(n, s, d))
                qhw = [sbd(f"{tag}qh{ic}", [128, 1024], BF16) for ic in range(IC)]
                wdw = [sbd(f"{tag}dw{ic}", [128, 1024], BF16) for ic in range(IC)]
                tin = [sbd(f"{tag}ti{i}", [128, 1024], F32) for i in range(4)]
                q8h = [sbd(f"{tag}q8h{i}", [128, 1024], I8) for i in range(4)]
                for half in range(2):
                    to = 1024 * half
                    b.wait("gpsimd", "t")
                    qge = [None, None, None, None]
                    qcv = []
                    for ic in range(IC):
                        sl = ic % 4
                        if half == 0:
                            b.dma("gpsimd", wdw[ic][:], wdq[128 * ic:128 * (ic + 1), :],
                                  sem="wdw")
                        if qge[sl] is not None:
                            b.wait("sync", f"qc{sl}")
                        b.dma("sync", tin[sl][:], t_sp[128 * ic:128 * (ic + 1), to:to + 1024],
                              sem=f"ti{sl}")
                        b.wait("scalar", f"ti{sl}")
                        if qge[sl] is not None:
                            b.wait("scalar", f"qc{sl}")
                        b.op("scalar", lambda e, sl=sl: e.activation(
                            q8h[sl][:], tin[sl][:], ACT.Copy, scale=invb[:]), "s")
                        b.wait("gpsimd", "s")
                        qge[sl] = b.dma("gpsimd", qhw[ic][:], q8h[sl][:], sem=f"qc{sl}")
                        qcv.append((sl, qge[sl]))
                    b.wait("tensor", "wdw")
                    stg = [None, None]
                    for g in range(2):
                        hcs = list(range(4 * g, 4 * g + 4))
                        for i in range(8):
                            banks.acquire(b, i)
                        dmmv = {}
                        for ic in range(IC):
                            if g == 0:
                                b.wait("tensor", f"qc{qcv[ic][0]}", qcv[ic][1])
                            for j, hc in enumerate(hcs):
                                for t2 in range(2):
                                    dmmv[hc] = b.op("tensor", lambda e, ic=ic, t2=t2, hc=hc, j=j: e.matmul(
                                        banks.t[2 * j + t2][:],
                                        wdw[ic][:, 128 * hc:128 * (hc + 1)],
                                        qhw[ic][:, TT * t2:TT * (t2 + 1)],
                                        start=(ic == 0), stop=(ic == IC - 1)),
                                        "t" if (ic == IC - 1 and t2 == 1) else None)
                        for j, hc in enumerate(hcs):
                            sl = hc % 2
                            src = resid_src(hc, half, sl)
                            b.wait("vector", "t", dmmv[hc])
                            for t2 in range(2):
                                vv = b.op("vector", lambda e, t2=t2, hc=hc, j=j, src=src, to=to:
                                          e.scalar_tensor_tensor(
                                              XR[hc][:, to + TT * t2:to + TT * (t2 + 1)],
                                              banks.t[2 * j + t2][:], sdC[:, hc:hc + 1],
                                              src[:, TT * t2:TT * (t2 + 1)],
                                              ALU.mult, ALU.add), "v")
                            for t2 in range(2):
                                banks.release(2 * j + t2, "v", vv)
                        if store_out:
                            b.wait("sync", "v")
                            for hc in hcs:
                                stg[hc % 2] = b.dma(
                                    "sync", oxT[128 * hc:128 * (hc + 1), to:to + 1024],
                                    XR[hc][:, to:to + 1024])
                b.barrier()

        with contextlib.ExitStack() as pA:
            sbA = lambda n, s, d: pA.enter_context(nc.sbuf_tensor(n, s, d))
            xq_bf = [sbA(f"xqb{c}", [128, TOK], BF16) for c in range(CC)]

            with contextlib.ExitStack() as pXS:
                sbX = lambda n, s, d: pXS.enter_context(nc.sbuf_tensor(n, s, d))
                ln1P = [sbX(f"ln1buf{c}", [128, TOK], F32) for c in range(CC)]
                natA.extend(sbX(f"natA{i}", [128, I], F32) for i in range(2))
                trnA.extend(sbX(f"trnA{i}", [128, I], F32) for i in range(2))
                for c in range(CC):
                    b.dma("sync", ln1P[c][:], xT[128 * c:128 * (c + 1), :], sem=f"xl{c}")

                def p1_chunk(c, sl, engs):
                    for e_ in engs:
                        b.wait(e_, f"xl{c}")
                    return ln1P[c][:]

                emit_weight_prep_a()
                emit_ln_quant(p1_chunk, xq_bf, 0, "ln1", mid=emit_weight_prep_b,
                              ln_bufs=ln1P)
            dsn = dump_tensor("sgN", [128, IC])
            if dsn is not None:
                b.wait("sync", "v")
                b.dma("sync", dsn, sgN[:])
                dsu = dump_tensor("suN", [128, IC])
                b.dma("sync", dsu, suN[:])
                b.barrier()
            dwq = dump_tensor("wgq", [C, I], I8)
            if dwq is not None:
                for s_ in ("wq8g0", "wq8g1", "wq8u0", "wq8u1", "wq8d0", "wq8d1"):
                    b.wait("sync", s_)
                b.dma("sync", dwq, wgq)
                duq = dump_tensor("wuq", [C, I], I8)
                b.dma("sync", duq, wuq)
                b.barrier()
            dsc = dump_tensor("sc1", [1, 8])
            if dsc is not None:
                b.wait("sync", "v")
                b.wait("sync", "dg")
                b.dma("sync", dsc, sc1[:])
                b.barrier()
            dinv = dump_tensor("invb", [128, 2])
            if dinv is not None:
                b.wait("sync", "v")
                b.dma("sync", dinv[:, 0:1], invb[:], allow_slow_non_contiguous=True)
                b.dma("sync", dinv[:, 1:2], scb[:], allow_slow_non_contiguous=True)
                b.barrier()
            dxq = dump_tensor("xq", [C, TOK], BF16)
            if dxq is not None:
                for c in range(CC):
                    b.dma("sync", dxq[128 * c:128 * (c + 1), :], xq_bf[c][:])
                b.barrier()
            if _STOP_AFTER != "p1":
                emit_ffn_gate_up(xq_bf, "f1", wait_marks=wp_marks)
                dsg = dump_tensor("sgC", [128, 2 * IC])
                if dsg is not None:
                    b.wait("sync", "v")
                    b.dma("sync", dsg[:, 0:IC], sgC[:])
                    b.dma("sync", dsg[:, IC:2 * IC], suC[:])
                    b.barrier()
                dt1 = dump_tensor("t1", [I, TOK])
                if dt1 is not None:
                    b.wait("sync", "tsp0")
                    b.wait("sync", "tsp1")
                    b.dma("sync", dt1, t_sp)
                    b.barrier()
                emit_global_scale(IC, 1, cm_i[1], cm_o[1])

        if _STOP_AFTER in ("p1", "p2"):
            b.barrier()
            return _emit(nc, es, b)

        XR.extend(sbuf(f"xr{c}", [128, TOK], F32) for c in range(CC))
        with contextlib.ExitStack() as p3s:
            xs3 = [p3s.enter_context(nc.sbuf_tensor(f"xs3{i}", [128, 1024], F32))
                   for i in range(2)]

            def p3_src(hc, half, sl):
                b.wait("sync", "v")
                b.dma("sync", xs3[sl][:],
                      xT[128 * hc:128 * (hc + 1), 1024 * half:1024 * (half + 1)])
                b.wait("vector", "ds")
                return xs3[sl][:]

            emit_ffn_down(p3_src, False, "f1", wait_marks=wp_marks)
        dx1 = dump_tensor("x1", [C, TOK])
        if dx1 is not None:
            for c in range(CC):
                b.dma("sync", dx1[128 * c:128 * (c + 1), :], XR[c][:])
            b.barrier()
        if _STOP_AFTER == "p3":
            b.barrier()
            return _emit(nc, es, b)

        m2s_dbg = []
        with contextlib.ExitStack() as pT:
            sbT = lambda n, s, d: pT.enter_context(nc.sbuf_tensor(n, s, d))
            h_bf = [sbT(f"hbf{c}", [128, TOK], BF16) for c in range(CC)]
            V_bf = [sbT(f"vbf{c}", [128, TOK], BF16) for c in range(CC)]
            wsl = [sbT(f"wsl{i}", [128, C], BF16) for i in range(2)]

            with contextlib.ExitStack() as sA:
                sba = lambda n, s, d: sA.enter_context(nc.sbuf_tensor(n, s, d))
                xb = [sba(f"axb{i}", [128, TOK], BF16) for i in range(2)]
                sqb = [sba(f"asq{i}", [128, TOK], BF16) for i in range(2)]
                m2 = sba("m2", [128, TOK], F32)
                r2 = sba("r2", [128, TOK], F32)
                m2s_dbg.extend([m2, r2])
                tmp = sba("atmp", [128, TT], F32)
                xaf = sba("xaf", [128, TOK], F32)
                xaw = [sba(f"xaw{i}", [128, TOK], BF16) for i in range(2)]

                def srcA(c, sl):
                    b.op("scalar", lambda e, sl=sl, c=c: e.activation(
                        xb[sl][:], XR[c][:], ACT.Copy), None)

                emit_stats(srcA, m2, r2, tmp, xb, sqb)
                xag = [None, None]
                for c in range(CC):
                    sl = c % 2
                    b.op("vector", lambda e, c=c: e.tensor_sub(xaf[:], XR[c][:], m2[:]), None)
                    if xag[sl] is not None:
                        b.wait("vector", f"xa{sl}")
                    b.op("vector", lambda e, sl=sl: e.tensor_mul(xaw[sl][:], xaf[:], r2[:]), "v")
                    b.wait("sync", "v")
                    xag[sl] = b.dma("sync", xa_sp[128 * c:128 * (c + 1), :], xaw[sl][:],
                                    sem=f"xa{sl}")
                b.barrier()

            dm2 = dump_tensor("m2r2", [256, TOK])
            if dm2 is not None:
                b.dma("sync", dm2[0:128, :], m2s_dbg[0][:])
                b.dma("sync", dm2[128:256, :], m2s_dbg[1][:])
                b.barrier()
            with contextlib.ExitStack() as sB:
                sbb = lambda n, s, d: sB.enter_context(nc.sbuf_tensor(n, s, d))
                xb = [sbb(f"bxb{i}", [128, TOK], BF16) for i in range(2)]
                sqb = [sbb(f"bsq{i}", [128, TOK], BF16) for i in range(2)]
                m3 = sbb("m3", [128, TOK], F32)
                r3 = sbb("r3", [128, TOK], F32)
                tmp = sbb("btmp", [128, TT], F32)
                uf = sbb("uf", [128, TOK], F32)

                def srcB(c, sl):
                    b.wait("sync", "s")
                    b.dma("sync", xb[sl][:], xa_sp[128 * c:128 * (c + 1), :])
                    b.wait("scalar", "ds")

                ps_s = [banks.acquire(b, i) for i in range(4)]
                ps_q = [banks.acquire(b, i + 4) for i in range(4)]
                g = [None, None]
                for c in range(CC):
                    sl = c % 2
                    if g[sl] is not None:
                        b.wait("sync", "t", g[sl])
                    srcB(c, sl)
                    b.op("scalar", lambda e, sl=sl: e.activation(sqb[sl][:], xb[sl][:], ACT.Square), "s")
                    b.wait("tensor", "s")
                    for tt in range(NT):
                        b.op("tensor", lambda e, sl=sl, tt=tt, c=c: e.matmul(
                            ps_s[tt][:], onesb[:], xb[sl][:, TT * tt:TT * (tt + 1)],
                            start=(c == 0), stop=(c == CC - 1)), None)
                    for tt in range(NT):
                        g[sl] = b.op("tensor", lambda e, sl=sl, tt=tt, c=c: e.matmul(
                            ps_q[tt][:], onesb[:], sqb[sl][:, TT * tt:TT * (tt + 1)],
                            start=(c == 0), stop=(c == CC - 1)), "t")
                b.wait("scalar", "t")
                for tt in range(NT):
                    sl_ = slice(TT * tt, TT * (tt + 1))
                    b.op("scalar", lambda e, tt=tt, sl_=sl_: e.activation(
                        m3[:, sl_], ps_s[tt][:], ACT.Copy, scale=1.0 / C), "s")
                    b.op("scalar", lambda e, tt=tt, sl_=sl_: e.activation(
                        r3[:, sl_], ps_q[tt][:], ACT.Copy, scale=1.0 / C), "s")
                b.wait("vector", "s")
                for tt in range(NT):
                    sl_ = slice(TT * tt, TT * (tt + 1))
                    b.op("vector", lambda e, sl_=sl_: e.tensor_mul(tmp[:], m3[:, sl_], m3[:, sl_]), None)
                    b.op("vector", lambda e, sl_=sl_: e.tensor_sub(r3[:, sl_], r3[:, sl_], tmp[:]), "v")
                b.wait("scalar", "v")
                b.op("scalar", lambda e: e.activation(r3[:], r3[:], ACT.Ln, bias=epsb[:]), None)
                b.op("scalar", lambda e: e.activation(r3[:], r3[:], ACT.Exp, scale=-0.5), "s")
                for i in range(8):
                    banks.release(i, "s", b.cur("s"))
                b.wait("vector", "s")
                for c in range(CC):
                    sl = c % 2
                    b.wait("sync", "v")
                    b.wait("sync", "s")
                    b.dma("sync", xb[sl][:], xa_sp[128 * c:128 * (c + 1), :])
                    b.wait("vector", "ds")
                    b.op("vector", lambda e, sl=sl: e.tensor_sub(uf[:], xb[sl][:], m3[:]), None)
                    b.op("vector", lambda e: e.tensor_mul(uf[:], uf[:], r3[:]), "v")
                    b.wait("scalar", "v")
                    b.op("scalar", lambda e, c=c: e.activation(
                        h_bf[c][:], uf[:], ACT.Identity,
                        scale=gT[:, c:c + 1], bias=bT[:, c:c + 1]), "s")
                b.barrier()
            dxa = dump_tensor("xa", [C, TOK], BF16)
            if dxa is not None:
                b.wait("sync", "xa0")
                b.wait("sync", "xa1")
                b.dma("sync", dxa, xa_sp)
                b.barrier()
            dh = dump_tensor("h", [C, TOK], BF16)
            if dh is not None:
                for c in range(CC):
                    b.dma("sync", dh[128 * c:128 * (c + 1), :], h_bf[c][:])
                b.barrier()

            with contextlib.ExitStack() as sC:
                sbc = lambda n, s, d: sC.enter_context(nc.sbuf_tensor(n, s, d))
                K_bf = [sbc(f"kbf{c}", [128, TOK], BF16) for c in range(CC)]
                KA = sbc("KA", [128, TOK], BF16)
                onesf = sbc("onesf", [128, 2], F32)
                bsl = [sbc(f"bsl{i}", [128, 128], BF16) for i in range(2)]
                errt = sbc("errt", [128, TOK], F32)
                grdt = sbc("grdt", [128, TOK], F32)
                shft = sbc("shft", [128, TOK], F32)
                b.op("vector", lambda e: e.memset(onesf[:], 1.0), "v")

                slab = Slab(wsl, "ws")
                flat = [(wk_s[dc], K_bf, dc) for dc in range(CC)] + \
                       [(wv_s[dc], V_bf, dc) for dc in range(CC)] + [(a_s, None, 0)]
                slab.load(b, 0, flat[0][0])
                for it, (wap, dst, dc) in enumerate(flat):
                    if it + 1 < len(flat):
                        slab.load(b, it + 1, flat[it + 1][0])
                    w = slab.use(b, it)
                    rhs = h_bf if dst is not None else K_bf
                    if dst is None:
                        b.wait("tensor", "s")
                    for i in range(4):
                        banks.acquire(b, i)
                    for c in range(CC):
                        for tt in range(NT):
                            mm = b.op("tensor", lambda e, c=c, tt=tt, w=w, rhs=rhs: e.matmul(
                                banks.t[tt][:], w[:, 128 * c:128 * (c + 1)],
                                rhs[c][:, TT * tt:TT * (tt + 1)],
                                start=(c == 0), stop=(c == CC - 1)),
                                "t" if (c == CC - 1 and tt == NT - 1) else None)
                    slab.done(it, mm)
                    b.wait("scalar", "t", mm)
                    out_t = (dst[dc] if dst is not None else KA)
                    for tt in range(NT):
                        sv = b.op("scalar", lambda e, tt=tt, out_t=out_t: e.activation(
                            out_t[:, TT * tt:TT * (tt + 1)], banks.t[tt][:], ACT.Copy), "s")
                    for i in range(4):
                        banks.release(i, "s", sv)

                bslab = Slab(bsl, "bs")
                slab.load(b, 17, w0_s[0])
                bslab.load(b, 0, b_s[0])
                eg = None
                sg_ = None
                for dc in range(CC):
                    if dc + 1 < CC:
                        slab.load(b, 17 + dc + 1, w0_s[dc + 1])
                        bslab.load(b, dc + 1, b_s[dc + 1])
                    w = slab.use(b, 17 + dc)
                    bw = bslab.use(b, dc)
                    for i in range(4):
                        banks.acquire(b, i)
                    for c in range(CC):
                        for tt in range(NT):
                            b.op("tensor", lambda e, c=c, tt=tt, w=w: e.matmul(
                                banks.t[tt][:], w[:, 128 * c:128 * (c + 1)],
                                K_bf[c][:, TT * tt:TT * (tt + 1)],
                                start=(c == 0), stop=False), None)
                    for tt in range(NT):
                        mm = b.op("tensor", lambda e, tt=tt, bw=bw: e.matmul(
                            banks.t[tt][:], bw[:], KA[:, TT * tt:TT * (tt + 1)],
                            start=False, stop=True),
                            "t" if tt == NT - 1 else None)
                    slab.done(17 + dc, mm)
                    bslab.done(dc, mm)
                    b.wait("vector", "t", mm)
                    if eg is not None:
                        b.wait("vector", "es")
                    for tt in range(NT):
                        vv = b.op("vector", lambda e, tt=tt, dc=dc: e.tensor_sub(
                            errt[:, TT * tt:TT * (tt + 1)], banks.t[tt][:],
                            V_bf[dc][:, TT * tt:TT * (tt + 1)]), "v")
                    for i in range(4):
                        banks.release(i, "v", vv)
                    b.op("vector", lambda e, dc=dc: e.tensor_mul(
                        grdt[:], K_bf[dc][:], errt[:]), None)
                    b.op("vector", lambda e: e.memset(shft[:, 0:1], 0.0), None)
                    b.op("vector", lambda e: e.tensor_tensor_scan(
                        shft[:, 1:TOK], onesf[:, 0:1].broadcast_to([128, TOK - 1]),
                        grdt[:, 0:TOK - 1], 0.0, ALU.mult, ALU.add), "v")
                    b.wait("vector", "v")
                    b.op("vector", lambda e, dc=dc: e.tensor_add(
                        carry[:, dc:dc + 1], shft[:, TOK - 1:TOK], grdt[:, TOK - 1:TOK]), "v")
                    b.wait("sync", "v")
                    eg = b.dma("sync", err_sp[128 * dc:128 * (dc + 1), :], errt[:], sem="es")
                    sg_ = b.dma("sync", sh_sp[128 * dc:128 * (dc + 1), :], shft[:], sem="es")
                b.barrier()

            b.op("vector", lambda e: e.tensor_scalar(carry[:], carry[:], ieb[:], None, ALU.mult), "v")
            b.wait("gpsimd", "v")
            b.dma("gpsimd", cy_i, carry[:])
            b.wait("gpsimd", "dg")
            b.op("gpsimd", lambda e: e.collective_compute(
                "AllReduce", ALU.add, replica_groups=[[0, 1], [2, 3], [4, 5], [6, 7]],
                ins=[cy_i], outs=[cy_o]), "cc")
            b.wait("gpsimd", "cc")
            b.dma("gpsimd", carry[:], cy_o)
            b.wait("vector", "dg")
            b.op("vector", lambda e: e.tensor_scalar(carry[:], carry[:], iob[:], None, ALU.mult), "v")
            b.barrier()

            with contextlib.ExitStack() as sF:
                sbff = lambda n, s, d: sF.enter_context(nc.sbuf_tensor(n, s, d))
                psb = sbff("psb", [128, TOK], F32)
                qlr = sbff("qlr", [128, TOK], F32)
                c1t = sbff("c1t", [128, TOK], F32)
                errF = [sbff(f"errF{i}", [128, TOK], F32) for i in range(2)]
                shF = [sbff(f"shF{i}", [128, TOK], F32) for i in range(2)]
                opb = [sbff(f"opb{i}", [128, TOK], BF16) for i in range(2)]
                b.dma("gpsimd", psb[:], posr.partition_broadcast(128))
                og = [None, None]
                cons = [None, None]
                sld = [None, None]
                slab = Slab(wsl, "ws")
                slab.load(b, 0, wq_s[0])

                def ld_es(dc):
                    sl = dc % 2
                    if cons[sl] is not None:
                        b.wait("sync", "v", cons[sl])
                    b.dma("sync", errF[sl][:], err_sp[128 * dc:128 * (dc + 1), :],
                          sem=f"ef{sl}")
                    sld[sl] = b.dma("sync", shF[sl][:], sh_sp[128 * dc:128 * (dc + 1), :],
                                    sem=f"ef{sl}")

                ld_es(0)
                for dc in range(CC):
                    sl = dc % 2
                    if dc + 1 < CC:
                        slab.load(b, dc + 1, wq_s[dc + 1])
                    w = slab.use(b, dc)
                    myld = sld[sl]
                    for i in range(4):
                        banks.acquire(b, i)
                    for c in range(CC):
                        for tt in range(NT):
                            mm = b.op("tensor", lambda e, c=c, tt=tt, w=w: e.matmul(
                                banks.t[tt][:], w[:, 128 * c:128 * (c + 1)],
                                h_bf[c][:, TT * tt:TT * (tt + 1)],
                                start=(c == 0), stop=(c == CC - 1)),
                                "t" if (c == CC - 1 and tt == NT - 1) else None)
                    slab.done(dc, mm)
                    b.wait("vector", "t", mm)
                    b.wait("vector", f"ef{sl}")
                    b.wait("vector", "dg")
                    for tt in range(NT):
                        vv = b.op("vector", lambda e, tt=tt, dc=dc: e.scalar_tensor_tensor(
                            qlr[:, TT * tt:TT * (tt + 1)], banks.t[tt][:],
                            lrT[:, dc:dc + 1], psb[:, TT * tt:TT * (tt + 1)],
                            ALU.mult, ALU.mult), "v")
                    for i in range(4):
                        banks.release(i, "v", vv)
                    b.op("vector", lambda e, dc=dc, sl=sl: e.scalar_tensor_tensor(
                        c1t[:], shF[sl][:], carry[:, dc:dc + 1], qlr[:],
                        ALU.add, ALU.mult), None)
                    cv = b.op("vector", lambda e, sl=sl: e.tensor_sub(qlr[:], errF[sl][:], c1t[:]), "v")
                    cons[sl] = cv
                    if dc + 1 < CC:
                        ld_es(dc + 1)
                    if og[sl] is not None:
                        b.wait("vector", f"os{sl}")
                    b.op("vector", lambda e, dc=dc, sl=sl: e.tensor_add(
                        opb[sl][:], qlr[:], V_bf[dc][:]), "v")
                    b.wait("sync", "v")
                    og[sl] = b.dma("sync", op_sp[128 * dc:128 * (dc + 1), :], opb[sl][:],
                                   sem=f"os{sl}")
                b.barrier()

            with contextlib.ExitStack() as sG:
                sbg = lambda n, s, d: sG.enter_context(nc.sbuf_tensor(n, s, d))
                xaG = [sbg(f"xag{c}", [128, TOK], BF16) for c in range(CC)]
                gtb = [sbg(f"gtb{i}", [128, TOK], BF16) for i in range(2)]
                opG = [sbg(f"opG{i}", [128, TOK], BF16) for i in range(2)]
                x1t = sbg("x1t", [128, TOK], F32)
                og_bf = h_bf
                for c in range(CC):
                    b.dma("sync", xaG[c][:], xa_sp[128 * c:128 * (c + 1), :])
                b.wait("tensor", "ds")
                slab = Slab(wsl, "ws")
                slab.load(b, 0, wg_s[0])
                for dc in range(CC):
                    sl = dc % 2
                    if dc + 1 < CC:
                        slab.load(b, dc + 1, wg_s[dc + 1])
                    w = slab.use(b, dc)
                    b.wait("sync", "v")
                    opg = b.dma("sync", opG[sl][:], op_sp[128 * dc:128 * (dc + 1), :],
                                sem=f"op{sl}")
                    for i in range(4):
                        banks.acquire(b, i)
                    for c in range(CC):
                        for tt in range(NT):
                            mm = b.op("tensor", lambda e, c=c, tt=tt, w=w: e.matmul(
                                banks.t[tt][:], w[:, 128 * c:128 * (c + 1)],
                                xaG[c][:, TT * tt:TT * (tt + 1)],
                                start=(c == 0), stop=(c == CC - 1)),
                                "t" if (c == CC - 1 and tt == NT - 1) else None)
                    slab.done(dc, mm)
                    b.wait("scalar", "t", mm)
                    b.wait("scalar", "v")
                    for tt in range(NT):
                        sv = b.op("scalar", lambda e, tt=tt, sl=sl: e.activation(
                            gtb[sl][:, TT * tt:TT * (tt + 1)], banks.t[tt][:], ACT.Sigmoid), "s")
                    for i in range(4):
                        banks.release(i, "s", sv)
                    b.wait("vector", "s", sv)
                    b.wait("vector", f"op{sl}")
                    b.op("vector", lambda e, dc=dc, sl=sl: e.tensor_mul(
                        og_bf[dc][:], opG[sl][:], gtb[sl][:]), "v")
                b.barrier()

                slab2 = Slab(wsl, "ws")
                slab2.load(b, 0, wo_s[0])
                for hc in range(CC):
                    sl = hc % 2
                    if hc + 1 < CC:
                        slab2.load(b, hc + 1, wo_s[hc + 1])
                    w = slab2.use(b, hc)
                    for i in range(4):
                        banks.acquire(b, i)
                    for dc in range(CC):
                        for tt in range(NT):
                            mm = b.op("tensor", lambda e, dc=dc, tt=tt, w=w: e.matmul(
                                banks.t[tt][:], w[:, 128 * dc:128 * (dc + 1)],
                                og_bf[dc][:, TT * tt:TT * (tt + 1)],
                                start=(dc == 0), stop=(dc == CC - 1)),
                                "t" if (dc == CC - 1 and tt == NT - 1) else None)
                    slab2.done(hc, mm)
                    b.op("vector", lambda e, hc=hc: e.tensor_add(
                        x1t[:], XR[hc][:], xaG[hc][:]), None)
                    b.wait("vector", "t", mm)
                    for tt in range(NT):
                        vv = b.op("vector", lambda e, tt=tt, hc=hc: e.scalar_tensor_tensor(
                            XR[hc][:, TT * tt:TT * (tt + 1)], banks.t[tt][:],
                            lsT[:, hc:hc + 1], x1t[:, TT * tt:TT * (tt + 1)],
                            ALU.mult, ALU.add), "v")
                    for i in range(4):
                        banks.release(i, "v", vv)
                b.barrier()

        dx2 = dump_tensor("x2", [C, TOK])
        if dx2 is not None:
            for c in range(CC):
                b.dma("sync", dx2[128 * c:128 * (c + 1), :], XR[c][:])
            b.barrier()
        if _STOP_AFTER == "p4":
            b.barrier()
            return _emit(nc, es, b)

        with contextlib.ExitStack() as pB:
            sbB = lambda n, s, d: pB.enter_context(nc.sbuf_tensor(n, s, d))
            xq_bf = [sbB(f"xqc{c}", [128, TOK], BF16) for c in range(CC)]

            def p5_chunk(c, sl, engs):
                return XR[c][:]

            emit_ln_quant(p5_chunk, xq_bf, 2, "ln3")
            emit_ffn_gate_up(xq_bf, "f2", dbufw=False)
            emit_global_scale(IC, 1, cm_i[3], cm_o[3])

            def p5_src(hc, half, sl):
                return XR[hc][:, 1024 * half:1024 * (half + 1)]

            emit_ffn_down(p5_src, True, "f2")

        b.barrier()
        return _emit(nc, es, b)


def _emit(nc, es, b):
    block = es.enter_context(nc.Block())

    @block.sync
    def _(e):
        for fn in b.q["sync"]:
            fn(e)

    @block.gpsimd
    def _(e):
        for fn in b.q["gpsimd"]:
            fn(e)

    @block.vector
    def _(e):
        for fn in b.q["vector"]:
            fn(e)

    @block.scalar
    def _(e):
        for fn in b.q["scalar"]:
            fn(e)

    @block.tensor
    def _(e):
        for fn in b.q["tensor"]:
            fn(e)

    return nc


def kernel(**inputs):
    f4 = np.float32
    x = np.asarray(inputs["L0_attn_x"], f4).reshape(NCORE * TOK, C)
    Wq = np.asarray(inputs["L0_attn_Wq"], f4)
    Wk = np.asarray(inputs["L0_attn_Wk"], f4)
    Wv = np.asarray(inputs["L0_attn_Wv"], f4)
    Wo = np.asarray(inputs["L0_attn_Wo"], f4)
    Wg = np.asarray(inputs["L0_attn_Wg"], f4)
    W0 = np.asarray(inputs["L0_attn_W0"], f4)
    A_init = np.asarray(inputs["L0_attn_A_init"], f4)
    B_init = np.asarray(inputs["L0_attn_B_init"], f4)

    wq_s = _slab(np.ascontiguousarray(Wq.T), CC)
    wk_s = _slab(np.ascontiguousarray(Wk.T), CC)
    wv_s = _slab(np.ascontiguousarray(Wv.T), CC)
    wg_s = _slab(np.ascontiguousarray(Wg.T), CC)
    wo_s = _slab(np.ascontiguousarray(Wo.T), CC)
    w0_s = _slab(np.ascontiguousarray(W0), CC)
    A_pad = np.zeros((C, 128), f4)
    A_pad[:, :8] = A_init
    a_s = _slab(A_pad, 1).reshape(128, C)
    B_pad = np.zeros((128, C), f4)
    B_pad[:8, :] = B_init
    b_s = np.ascontiguousarray(B_pad.reshape(128, CC, 128).transpose(1, 0, 2))

    pos = np.arange(1, 4097, dtype=f4)
    pos_scale = (f4(1.0) / (f4(1.0) + f4(0.1) * np.log(pos))).astype(f4)

    common = dict(
        wq_s=wq_s, wk_s=wk_s, wv_s=wv_s, wg_s=wg_s, wo_s=wo_s, w0_s=w0_s,
        a_s=a_s, b_s=b_s,
        wgT=np.ascontiguousarray(np.asarray(inputs["L0_ffn_w_gate"], f4).T),
        wuT=np.ascontiguousarray(np.asarray(inputs["L0_ffn_w_up"], f4).T),
        wdT=np.ascontiguousarray(np.asarray(inputs["L0_ffn_w_down"], f4).T),
        wgN=np.ascontiguousarray(np.asarray(inputs["L0_ffn_w_gate"], f4)),
        wuN=np.ascontiguousarray(np.asarray(inputs["L0_ffn_w_up"], f4)),
        wdN=np.ascontiguousarray(np.asarray(inputs["L0_ffn_w_down"], f4)),
        logr=_vec8(inputs["L0_attn_log_lr"]), lng=_vec8(inputs["L0_attn_ln_g"]),
        lnb=_vec8(inputs["L0_attn_ln_b"]), lsg=_vec8(inputs["L0_attn_ls_g"]),
        clipx=np.asarray(inputs["L0_ffn_clip_x"], f4).reshape(1, 1),
        clipd=np.asarray(inputs["L0_ffn_clip_down"], f4).reshape(1, 1),
    )
    in_maps = []
    for k in range(NCORE):
        m = dict(common)
        m["xT"] = np.ascontiguousarray(x[TOK * k:TOK * (k + 1), :].T)
        seq_half = k % 2
        m["posr"] = np.ascontiguousarray(
            pos_scale[TOK * seq_half:TOK * (seq_half + 1)].reshape(1, TOK))
        m["iseven"] = np.asarray([[1.0 if k % 2 == 0 else 0.0]], f4)
        m["isodd"] = np.asarray([[1.0 if k % 2 == 1 else 0.0]], f4)
        in_maps.append(m)

    nc = build_program()
    res = run_bass_kernel_spmd(nc, in_maps, core_ids=list(range(NCORE)))
    globals()["_LAST_RESULTS"] = res
    out = np.empty((NCORE * TOK, C), f4)
    for k in range(NCORE):
        out[TOK * k:TOK * (k + 1), :] = res.results[k]["oxT"].T
    return out.reshape(4, 4096, C)
